# version 01
# baseline (speedup 1.0000x reference)
import os
import numpy as np
import ml_dtypes
import concourse.bass as bass
import concourse.mybir as mybir
import concourse.library_config as library_config
from concourse.library_overlay import lower_extended_insts
from concourse.bass_utils import run_bass_kernel_spmd

F32 = mybir.dt.float32
BF16 = mybir.dt.bfloat16
I16 = mybir.dt.int16
U32 = mybir.dt.uint32
AF = mybir.ActivationFunctionType
ALU = mybir.AluOpType
AX = mybir.AxisListType

S, D, H, DH, FF, E, CAP, IFF = 4096, 1024, 16, 64, 256, 8, 2048, 4096
NT, NG, TPG, NC = 32, 8, 4, 8
EPS = 1e-5
DROP = 60000.0

OFF_A = 34816
OFF_B = 100352
OFF_C = 165888
OFF_D = 198656
OFF_OV = 206848

UPTO = int(os.environ.get("KUPTO", "7"))
KSKIP = set(s for s in os.environ.get("KSKIP", "").split(",") if s)
TAPS = set(t for t in os.environ.get("KTAPS", "").split(",") if t)


class Sem:
    def __init__(self, nc, name):
        self.h = nc.alloc_semaphore(name)
        self.n = 0

    def inc(self, ins, k=1):
        ins.then_inc(self.h, k)
        self.n += k
        return ins


class Chan:

    def __init__(self, nc, name, k=1, per_slot=1):
        self.sems = [nc.alloc_semaphore(f"{name}{i}") for i in range(k)]
        self.k = k
        self.per_slot = per_slot
        self.tot = [0] * k
        self.cnt = {}

    def inc(self, ins, slot):
        j = slot % self.k
        self.tot[j] += 16
        ins.then_inc(self.sems[j], 16)
        self.cnt[slot] = self.cnt.get(slot, 0) + 1
        assert self.cnt[slot] <= self.per_slot

    def wait(self, eng, slot):
        if slot < 0:
            return
        v = 16 * self.per_slot * (slot // self.k + 1)
        eng.wait_ge(self.sems[slot % self.k], v)

    def wait_all(self, eng):
        for j in range(self.k):
            if self.tot[j]:
                eng.wait_ge(self.sems[j], self.tot[j])


class Bld:
    def __init__(self):
        nc = bass.Bass("TRN2", target_bir_lowering=False, detect_race_conditions=False)
        self.nc = nc
        dram = lambda n, s, d, k="Internal": nc.dram_tensor(n, s, d, kind=k)
        self.inp = {}
        for name, shape in [
                ("x", [S, D]), ("wq", [D, D]), ("wk", [D, D]), ("wv", [D, D]),
                ("wp", [D, D]), ("bq", [128, 8]), ("bk", [128, 8]),
                ("bvr", [1, D]), ("bpr", [1, D]), ("wqf4", [128, 512]),
                ("wkf2", [128, 512]), ("wgt", [128, 64]), ("grep", [128, D]),
                ("brep", [128, D]), ("wi_c", [D, IFF]), ("wo_c", [IFF, D]),
                ("triu", [128, 128]), ("onesc", [128, 128]),
                ("identc", [128, 128]), ("sel8", [128, 128]),
                ("ids", [128, NT]), ("pref", [128, 1])]:
            self.inp[name] = dram(name, shape, F32, "ExternalInput")
        self.inp["iota8u"] = dram("iota8u", [128, 8], U32, "ExternalInput")
        self.inp["meta_u"] = dram("meta_u", [1, 2], U32, "ExternalInput")
        self.out = dram("out", [S, D], F32, "ExternalOutput")

        self.d_x = dram("d_x", [S, D], BF16)
        self.d_ln1 = dram("d_ln1", [S, D], BF16)
        self.d_cat = dram("d_cat", [S, D], BF16)
        self.d_y = dram("d_y", [S, D], BF16)
        self.d_z = dram("d_z", [S, D], BF16)
        self.slab_loc = dram("slab_loc", [S, D], BF16)
        self.slab_all = dram("slab_all", [NC * S, D], BF16)
        self.cnt_loc = dram("cnt_loc", [1, E], F32)
        self.cnt_all = dram("cnt_all", [NC, E], F32)
        self.tab_loc = dram("tab_loc", [E * CAP, 1], F32)
        self.tab_all = dram("tab_all", [E * CAP, 1], F32)
        self.eo_loc = dram("eo_loc", [CAP, D], BF16)
        self.eo_all = dram("eo_all", [E * CAP, D], BF16)
        self.offs_d = dram("offs_d", [NT, E], F32)

        sb = nc.alloc_sbuf_tensor
        self.c_wqf4 = sb("c_wqf4", [128, 512], BF16)
        self.c_wkf2 = sb("c_wkf2", [128, 512], BF16)
        self.c_bq = sb("c_bq", [128, 8], F32)
        self.c_bk = sb("c_bk", [128, 8], F32)
        self.c_bvr = sb("c_bvr", [1, D], BF16)
        self.c_bpr = sb("c_bpr", [1, D], BF16)
        self.c_wgt = sb("c_wgt", [128, 64], BF16)
        self.c_triu = sb("c_triu", [128, 128], BF16)
        self.c_ones = sb("c_ones", [128, 128], BF16)
        self.c_onesf = sb("c_onesf", [128, 128], F32)
        self.c_identb = sb("c_identb", [128, 128], BF16)
        self.c_identf = sb("c_identf", [128, 128], F32)
        self.c_sel8 = sb("c_sel8", [128, 128], F32)
        self.c_iota8u = sb("c_iota8u", [128, 8], U32)
        self.c_iota8f = sb("c_iota8f", [128, 8], F32)
        self.c_ids = sb("c_ids", [128, NT], F32)
        self.c_pref = sb("c_pref", [128, 1], F32)
        self.c_meta = sb("c_meta", [1, 2], U32)
        self.zero128 = sb("zero128", [128, 128], F32)
        self.st_sum = sb("st_sum", [128, NT], F32)
        self.st_ssq = sb("st_ssq", [128, NT], F32)
        self.st_negmu = sb("st_negmu", [128, NT], F32)
        self.st_inv = sb("st_inv", [128, NT], F32)
        self.st_b2 = sb("st_b2", [128, NT], F32)
        self.st2_ssq = sb("st2_ssq", [128, NT], F32)
        self.st2_negmu = sb("st2_negmu", [128, NT], F32)
        self.st2_inv = sb("st2_inv", [128, NT], F32)
        self.st2_b2 = sb("st2_b2", [128, NT], F32)
        self.r_rank = sb("r_rank", [128, NT * E], F32)
        self.r_ohall = sb("r_ohall", [128, NT * E], BF16)
        self.r_w = sb("r_w", [128, NT], F32)
        self.r_keep = sb("r_keep", [128, NT], F32)
        self.r_e = sb("r_e", [128, NT], F32)
        self.r_rk = sb("r_rk", [128, NT], F32)
        self.r_rm = sb("r_rm", [128, NT], F32)
        self.r_rdi = sb("r_rdi", [128, NT], I16)
        self.r_rdf = sb("r_rdf", [128, NT], F32)
        self.r_a1 = sb("r_a1", [128, NT], F32)
        self.r_a1u = sb("r_a1u", [128, NT], U32)
        self.r_a2 = sb("r_a2", [128, NT], F32)
        self.r_t1 = sb("r_t1", [128, NT], F32)
        self.r_max = sb("r_max", [128, 8], F32)
        self.r_idx = sb("r_idx", [128, 8], U32)
        self.r_oh = sb("r_oh", [128, 8], BF16)
        self.r_ohf = sb("r_ohf", [128, 8], F32)
        self.r_exp = sb("r_exp", [128, 8], F32)
        self.r_s1 = sb("r_s1", [128, 1], F32)
        self.r_cnt = sb("r_cnt", [8, NT], F32)
        self.r_cntall = sb("r_cntall", [8, 8], F32)
        self.r_scan = sb("r_scan", [8, NT], F32)
        self.r_offs = sb("r_offs", [8, NT], F32)
        self.r_base = sb("r_base", [8, 1], F32)
        self.r_offsT = sb("r_offsT", [NT, 8], F32)
        self.r_offsrow = sb("r_offsrow", [1, NT * E], F32)
        self.gb_f = sb("gb_f", [16, 256], F32)
        self.gb_i = sb("gb_i", [128, 256], I16)
        self.di_f = sb("di_f", [16, 128], F32)
        self.di_i = sb("di_i", [128, 128], I16)
        self.dall = sb("dall", [128, 16], F32)
        self.dinv = sb("dinv", [128, 16], F32)

        at = nc.alloc_sbuf_tensor_at
        self.qT = at("qT", [128, 8 * S], BF16, offset=OFF_A)
        self.wi_bf = at("wi_bf", [128, 8 * IFF], BF16, offset=OFF_A)
        self.comb = at("comb", [128, NT // 2, 2, D], BF16, offset=OFF_A)
        self.wq_bf = at("wq_bf", [128, 8 * D], BF16, offset=OFF_B)
        self.wk_bf = at("wk_bf", [128, 8 * D], BF16, offset=OFF_B + 16384)
        self.wv_bf = at("wv_bf", [128, 8 * D], BF16, offset=OFF_B + 32768)
        self.wp_bf = at("wp_bf", [128, 8 * D], BF16, offset=OFF_B + 49152)
        self.wo_bf = at("wo_bf", [128, 32 * D], BF16, offset=OFF_B)
        self.lnxT = at("lnxT", [128, 8 * 512], BF16, offset=OFF_C)
        self.kT = at("kT", [128, 8 * 512], BF16, offset=OFF_C + 8192)
        self.v_g = at("v_g", [128, TPG * D], BF16, offset=OFF_C + 16384)
        self.kp = at("kp", [128, 2 * FF], BF16, offset=OFF_C + 24576)
        self.qp = at("qp", [128, 32 * 512], BF16, offset=OFF_C)
        self.catT = at("catT", [128, 8 * 512], BF16, offset=OFF_C)
        self.ln2tT = at("ln2tT", [128, 8 * 512], BF16, offset=OFF_C)
        self.dispT = at("dispT", [128, 8, 8, 256], BF16, offset=OFF_C)
        self.bf_st = at("bf_st", [128, 4 * D], BF16, offset=OFF_D)
        self.disc = at("disc", [128, D], BF16, offset=OFF_C + 30720)
        self.xs = at("xs", [128, 2 * D], F32, offset=OFF_OV)
        self.ctx = at("ctx", [128, H * FF], F32, offset=OFF_OV)
        self.ctxb = at("ctxb", [128, H * 130], BF16, offset=OFF_OV)
        self.c_grep = at("c_grep", [128, D], BF16, offset=OFF_OV)
        self.c_brep = at("c_brep", [128, D], BF16, offset=OFF_OV + 2048)
        self.xb = at("xb", [128, 2 * D], BF16, offset=OFF_OV + 4096)
        self.hT = at("hT", [128, 32 * 256], BF16, offset=OFF_OV)
        self.zst = at("zst", [128, 2 * D], BF16, offset=OFF_OV)
        self.yst = at("yst", [128, 2 * D], BF16, offset=OFF_OV + 4096)
        self.fst = at("fst", [128, 2 * D], BF16, offset=OFF_OV + 8192)
        self.ost = at("ost", [128, 2 * D], F32, offset=OFF_OV + 12288)

        ps = nc.alloc_psum_tensor
        self.pA = ps("pA", [128, 512], F32)
        self.pB = ps("pB", [128, 512], F32)
        self.pC = ps("pC", [128, 512], F32)
        self.pD = ps("pD", [128, 512], F32)
        self.pE = ps("pE", [128, 512], F32)
        self.pF = ps("pF", [128, 512], F32)
        self.pG = ps("pG", [128, 512], F32)
        self.pT = ps("pT", [128, 1024], BF16)

        self.s_ld = Sem(nc, "s_ld")
        self.s_gd = Sem(nc, "s_gd")
        self.s_td = Sem(nc, "s_td")
        self.s_st = Sem(nc, "s_st")
        self.s_mm = Sem(nc, "s_mm")
        self.s_ev = Sem(nc, "s_ev")
        self.s_ac = Sem(nc, "s_ac")
        self.s_cc = Sem(nc, "s_cc")
        self.s_gp = Sem(nc, "s_gp")
        C = lambda n, k=1: Chan(nc, n, k)
        self.ch_const = C("ccst")
        self.ch_wc = C("cwc")
        self.ch_x = C("cx", 2)
        self.ch_dx = C("cdx", 2)
        self.ch_l1 = C("cl1", 2)
        self.ch_t2 = Chan(nc, "ct2", 2, per_slot=8)
        self.ch_cat = C("ccat", 2)
        self.ch_t4 = Chan(nc, "ct4", 2, per_slot=8)
        self.ch_xb = C("cxb", 2)
        self.ch_y = C("cy", 2)
        self.ch_z = C("cz", 2)
        self.ch_slab = C("cslab", 2)
        self.ch_t5 = Chan(nc, "ct5", 2, per_slot=8)
        self.ch_cnt = C("ccnt")
        self.ch_cna = C("ccna")
        self.ch_off = C("coff")
        self.ch_orow = C("corow")
        self.ch_rdf = C("crdf")
        self.ch_tz = C("ctz")
        self.ch_gbr = C("cgbr")
        self.ch_wi = C("cwi")
        self.ch_wo = C("cwo")
        self.ch_sc = C("csc")
        self.ch_tr = C("ctr")
        self.ch_direp = C("cdirep")
        self.ch_dg = C("cdg")
        self.ch_eo = C("ceo", 2)
        self.ch_gb2 = C("cgb2")
        self.ch_zy = Chan(nc, "czy", 2, per_slot=2)
        self.ch_out = C("cout", 2)
        self.ch_grp = C("cgrp")

    def incd(self, eng, sem, k=1):
        eng.drain()
        sem.inc(eng.nop(), k)

    def p0(self):
        nc = self.nc
        with nc.Block() as blk:
            @blk.sync
            def _(sy):
                for i, (dst, src) in enumerate([
                        (self.c_bq, "bq"), (self.c_bk, "bk"),
                        (self.c_sel8, "sel8"), (self.c_ids, "ids"),
                        (self.c_pref, "pref"), (self.c_iota8u, "iota8u"),
                        (self.c_onesf, "onesc"), (self.c_identf, "identc"),
                        (self.c_meta, "meta_u")]):
                    self.ch_const.inc(sy.dma_start(out=dst[:], in_=self.inp[src].ap()), i)

            @blk.gpsimd
            def _(gp):
                i = 0
                for dst, src in [(self.c_wqf4, "wqf4"), (self.c_wkf2, "wkf2"),
                                 (self.c_bvr, "bvr"), (self.c_bpr, "bpr"),
                                 (self.c_wgt, "wgt"), (self.c_triu, "triu"),
                                 (self.c_ones, "onesc"), (self.c_identb, "identc"),
                                 (self.c_iota8f, "iota8u")]:
                    self.ch_wc.inc(gp.dma_start(out=dst[:], in_=self.inp[src].ap()), i)
                    i += 1
                for kc in range(8):
                    for wbf, wsrc in [(self.wq_bf, "wq"), (self.wk_bf, "wk"),
                                      (self.wv_bf, "wv"), (self.wp_bf, "wp")]:
                        self.ch_wc.inc(gp.dma_start(
                            out=wbf[:, kc * D:(kc + 1) * D],
                            in_=self.inp[wsrc].ap()[kc * 128:(kc + 1) * 128, :]), i)
                        i += 1

            @blk.vector
            def _(ve):
                self.ch_const.wait_all(ve)
                self.s_ev.inc(ve.memset(self.zero128[:], 0.0), 1)
        self.p0_zero = self.s_ev.n

    def p1(self):
        nc = self.nc
        ld, ev, ac, st = self.s_ld, self.s_ev, self.s_ac, self.s_st
        EV0, AC0 = ev.n, ac.n
        ev_t = lambda t, j: EV0 + 5 * t + j
        ac_t = lambda t, j: AC0 + 2 * t + j

        def p1_store(sy, t):
            sy.wait_ge(ev.h, ev_t(t, 2))
            self.ch_dx.inc(sy.dma_start(out=self.d_x.ap()[t * 128:(t + 1) * 128, :],
                                        in_=self.bf_st[:, (t % 2) * D:(t % 2 + 1) * D]), t)
            sy.wait_ge(ev.h, ev_t(t, 5))
            self.ch_l1.inc(sy.dma_start(out=self.d_ln1.ap()[t * 128:(t + 1) * 128, :],
                                        in_=self.bf_st[:, (2 + t % 2) * D:(3 + t % 2) * D]), t)

        with nc.Block() as blk:
            @blk.sync
            def _(sy):
                for t in range(NT):
                    if t >= 2:
                        sy.wait_ge(ev.h, ev_t(t - 2, 5))
                    self.ch_x.inc(sy.dma_start(out=self.xs[:, (t % 2) * D:(t % 2 + 1) * D],
                                        in_=self.inp["x"].ap()[t * 128:(t + 1) * 128, :]), t)
                    if t >= 2:
                        p1_store(sy, t - 2)
                for t in range(NT - 2, NT):
                    p1_store(sy, t)

            @blk.vector
            def _(ve):
                for t in range(NT):
                    xt = self.xs[:, (t % 2) * D:(t % 2 + 1) * D]
                    self.ch_x.wait(ve, t)
                    ve.reduce_sum(self.st_sum[:, t:t + 1], xt, axis=AX.X)
                    ve.drain()
                    ve.tensor_scalar_mul(self.st_negmu[:, t:t + 1],
                                         self.st_sum[:, t:t + 1], -1.0 / D)
                    self.incd(ve, ev)
                    self.ch_dx.wait(ve, t - 2)
                    ev.inc(ve.tensor_copy(self.bf_st[:, (t % 2) * D:(t % 2 + 1) * D], xt), 1)
                    ve.wait_ge(ac.h, ac_t(t, 1))
                    ve.tensor_scalar(self.st_inv[:, t:t + 1], self.st_ssq[:, t:t + 1],
                                     1.0 / D, EPS, ALU.mult, ALU.add)
                    ve.drain()
                    ve.reciprocal(self.st_inv[:, t:t + 1], self.st_inv[:, t:t + 1])
                    self.incd(ve, ev)
                    ve.wait_ge(ac.h, ac_t(t, 2))
                    ve.tensor_mul(self.st_b2[:, t:t + 1], self.st_negmu[:, t:t + 1],
                                  self.st_inv[:, t:t + 1])
                    ve.drain()
                    self.incd(ve, ev)
                    self.ch_l1.wait(ve, t - 2)
                    ev.inc(ve.tensor_scalar(self.bf_st[:, (2 + t % 2) * D:(3 + t % 2) * D],
                                            xt, self.st_inv[:, t:t + 1],
                                            self.st_b2[:, t:t + 1], ALU.mult, ALU.add), 1)

            @blk.scalar
            def _(sc):
                for t in range(NT):
                    xt = self.xs[:, (t % 2) * D:(t % 2 + 1) * D]
                    sc.wait_ge(ev.h, ev_t(t, 1))
                    sc.activation(self.disc[:], xt, AF.Square,
                                  bias=self.st_negmu[:, t:t + 1], scale=1.0,
                                  accum_out=self.st_ssq[:, t:t + 1])
                    self.incd(sc, ac)
                    sc.wait_ge(ev.h, ev_t(t, 3))
                    sc.activation(self.st_inv[:, t:t + 1], self.st_inv[:, t:t + 1],
                                  AF.Sqrt)
                    self.incd(sc, ac)

        assert ev.n == EV0 + 5 * NT and ac.n == AC0 + 2 * NT

    def p2(self):
        nc = self.nc
        mm, ev, ac = self.s_mm, self.s_ev, self.s_ac
        MM0, EV0, AC0 = mm.n, ev.n, ac.n
        GPT, EVG, ACG = 152, 41, 64
        mm_qk = lambda g, w, oc: MM0 + GPT * g + w * 8 + oc + 1
        mm_v = lambda g, tl, hf: MM0 + GPT * g + 16 + tl * 2 + hf + 1
        mm_kp = lambda g, h, tl: MM0 + GPT * g + 24 + 8 * h + 2 * tl + 1
        mm_ctx = lambda g, h: MM0 + GPT * g + 24 + 8 * h + 8
        ev_qk = lambda g, w, oc: EV0 + EVG * g + w * 8 + oc + 1
        ev_v = lambda g, tl, hf: EV0 + EVG * g + 16 + tl * 2 + hf + 1
        ev_fold = lambda g, h: EV0 + EVG * g + 24 + h + 1
        ac_exp = lambda g, h, tl: AC0 + ACG * g + 4 * h + tl + 1

        with nc.Block() as blk:
            @blk.scalar
            def _(sc):
                for g in range(NG):
                    if g == 0:
                        self.ch_l1.wait_all(sc)
                    if g >= 1:
                        sc.wait_ge(mm.h, mm_v(g - 1, TPG - 1, 1))
                    for kc in range(8):
                        self.ch_t2.inc(sc.dma_start(
                            out=self.lnxT[:, kc * 512:(kc + 1) * 512],
                            in_=self.d_ln1.ap()[g * 512:(g + 1) * 512,
                                                kc * 128:(kc + 1) * 128],
                            transpose=True), g)
                    for h in range(H):
                        for tl in range(TPG):
                            sc.wait_ge(mm.h, mm_kp(g, h, tl))
                            bank = [self.pD, self.pE][tl % 2]
                            ac.inc(sc.activation(
                                self.kp[:, (tl % 2) * FF:(tl % 2 + 1) * FF],
                                bank[:, 0:256], AF.Exp), 1)

            @blk.tensor
            def _(te):
                self.ch_wc.wait_all(te)
                self.ch_const.wait_all(te)
                for g in range(NG):
                    self.ch_t2.wait(te, g)
                    for w, wbf in [(0, self.wq_bf), (1, self.wk_bf)]:
                        for oc in range(8):
                            bank = [self.pA, self.pB][oc % 2]
                            k = 2 * g * 8 + w * 8 + oc
                            if k >= 2:
                                pk = k - 2
                                pg, r = divmod(pk, 16)
                                te.wait_ge(ev.h, ev_qk(pg, r // 8, r % 8))
                            ins = None
                            for i in range(8):
                                ins = te.matmul(bank[:, 0:512],
                                                wbf[:, i * D + oc * 128: i * D + (oc + 1) * 128],
                                                self.lnxT[:, i * 512:(i + 1) * 512],
                                                start=(i == 0), stop=(i == 7))
                            mm.inc(ins, 1)
                    for tl in range(TPG):
                        for hf in range(2):
                            k = g * 8 + tl * 2 + hf
                            if k >= 1:
                                pg, r = divmod(k - 1, 8)
                                te.wait_ge(ev.h, ev_v(pg, r // 2, r % 2))
                            for i in range(8):
                                te.matmul(self.pC[:, 0:512],
                                          self.lnxT[:, i * 512 + tl * 128: i * 512 + (tl + 1) * 128],
                                          self.wv_bf[:, i * D + hf * 512: i * D + (hf + 1) * 512],
                                          start=(i == 0), stop=False)
                            ins = te.matmul(self.pC[:, 0:512], self.c_ones[0:1, :],
                                            self.c_bvr[0:1, hf * 512:(hf + 1) * 512],
                                            start=False, stop=True)
                            mm.inc(ins, 1)
                    te.wait_ge(ev.h, ev_qk(g, 1, 7))
                    te.wait_ge(ev.h, ev_v(g, TPG - 1, 1))
                    for h in range(H):
                        ch, par = h // 2, h % 2
                        cbank = [self.pF, self.pG][h % 2]
                        if h >= 2:
                            te.wait_ge(ev.h, ev_fold(g, h - 2))
                        elif g >= 1:
                            te.wait_ge(ev.h, ev_fold(g - 1, H - 2 + h))
                        for tl in range(TPG):
                            bank = [self.pD, self.pE][tl % 2]
                            k = (g * H + h) * TPG + tl
                            if k >= 2:
                                pk = k - 2
                                pg2, r = divmod(pk, H * TPG)
                                te.wait_ge(ac.h, ac_exp(pg2, r // TPG, r % TPG))
                            ins = te.matmul(bank[:, 0:256],
                                            self.kT[:, ch * 512 + tl * 128: ch * 512 + (tl + 1) * 128],
                                            self.c_wkf2[:, par * 256:(par + 1) * 256],
                                            start=True, stop=True)
                            mm.inc(ins, 1)
                            te.wait_ge(ac.h, ac_exp(g, h, tl))
                            kpb = self.kp[:, (tl % 2) * FF:(tl % 2 + 1) * FF]
                            te.matmul(cbank[0:64, 0:256],
                                      self.v_g[:, tl * D + h * DH: tl * D + (h + 1) * DH],
                                      kpb, start=(tl == 0), stop=False,
                                      skip_group_check=True)
                            ins = te.matmul(cbank[64:65, 0:256], self.c_ones[:, 0:1], kpb,
                                            start=(tl == 0), stop=(tl == TPG - 1),
                                            skip_group_check=True)
                            mm.inc(ins, 1)

            @blk.vector
            def _(ve):
                for g in range(NG):
                    for w in range(2):
                        for oc in range(8):
                            bank = [self.pA, self.pB][oc % 2]
                            ve.wait_ge(mm.h, mm_qk(g, w, oc))
                            dst = (self.qT[:, oc * S + g * 512: oc * S + (g + 1) * 512]
                                   if w == 0 else self.kT[:, oc * 512:(oc + 1) * 512])
                            bias = (self.c_bq if w == 0 else self.c_bk)[:, oc:oc + 1]
                            ev.inc(ve.tensor_scalar_add(dst, bank[:, 0:512], bias), 1)
                    for tl in range(TPG):
                        for hf in range(2):
                            ve.wait_ge(mm.h, mm_v(g, tl, hf))
                            ev.inc(ve.tensor_copy(
                                self.v_g[:, tl * D + hf * 512: tl * D + (hf + 1) * 512],
                                self.pC[:, 0:512]), 1)
                    for h in range(H):
                        cbank = [self.pF, self.pG][h % 2]
                        ve.wait_ge(mm.h, mm_ctx(g, h))
                        cslice = self.ctx[0:65, h * FF:(h + 1) * FF]
                        if g == 0:
                            ev.inc(ve.tensor_copy(cslice, cbank[0:65, 0:256]), 1)
                        else:
                            ev.inc(ve.tensor_add(cslice, cslice, cbank[0:65, 0:256]), 1)
                    ev.inc(ve.nop(), 1)
        assert mm.n == MM0 + GPT * NG and ev.n == EV0 + EVG * NG and ac.n == AC0 + ACG * NG

    def p25(self):
        nc = self.nc
        mm, ev = self.s_mm, self.s_ev
        MM0, EV0 = mm.n, ev.n
        with nc.Block() as blk:
            @blk.vector
            def _(ve):
                for h in range(H):
                    ev.inc(ve.tensor_copy(self.bf_st[0:65, h * FF:(h + 1) * FF],
                                          self.ctx[0:65, h * FF:(h + 1) * FF]), 1)

            @blk.tensor
            def _(te):
                te.wait_ge(ev.h, EV0 + H)
                for j in range(2 * H):
                    h, cc = divmod(j, 2)
                    if j >= 1:
                        te.wait_ge(ev.h, EV0 + H + j)
                    ins = te.transpose(self.pT[:, 0:65],
                                       self.bf_st[0:65, h * FF + cc * 128: h * FF + cc * 128 + 128],
                                       self.c_identb[0:65, 0:65])
                    mm.inc(ins, 1)

            @blk.vector
            def _(ve):
                for j in range(2 * H):
                    h, cc = divmod(j, 2)
                    ve.wait_ge(mm.h, MM0 + j + 1)
                    ev.inc(ve.tensor_copy(
                        self.ctxb[:, h * 130 + cc * 65: h * 130 + (cc + 1) * 65],
                        self.pT[:, 0:65]), 1)
        self.p25_ev = ev.n

    def p3(self):
        nc = self.nc
        mm, ev, ac = self.s_mm, self.s_ev, self.s_ac
        MM0, EV0, AC0 = mm.n, ev.n, ac.n
        GT = 32 + TPG * H
        mm_qp = lambda g, h, fh: MM0 + GT * g + h * 2 + fh + 1
        mm_nm = lambda g, tl, h: MM0 + GT * g + 32 + tl * H + h + 1
        ac_qp = lambda g, h, fh: AC0 + 32 * g + h * 2 + fh + 1
        ev_tile = lambda g, tl: EV0 + 32 * g + 8 * (tl + 1)
        NBK = [(0, 7, "pD"), (7, 14, "pE"), (14, 16, "pF")]

        with nc.Block() as blk:
            @blk.tensor
            def _(te):
                te.wait_ge(ev.h, self.p25_ev)
                for g in range(NG):
                    for h in range(H):
                        for fh in range(2):
                            bank = [self.pA, self.pB][(h * 2 + fh) % 2]
                            k = g * 32 + h * 2 + fh
                            if k >= 2:
                                pk = k - 2
                                pg, r = divmod(pk, 32)
                                te.wait_ge(ac.h, ac_qp(pg, r // 2, r % 2))
                            ins = te.matmul(bank[:, 0:512],
                                            self.c_wqf4[:, (h % 2) * 256 + fh * 128:
                                                        (h % 2) * 256 + (fh + 1) * 128],
                                            self.qT[:, (h // 2) * S + g * 512:
                                                    (h // 2) * S + (g + 1) * 512],
                                            start=True, stop=True)
                            mm.inc(ins, 1)
                    for tl in range(TPG):
                        gk = g * TPG + tl
                        if gk >= 1:
                            pg, ptl = divmod(gk - 1, TPG)
                            te.wait_ge(ev.h, ev_tile(pg, ptl))
                        for h in range(H):
                            lo, hi, bname = next(b for b in NBK if b[0] <= h < b[1])
                            bank = getattr(self, bname)
                            col = (h - lo) * 65
                            te.wait_ge(ac.h, ac_qp(g, h, 1))
                            ins = None
                            for fh in range(2):
                                ins = te.matmul(bank[:, col:col + 65],
                                                self.qp[:, (h * 2 + fh) * 512 + tl * 128:
                                                        (h * 2 + fh) * 512 + (tl + 1) * 128],
                                                self.ctxb[:, h * 130 + fh * 65:
                                                          h * 130 + (fh + 1) * 65],
                                                start=(fh == 0), stop=(fh == 1),
                                                skip_group_check=True)
                            mm.inc(ins, 1)

            @blk.scalar
            def _(sc):
                for g in range(NG):
                    for h in range(H):
                        for fh in range(2):
                            bank = [self.pA, self.pB][(h * 2 + fh) % 2]
                            sc.wait_ge(mm.h, mm_qp(g, h, fh))
                            ac.inc(sc.activation(
                                self.qp[:, (h * 2 + fh) * 512:(h * 2 + fh + 1) * 512],
                                bank[:, 0:512], AF.Exp), 1)

            @blk.vector
            def _(ve):
                for g in range(NG):
                    for tl in range(TPG):
                        t = g * TPG + tl
                        ve.wait_ge(mm.h, mm_nm(g, tl, H - 1))
                        self.ch_cat.wait(ve, t - 2)
                        for bi, (lo, hi, bname) in enumerate(NBK):
                            bank = getattr(self, bname)
                            nh = hi - lo
                            dsrc = bass.AP(bank, 64, [[512, 128], [65, nh]])
                            if bi == 2:
                                ve.tensor_copy(self.dall[:, lo:hi], dsrc)
                                ve.drain()
                                ev.inc(ve.nop(), 1)
                            else:
                                ev.inc(ve.tensor_copy(self.dall[:, lo:hi], dsrc), 1)
                        ve.tensor_scalar_add(self.dall[:], self.dall[:], 1e-6)
                        ve.drain()
                        ev.inc(ve.nop(), 1)
                        ve.reciprocal(self.dinv[:], self.dall[:])
                        ve.drain()
                        ev.inc(ve.nop(), 1)
                        for lo, hi, bname in NBK:
                            bank = getattr(self, bname)
                            nh = hi - lo
                            nsrc = bass.AP(bank, 0, [[512, 128], [65, nh], [1, 64]])
                            dsrc = bass.AP(self.dinv, lo, [[16, 128], [1, nh], [0, 64]])
                            dst = bass.AP(self.bf_st, (t % 2) * 2 * D + lo * 64,
                                          [[4 * D, 128], [64, nh], [1, 64]])
                            ev.inc(ve.tensor_tensor(dst, nsrc, dsrc, ALU.mult), 1)

            @blk.sync
            def _(sy):
                for g in range(NG):
                    for tl in range(TPG):
                        t = g * TPG + tl
                        sy.wait_ge(ev.h, ev_tile(g, tl))
                        self.ch_cat.inc(sy.dma_start(
                            out=self.d_cat.ap()[t * 128:(t + 1) * 128, :],
                            in_=self.bf_st[:, (t % 2) * 2 * D:(t % 2) * 2 * D + D]), t)
        assert mm.n == MM0 + GT * NG and ev.n == EV0 + 32 * NG
        assert ac.n == AC0 + 32 * NG

    def p4(self):
        nc = self.nc
        mm, ev, ac = self.s_mm, self.s_ev, self.s_ac
        MM0, EV0, AC0 = mm.n, ev.n, ac.n
        mm_pj = lambda t, hf: MM0 + 2 * t + hf + 1

        with nc.Block() as blk:
            @blk.gpsimd
            def _(gp):
                gp.wait_ge(mm.h, MM0)
                self.ch_grp.inc(gp.dma_start(out=self.c_grep[:],
                                             in_=self.inp["grep"].ap()), 0)
                self.ch_grp.inc(gp.dma_start(out=self.c_brep[:],
                                             in_=self.inp["brep"].ap()), 1)
        ac_t = lambda t, j: AC0 + 6 * t + j
        ev_t = lambda t, j: EV0 + 9 * t + j

        with nc.Block() as blk:
            @blk.sync
            def _(sy):
                def p4_store(t):
                    A = self.bf_st[:, (t % 2) * D:(t % 2 + 1) * D]
                    B = self.bf_st[:, (2 + t % 2) * D:(3 + t % 2) * D]
                    xbt = self.xb[:, (t % 2) * D:(t % 2 + 1) * D]
                    sy.wait_ge(ev.h, ev_t(t, 5))
                    self.ch_y.inc(sy.dma_start(
                        out=self.d_y.ap()[t * 128:(t + 1) * 128, :], in_=A), t)
                    sy.wait_ge(ev.h, ev_t(t, 8))
                    self.ch_z.inc(sy.dma_start(
                        out=self.d_z.ap()[t * 128:(t + 1) * 128, :], in_=B), t)
                    sy.wait_ge(ev.h, ev_t(t, 9))
                    self.ch_slab.inc(sy.dma_start(
                        out=self.slab_loc.ap()[t * 128:(t + 1) * 128, :], in_=xbt), t)
                for t in range(NT):
                    self.ch_dx.wait_all(sy) if t == 0 else None
                    self.ch_xb.inc(sy.dma_start(
                        out=self.xb[:, (t % 2) * D:(t % 2 + 1) * D],
                        in_=self.d_x.ap()[t * 128:(t + 1) * 128, :]), t)
                    if t >= 1:
                        p4_store(t - 1)
                p4_store(NT - 1)

            @blk.tensor
            def _(te):
                for t in range(NT):
                    g, tl = divmod(t, TPG)
                    self.ch_t4.wait(te, g)
                    if t >= 1:
                        te.wait_ge(ac.h, ac_t(t - 1, 2))
                    for hf in range(2):
                        bank = [self.pA, self.pB][hf]
                        for i in range(8):
                            te.matmul(bank[:, 0:512],
                                      self.catT[:, i * 512 + tl * 128: i * 512 + (tl + 1) * 128],
                                      self.wp_bf[:, i * D + hf * 512: i * D + (hf + 1) * 512],
                                      start=(i == 0), stop=False)
                        ins = te.matmul(bank[:, 0:512], self.c_ones[0:1, :],
                                        self.c_bpr[0:1, hf * 512:(hf + 1) * 512],
                                        start=False, stop=True)
                        mm.inc(ins, 1)

            @blk.scalar
            def _(sc):
                for t in range(NT):
                    if t % TPG == 0:
                        g = t // TPG
                        if g == 0:
                            self.ch_cat.wait_all(sc)
                        else:
                            sc.wait_ge(mm.h, mm_pj(g * TPG - 1, 1))
                        for kc in range(8):
                            self.ch_t4.inc(sc.dma_start(
                                out=self.catT[:, kc * 512:(kc + 1) * 512],
                                in_=self.d_cat.ap()[g * 512:(g + 1) * 512,
                                                    kc * 128:(kc + 1) * 128],
                                transpose=True), g)
                    A = self.bf_st[:, (t % 2) * D:(t % 2 + 1) * D]
                    sc.wait_ge(mm.h, mm_pj(t, 1))
                    self.ch_y.wait(sc, t - 2)
                    sc.activation(A[:, 0:512], self.pA[:, 0:512], AF.Copy,
                                  accum_out=self.st_sum[:, t:t + 1])
                    sc.activation(A[:, 512:1024], self.pB[:, 0:512], AF.Copy,
                                  accum_out=self.st_ssq[:, t:t + 1])
                    self.incd(sc, ac, 2)
                    sc.wait_ge(ev.h, ev_t(t, 1))
                    sc.activation(self.disc[:], A, AF.Square,
                                  bias=self.st_negmu[:, t:t + 1], scale=1.0,
                                  accum_out=self.st2_ssq[:, t:t + 1])
                    self.incd(sc, ac)
                    sc.wait_ge(ev.h, ev_t(t, 2))
                    sc.activation(self.st_inv[:, t:t + 1], self.st_inv[:, t:t + 1],
                                  AF.Sqrt)
                    self.incd(sc, ac)
                    sc.wait_ge(ev.h, ev_t(t, 6))
                    sc.activation(self.disc[:], A, AF.Square,
                                  bias=self.st2_negmu[:, t:t + 1], scale=1.0,
                                  accum_out=self.st2_ssq[:, t:t + 1])
                    self.incd(sc, ac)
                    sc.wait_ge(ev.h, ev_t(t, 7))
                    sc.activation(self.st2_inv[:, t:t + 1], self.st2_inv[:, t:t + 1],
                                  AF.Sqrt)
                    self.incd(sc, ac)

            @blk.vector
            def _(ve):
                for t in range(NT):
                    A = self.bf_st[:, (t % 2) * D:(t % 2 + 1) * D]
                    B = self.bf_st[:, (2 + t % 2) * D:(3 + t % 2) * D]
                    xbt = self.xb[:, (t % 2) * D:(t % 2 + 1) * D]
                    self.ch_z.wait(ve, t - 2)
                    ve.wait_ge(ac.h, ac_t(t, 2))
                    ve.tensor_add(self.st_sum[:, t:t + 1], self.st_sum[:, t:t + 1],
                                  self.st_ssq[:, t:t + 1])
                    ve.drain()
                    ve.tensor_scalar_mul(self.st_negmu[:, t:t + 1],
                                         self.st_sum[:, t:t + 1], -1.0 / D)
                    self.incd(ve, ev)
                    ve.wait_ge(ac.h, ac_t(t, 3))
                    ve.tensor_scalar(self.st_inv[:, t:t + 1], self.st2_ssq[:, t:t + 1],
                                     1.0 / D, EPS, ALU.mult, ALU.add)
                    ve.drain()
                    ve.reciprocal(self.st_inv[:, t:t + 1], self.st_inv[:, t:t + 1])
                    self.incd(ve, ev)
                    ve.wait_ge(ac.h, ac_t(t, 4))
                    ve.tensor_mul(self.st_b2[:, t:t + 1], self.st_negmu[:, t:t + 1],
                                  self.st_inv[:, t:t + 1])
                    ve.drain()
                    ev.inc(ve.nop(), 1)
                    ev.inc(ve.tensor_scalar(B, A, self.st_inv[:, t:t + 1],
                                            self.st_b2[:, t:t + 1], ALU.mult, ALU.add), 1)
                    self.ch_xb.wait(ve, t)
                    ev.inc(ve.tensor_add(A, xbt, B), 1)
                    ve.reduce_sum(self.st_sum[:, t:t + 1], A, axis=AX.X)
                    ve.drain()
                    ve.tensor_scalar_mul(self.st2_negmu[:, t:t + 1],
                                         self.st_sum[:, t:t + 1], -1.0 / D)
                    self.incd(ve, ev)
                    ve.wait_ge(ac.h, ac_t(t, 5))
                    ve.tensor_scalar(self.st2_inv[:, t:t + 1], self.st2_ssq[:, t:t + 1],
                                     1.0 / D, EPS, ALU.mult, ALU.add)
                    ve.drain()
                    ve.reciprocal(self.st2_inv[:, t:t + 1], self.st2_inv[:, t:t + 1])
                    self.incd(ve, ev)
                    ve.wait_ge(ac.h, ac_t(t, 6))
                    ve.tensor_mul(self.st2_b2[:, t:t + 1], self.st2_negmu[:, t:t + 1],
                                  self.st2_inv[:, t:t + 1])
                    ve.drain()
                    ev.inc(ve.tensor_scalar(B, A, self.st2_inv[:, t:t + 1],
                                            self.st2_b2[:, t:t + 1], ALU.mult, ALU.add), 1)
                    if t == 0:
                        self.ch_grp.wait_all(ve)
                    ve.tensor_mul(xbt, B, self.c_grep[:])
                    ev.inc(ve.tensor_add(xbt, xbt, self.c_brep[:]), 1)

        assert mm.n == MM0 + 2 * NT and ac.n == AC0 + 6 * NT
        assert ev.n == EV0 + 9 * NT
        self.p4_mm_end = mm.n

    def p5(self):
        nc = self.nc
        mm, ev, ac = self.s_mm, self.s_ev, self.s_ac
        cc = self.s_cc
        MM0, EV0, AC0 = mm.n, ev.n, ac.n
        mm_lg = lambda t: MM0 + 3 * t + 1
        mm_rk = lambda t: MM0 + 3 * t + 2
        EVT = 8
        ev_t = lambda t, j: EV0 + EVT * t + j
        ac_exp = lambda t: AC0 + t + 1

        with nc.Block() as blk:
            @blk.scalar
            def _(sc):
                for g in range(NG):
                    if g == 0:
                        self.ch_slab.wait_all(sc)
                    if g >= 1:
                        sc.wait_ge(mm.h, mm_lg((g - 1) * TPG + TPG - 1))
                    for kc in range(8):
                        self.ch_t5.inc(sc.dma_start(
                            out=self.ln2tT[:, kc * 512:(kc + 1) * 512],
                            in_=self.slab_loc.ap()[g * 512:(g + 1) * 512,
                                                   kc * 128:(kc + 1) * 128],
                            transpose=True), g)
                    for tl in range(TPG):
                        t = g * TPG + tl
                        sc.wait_ge(ev.h, ev_t(t, 2))
                        ac.inc(sc.activation(self.r_exp[:], self.pD[:, 0:8], AF.Exp,
                                             bias=self.r_s1[:, 0:1], scale=1.0), 1)

            @blk.tensor
            def _(te):
                for t in range(NT):
                    g, tl = divmod(t, TPG)
                    self.ch_t5.wait(te, g)
                    if t >= 1:
                        te.wait_ge(ev.h, ev_t(t - 1, 3))
                        te.wait_ge(ac.h, ac_exp(t - 1))
                    ins = None
                    for i in range(8):
                        ins = te.matmul(self.pD[:, 0:8],
                                        self.ln2tT[:, i * 512 + tl * 128: i * 512 + (tl + 1) * 128],
                                        self.c_wgt[:, i * 8:(i + 1) * 8],
                                        start=(i == 0), stop=(i == 7))
                    mm.inc(ins, 1)
                    te.wait_ge(ev.h, ev_t(t, 5))
                    if t >= 1:
                        te.wait_ge(ev.h, ev_t(t - 1, 8))
                    ins = te.matmul(self.pE[:, 0:8], self.c_triu[:], self.r_oh[:],
                                    start=True, stop=True)
                    mm.inc(ins, 1)
                    ins = te.matmul(self.pF[0:8, t:t + 1], self.r_oh[:], self.c_ones[:, 0:1],
                                    start=True, stop=True, skip_group_check=True)
                    mm.inc(ins, 1)

            @blk.vector
            def _(ve):
                for t in range(NT):
                    ve.wait_ge(mm.h, mm_lg(t))
                    ve.max(self.r_max[:], self.pD[:, 0:8])
                    ve.drain()
                    ve.max_index(self.r_idx[:], self.r_max[:], self.pD[:, 0:8])
                    self.incd(ve, ev)
                    ve.tensor_scalar_mul(self.r_s1[:], self.r_max[:, 0:1], -1.0)
                    self.incd(ve, ev)
                    ve.wait_ge(ac.h, ac_exp(t))
                    ve.reduce_sum(self.r_s1[:], self.r_exp[:], axis=AX.X)
                    ve.drain()
                    ve.reciprocal(self.r_w[:, t:t + 1], self.r_s1[:])
                    self.incd(ve, ev)
                    if t >= 1:
                        ve.wait_ge(mm.h, MM0 + 3 * (t - 1) + 3)
                    ve.tensor_single_scalar(
                        self.r_ohf[:], self.c_iota8u.ap().bitcast(F32),
                        bass.AP(self.r_idx, 0, [[8, 128], [1, 1]]).bitcast(F32),
                        ALU.is_equal)
                    ve.drain()
                    ev.inc(ve.nop(), 1)
                    ve.tensor_copy(self.r_oh[:], self.r_ohf[:])
                    self.incd(ve, ev)
                    ev.inc(ve.tensor_copy(self.r_ohall[:, t * 8:(t + 1) * 8],
                                          self.r_ohf[:]), 1)
                    ve.tensor_mul(self.r_exp[:], self.r_ohf[:], self.c_iota8f[:])
                    ve.drain()
                    ev.inc(ve.reduce_sum(self.r_e[:, t:t + 1], self.r_exp[:], axis=AX.X), 1)
                    ve.wait_ge(mm.h, mm_rk(t))
                    ev.inc(ve.tensor_copy(self.r_rank[:, t * 8:(t + 1) * 8],
                                          self.pE[:, 0:8]), 1)
        assert mm.n == MM0 + 3 * NT and ev.n == EV0 + EVT * NT and ac.n == AC0 + NT

        MM1, EV1 = mm.n, ev.n
        with nc.Block() as blk:
            @blk.vector
            def _(ve):
                ve.wait_ge(mm.h, MM1)
                ve.tensor_copy(self.r_cnt[:], self.pF[0:8, 0:NT])
                ve.drain()
                ve.reduce_sum(self.r_base[:], self.r_cnt[:], axis=AX.X)
                self.incd(ve, ev)
                ve.tensor_tensor_scan(self.r_scan[:], self.r_cnt[:],
                                      self.zero128[0:8, 0:NT], 0.0, ALU.add, ALU.add)
                ve.drain()
                ve.tensor_sub(self.r_offs[:], self.r_scan[:], self.r_cnt[:])
                self.incd(ve, ev)

            @blk.sync
            def _(sy):
                sy.wait_ge(ev.h, EV1 + 1)
                self.ch_cnt.inc(sy.dma_start(out=self.cnt_loc.ap(),
                                              in_=self.r_base[0:8, 0:1]), 0)

        with nc.Block() as blk:
            @blk.gpsimd
            def _(gpe):
                self.ch_slab.wait_all(gpe)
                cc.inc(gpe.collective_compute(
                    "AllGather", ALU.bypass, replica_groups=[list(range(NC))],
                    ins=[self.slab_loc.ap().opt()], outs=[self.slab_all.ap().opt()]), 1)
                self.ch_cnt.wait_all(gpe)
                cc.inc(gpe.collective_compute(
                    "AllGather", ALU.bypass, replica_groups=[list(range(NC))],
                    ins=[self.cnt_loc.ap().opt()], outs=[self.cnt_all.ap().opt()]), 1)
        CC_AGS, CC_AGC = cc.n - 1, cc.n

        with nc.Block() as blk:
            @blk.sync
            def _(sy):
                sy.wait_ge(cc.h, CC_AGC)
                self.ch_cna.inc(sy.dma_start(out=self.r_cntall[:], in_=self.cnt_all.ap()), 0)

        with nc.Block() as blk:
            @blk.tensor
            def _(te):
                self.ch_cna.wait_all(te)
                ins = te.matmul(self.pG[0:8, 0:1], self.r_cntall[:], self.c_pref[0:8, 0:1],
                                start=True, stop=True)
                mm.inc(ins, 1)

            @blk.vector
            def _(ve):
                ve.wait_ge(mm.h, MM1 + 1)
                ve.tensor_copy(self.r_base[:], self.pG[0:8, 0:1])
                ve.drain()
                ev.inc(ve.nop(), 1)
                ve.wait_ge(ev.h, EV1 + 2)
                ve.tensor_scalar_add(self.r_offs[:], self.r_offs[:],
                                     self.r_base[:, 0:1])
                self.incd(ve, ev)
        EV2 = ev.n

        with nc.Block() as blk:
            @blk.tensor
            def _(te):
                te.wait_ge(ev.h, EV2)
                ins = te.transpose(self.pG[0:32, 0:8], self.r_offs[:],
                                   self.c_identf[0:8, 0:8])
                mm.inc(ins, 1)

            @blk.vector
            def _(ve):
                ve.wait_ge(mm.h, MM1 + 2)
                ve.tensor_copy(self.r_offsT[:], self.pG[0:32, 0:8])
                self.incd(ve, ev)

            @blk.sync
            def _(sy):
                sy.wait_ge(ev.h, EV2 + 1)
                self.ch_off.inc(sy.dma_start(out=self.offs_d.ap(), in_=self.r_offsT[:]), 0)
                self.ch_off.wait_all(sy)
                self.ch_orow.inc(sy.dma_start(
                    out=self.r_offsrow[:],
                    in_=bass.AP(self.offs_d, 0, [[1, 1], [1, NT * E]])), 0)

        MM2 = mm.n
        with nc.Block() as blk:
            @blk.tensor
            def _(te):
                self.ch_orow.wait_all(te)
                for t in range(NT):
                    if t >= 1:
                        te.wait_ge(ev.h, EV2 + 1 + 3 * t)
                    ins = te.matmul(self.pG[:, 0:8], self.c_onesf[0:1, :],
                                    self.r_offsrow[0:1, t * 8:(t + 1) * 8],
                                    start=True, stop=True)
                    mm.inc(ins, 1)

            @blk.vector
            def _(ve):
                for t in range(NT):
                    ve.wait_ge(mm.h, MM2 + t + 1)
                    ve.tensor_add(self.r_exp[:], self.pG[:, 0:8],
                                  self.r_rank[:, t * 8:(t + 1) * 8])
                    ve.drain()
                    ev.inc(ve.nop(), 1)
                    ve.tensor_mul(self.r_exp[:], self.r_exp[:],
                                  self.r_ohall[:, t * 8:(t + 1) * 8])
                    ve.drain()
                    ev.inc(ve.nop(), 1)
                    ve.reduce_sum(self.r_rk[:, t:t + 1], self.r_exp[:], axis=AX.X)
                    self.incd(ve, ev)
        EV3 = ev.n

        with nc.Block() as blk:
            @blk.vector
            def _(ve):
                ve.wait_ge(ev.h, EV3)
                ve.tensor_single_scalar(self.r_keep[:], self.r_rk[:], float(CAP), ALU.is_lt)
                ve.drain()
                ve.tensor_mul(self.r_w[:], self.r_w[:], self.r_keep[:])
                ve.drain()
                ve.tensor_scalar(self.r_t1[:], self.r_rk[:], 1.0 / 16.0, -0.499,
                                 ALU.mult, ALU.add)
                ve.drain()
                ve.tensor_copy(self.r_rdi[:], self.r_t1[:])
                self.incd(ve, ev)
        EV4 = ev.n

        with nc.Block() as blk:
            @blk.gpsimd
            def _(gpe):
                gpe.wait_ge(ev.h, EV4)
                self.ch_rdf.inc(gpe.dma_start(out=self.r_rdf[:], in_=self.r_rdi[:]), 0)

        with nc.Block() as blk:
            @blk.vector
            def _(ve):
                self.ch_rdf.wait_all(ve)
                ve.scalar_tensor_tensor(self.r_rm[:], self.r_rdf[:], -16.0, self.r_rk[:],
                                        ALU.mult, ALU.add)
                ve.drain()
                ve.tensor_scalar_mul(self.r_a1[:], self.r_e[:], float(CAP))
                ve.drain()
                ve.scalar_tensor_tensor(self.r_a1[:], self.r_rm[:], 128.0, self.r_a1[:],
                                        ALU.mult, ALU.add)
                ve.drain()
                ve.tensor_add(self.r_a1[:], self.r_a1[:], self.r_rdf[:])
                ve.drain()
                ve.tensor_mul(self.r_a1[:], self.r_a1[:], self.r_keep[:])
                ve.drain()
                ve.tensor_scalar(self.r_t1[:], self.r_keep[:], -DROP, DROP,
                                 ALU.mult, ALU.add)
                ve.drain()
                ve.tensor_add(self.r_a1[:], self.r_a1[:], self.r_t1[:])
                ve.drain()
                ve.tensor_copy(self.r_a1u[:], self.r_a1[:])
                self.incd(ve, ev)
                ve.tensor_scalar_mul(self.r_a2[:], self.r_e[:], float(CAP))
                ve.drain()
                ve.tensor_add(self.r_a2[:], self.r_a2[:], self.r_rk[:])
                ve.drain()
                ve.tensor_mul(self.r_a2[:], self.r_a2[:], self.r_keep[:])
                self.incd(ve, ev)
        EV5 = ev.n

        MM3 = mm.n
        with nc.Block() as blk:
            @blk.tensor
            def _(te):
                te.wait_ge(ev.h, EV5)
                for r in range(8):
                    if r >= 1:
                        te.wait_ge(ev.h, EV5 + r)
                    ins = te.matmul(self.pG[0:16, 0:32], self.c_sel8[:, r * 16:(r + 1) * 16],
                                    self.r_a2[:], start=True, stop=True)
                    mm.inc(ins, 1)

            @blk.vector
            def _(ve):
                for r in range(8):
                    ve.wait_ge(mm.h, MM3 + r + 1)
                    dst = bass.AP(self.gb_f, r, [[256, 16], [8, 32]])
                    if r == 7:
                        ve.tensor_copy(dst, self.pG[0:16, 0:32])
                        self.incd(ve, ev)
                    else:
                        ev.inc(ve.tensor_copy(dst, self.pG[0:16, 0:32]), 1)
                ve.wait_ge(ev.h, EV5 + 8)
                ve.memset(self.gb_i[:], 0)
                ve.drain()
                ve.tensor_copy(self.gb_i[0:16, :], self.gb_f[:])
                self.incd(ve, ev)
        EV6 = ev.n
        self.p5_ev_gb = EV6

        with nc.Block() as blk:
            @blk.sync
            def _(sy):
                sy.wait_ge(ev.h, self.p0_zero)
                self.ch_tz.inc(sy.dma_start(
                    out=bass.AP(self.tab_loc, 0, [[1, 1], [1, E * CAP]]),
                    in_=self.zero128[:]), 0)
                sy.wait_ge(ev.h, EV6)
                self.ch_gbr.inc(sy.dma_start(out=self.gb_i[16:32, :],
                                             in_=self.gb_i[0:16, :]), 0)

        with nc.Block() as blk:
            @blk.gpsimd
            def _(gpe):
                gpe.wait_ge(mm.h, self.p3_mm_end)
                for kc in range(8):
                    self.ch_wi.inc(gpe.dma_start(
                        out=self.wi_bf[:, kc * IFF:(kc + 1) * IFF],
                        in_=self.inp["wi_c"].ap()[kc * 128:(kc + 1) * 128, :]), kc)
                gpe.wait_ge(mm.h, self.p4_mm_end)
                for ic in range(32):
                    self.ch_wo.inc(gpe.dma_start(
                        out=self.wo_bf[:, ic * D:(ic + 1) * D],
                        in_=self.inp["wo_c"].ap()[ic * 128:(ic + 1) * 128, :]), ic)
                gpe.wait_ge(ev.h, EV5)
                self.ch_tz.wait_all(gpe)
                for t in range(NT):
                    self.ch_sc.inc(gpe.indirect_dma_start(
                        self.tab_loc.ap(),
                        bass.IndirectOffsetOnAxis(self.r_a1u[:, t:t + 1], 0),
                        self.c_ids[:, t:t + 1], None,
                        bounds_check=E * CAP - 1, oob_is_err=False), t)
                self.ch_sc.wait_all(gpe)
                cc.inc(gpe.collective_compute(
                    "AllReduce", ALU.add, replica_groups=[list(range(NC))],
                    ins=[self.tab_loc.ap().opt()], outs=[self.tab_all.ap().opt()]), 1)
        self.p5_cc_tab = cc.n
        self.p5_cc_slab = CC_AGS

    def p6(self):
        nc = self.nc
        mm, ev, ac, cc = self.s_mm, self.s_ev, self.s_ac, self.s_cc
        MM0, EV0, AC0 = mm.n, ev.n, ac.n

        with nc.Block() as blk:
            @blk.sync
            def _(sy):
                sy.wait_ge(cc.h, self.p5_cc_tab)
                if "tabread" in KSKIP:
                    self.ch_tr.inc(sy.dma_start(
                        out=self.di_f[:],
                        in_=bass.AP(self.tab_all, 0, [[128, 16], [1, 128]])), 0)
                else:
                    with sy.register("tabreg") as reg:
                        sy.reg_load(reg, self.c_meta[0:1, 0:1])
                        tsrc = bass.AP(self.tab_all, reg, [[128, 16], [1, 128]])
                        self.ch_tr.inc(sy.dma_start(out=self.di_f[:], in_=tsrc), 0)

        with nc.Block() as blk:
            @blk.vector
            def _(ve):
                self.ch_tr.wait_all(ve)
                ve.memset(self.di_i[:], 0)
                ve.drain()
                ve.tensor_copy(self.di_i[0:16, :], self.di_f[:])
                self.incd(ve, ev)
        EV1 = ev.n

        with nc.Block() as blk:
            @blk.sync
            def _(sy):
                sy.wait_ge(ev.h, EV1)
                self.ch_direp.inc(sy.dma_start(out=self.di_i[16:32, :],
                                               in_=self.di_i[0:16, :]), 0)

        with nc.Block() as blk:
            @blk.gpsimd
            def _(gpe):
                gpe.load_library(library_config.mlp)
                self.ch_direp.wait_all(gpe)
                gpe.wait_ge(cc.h, self.p5_cc_slab)
                self.g_nreg = gpe.alloc_register("gnum")
                gpe.reg_mov(self.g_nreg, 256)
                for sg in range(8):
                    self.ch_dg.inc(gpe.dma_gather(
                        self.dispT[:, sg], self.slab_all.ap(),
                        self.di_i[:, sg * 16:(sg + 1) * 16],
                        256, self.g_nreg, D, transpose=True), sg)

        NSG = 8
        MM0, EV0, AC0 = mm.n, ev.n, ac.n
        mm_wi = lambda sg, ic: MM0 + 36 * sg + ic + 1
        mm_wo = lambda sg, stt, hf: MM0 + 36 * sg + 32 + stt * 2 + hf + 1
        ac_ge = lambda sg, ic: AC0 + 32 * sg + ic + 1
        ev_wo = lambda sg, stt, hf: EV0 + 4 * sg + stt * 2 + hf + 1

        with nc.Block() as blk:
            @blk.tensor
            def _(te):
                self.ch_dg.wait_all(te)
                self.ch_wi.wait_all(te)
                self.ch_wo.wait_all(te)
                for sg in range(NSG):
                    for ic in range(32):
                        bank = [self.pA, self.pB][ic % 2]
                        k = sg * 32 + ic
                        if k >= 2:
                            pk = k - 2
                            te.wait_ge(ac.h, ac_ge(pk // 32, pk % 32))
                        ins = None
                        for kc in range(8):
                            ins = te.matmul(bank[:, 0:256],
                                            self.wi_bf[:, kc * IFF + ic * 128: kc * IFF + (ic + 1) * 128],
                                            self.dispT[:, sg, kc],
                                            start=(kc == 0), stop=(kc == 7))
                        mm.inc(ins, 1)
                    for stt in range(2):
                        for hf in range(2):
                            bank = [self.pC, self.pD][(stt * 2 + hf) % 2]
                            k = sg * 4 + stt * 2 + hf
                            if k >= 2:
                                pk = k - 2
                                te.wait_ge(ev.h, ev_wo(pk // 4, (pk % 4) // 2, pk % 2))
                            te.wait_ge(ac.h, ac_ge(sg, 31))
                            ins = None
                            for ic in range(32):
                                ins = te.matmul(bank[:, 0:512],
                                                self.hT[:, ic * 256 + stt * 128: ic * 256 + (stt + 1) * 128],
                                                self.wo_bf[:, ic * D + hf * 512: ic * D + (hf + 1) * 512],
                                                start=(ic == 0), stop=(ic == 31))
                            mm.inc(ins, 1)

            @blk.scalar
            def _(sc):
                for sg in range(NSG):
                    for ic in range(32):
                        bank = [self.pA, self.pB][ic % 2]
                        sc.wait_ge(mm.h, mm_wi(sg, ic))
                        if sg >= 1:
                            sc.wait_ge(mm.h, mm_wo(sg - 1, 1, 1))
                        ac.inc(sc.activation(self.hT[:, ic * 256:(ic + 1) * 256],
                                             bank[:, 0:256], AF.Gelu), 1)

            @blk.vector
            def _(ve):
                for sg in range(NSG):
                    for stt in range(2):
                        for hf in range(2):
                            bank = [self.pC, self.pD][(stt * 2 + hf) % 2]
                            ve.wait_ge(mm.h, mm_wo(sg, stt, hf))
                            self.ch_eo.wait(ve, sg * 2 + stt - 2)
                            ev.inc(ve.tensor_copy(
                                self.bf_st[:, (stt % 2) * D + hf * 512:
                                           (stt % 2) * D + (hf + 1) * 512],
                                bank[:, 0:512]), 1)

            @blk.sync
            def _(sy):
                for sg in range(NSG):
                    for stt in range(2):
                        sy.wait_ge(ev.h, ev_wo(sg, stt, 1))
                        row = sg * 256 + stt * 128
                        self.ch_eo.inc(sy.dma_start(
                            out=self.eo_loc.ap()[row:row + 128, :],
                            in_=self.bf_st[:, (stt % 2) * D:(stt % 2 + 1) * D]), sg * 2 + stt)
        assert mm.n == MM0 + 36 * NSG and ac.n == AC0 + 32 * NSG
        assert ev.n == EV0 + 4 * NSG

        with nc.Block() as blk:
            @blk.gpsimd
            def _(gpe):
                self.ch_eo.wait_all(gpe)
                cc.inc(gpe.collective_compute(
                    "AllGather", ALU.bypass, replica_groups=[list(range(NC))],
                    ins=[self.eo_loc.ap().opt()], outs=[self.eo_all.ap().opt()]), 1)
        self.p6_cc_eo = cc.n

    def p7(self):
        nc = self.nc
        mm, ev, ac, cc = self.s_mm, self.s_ev, self.s_ac, self.s_cc
        EV0, AC0 = ev.n, ac.n

        with nc.Block() as blk:
            @blk.gpsimd
            def _(gpe):
                gpe.wait_ge(cc.h, self.p6_cc_eo)
                gpe.wait_ge(ev.h, self.p5_ev_gb)
                self.ch_gbr.wait_all(gpe)
                for c in range(NT // 2):
                    self.ch_gb2.inc(gpe.dma_gather(
                        self.comb[:, c], self.eo_all.ap(),
                        self.gb_i[:, c * 16:(c + 1) * 16],
                        256, self.g_nreg, D, transpose=False), c)

        ev_t = lambda t, j: EV0 + 5 * t + j
        ac_t = lambda t, j: AC0 + 2 * t + j

        with nc.Block() as blk:
            @blk.sync
            def _(sy):
                def p7_store(t):
                    sy.wait_ge(ev.h, ev_t(t, 5))
                    self.ch_out.inc(sy.dma_start(
                        out=self.out.ap()[t * 128:(t + 1) * 128, :],
                        in_=self.ost[:, (t % 2) * D:(t % 2 + 1) * D]), t)
                for t in range(NT):
                    if t >= 2:
                        sy.wait_ge(ev.h, ev_t(t - 2, 5))
                    self.ch_zy.inc(sy.dma_start(out=self.zst[:, (t % 2) * D:(t % 2 + 1) * D],
                                        in_=self.d_z.ap()[t * 128:(t + 1) * 128, :]), t)
                    self.ch_zy.inc(sy.dma_start(out=self.yst[:, (t % 2) * D:(t % 2 + 1) * D],
                                        in_=self.d_y.ap()[t * 128:(t + 1) * 128, :]), t)
                    if t >= 1:
                        p7_store(t - 1)
                p7_store(NT - 1)

            @blk.vector
            def _(ve):
                self.ch_gb2.wait_all(ve)
                for t in range(NT):
                    Ft = self.fst[:, (t % 2) * D:(t % 2 + 1) * D]
                    Ot = self.ost[:, (t % 2) * D:(t % 2 + 1) * D]
                    self.ch_zy.wait(ve, t)
                    self.ch_out.wait(ve, t - 2)
                    ve.scalar_tensor_tensor(Ft, self.comb.ap()[:, t // 2, t % 2, :],
                                            self.r_w[:, t:t + 1],
                                            self.zst[:, (t % 2) * D:(t % 2 + 1) * D],
                                            ALU.mult, ALU.add)
                    ve.reduce_sum(self.st_sum[:, t:t + 1], Ft, axis=AX.X)
                    ve.drain()
                    ve.tensor_scalar_mul(self.st_negmu[:, t:t + 1],
                                         self.st_sum[:, t:t + 1], -1.0 / D)
                    self.incd(ve, ev)
                    ve.wait_ge(ac.h, ac_t(t, 1))
                    ve.tensor_scalar(self.st_inv[:, t:t + 1], self.st_ssq[:, t:t + 1],
                                     1.0 / D, EPS, ALU.mult, ALU.add)
                    ve.drain()
                    ve.reciprocal(self.st_inv[:, t:t + 1], self.st_inv[:, t:t + 1])
                    self.incd(ve, ev)
                    ve.wait_ge(ac.h, ac_t(t, 2))
                    ve.tensor_mul(self.st_b2[:, t:t + 1], self.st_negmu[:, t:t + 1],
                                  self.st_inv[:, t:t + 1])
                    ve.drain()
                    ev.inc(ve.nop(), 1)
                    ev.inc(ve.tensor_scalar(Ot, Ft, self.st_inv[:, t:t + 1],
                                            self.st_b2[:, t:t + 1], ALU.mult, ALU.add), 1)
                    ev.inc(ve.tensor_add(Ot, Ot,
                                         self.yst[:, (t % 2) * D:(t % 2 + 1) * D]), 1)

            @blk.scalar
            def _(sc):
                for t in range(NT):
                    Ft = self.fst[:, (t % 2) * D:(t % 2 + 1) * D]
                    sc.wait_ge(ev.h, ev_t(t, 1))
                    sc.activation(self.disc[:], Ft, AF.Square,
                                  bias=self.st_negmu[:, t:t + 1], scale=1.0,
                                  accum_out=self.st_ssq[:, t:t + 1])
                    self.incd(sc, ac)
                    sc.wait_ge(ev.h, ev_t(t, 2))
                    sc.activation(self.st_inv[:, t:t + 1], self.st_inv[:, t:t + 1],
                                  AF.Sqrt)
                    self.incd(sc, ac)

        assert ev.n == EV0 + 5 * NT and ac.n == AC0 + 2 * NT

    def build(self):
        self.p0()
        if UPTO >= 1:
            self.p1()
        if UPTO >= 2:
            self.p2()
            self.p25()
        if UPTO >= 3:
            self.p3()
        self.p3_mm_end = self.s_mm.n
        if UPTO >= 4:
            self.p4()
        else:
            self.p4_mm_end = self.s_mm.n
        if UPTO >= 5:
            self.p5()
        if UPTO >= 6:
            self.p6()
        if UPTO >= 7:
            self.p7()
        self.finish()
        lower_extended_insts(self.nc)
        return self.nc

    def finish(self):
        nc = self.nc
        tapmap = {"ln1": self.d_ln1, "dx": self.d_x, "cat": self.d_cat,
                  "y": self.d_y, "z": self.d_z, "slab": self.slab_loc,
                  "tab": self.tab_all, "eo": self.eo_loc, "offs": self.offs_d}
        taps = [(tapmap[n], "t_" + n) for n in sorted(TAPS) if n in tapmap]
        sbtaps = []
        for n in sorted(TAPS):
            if n.startswith("sb:"):
                _, attr, rows, cols = n.split(":")
                t = getattr(self, attr)
                sbtaps.append((t, "t_" + attr, int(rows), int(cols)))
        tap_ch = Chan(nc, "ctap")
        with nc.Block() as blk:
            @blk.sync
            def _(sy):
                for s in [self.s_mm, self.s_ev, self.s_ac, self.s_cc, self.s_gp]:
                    if s.n:
                        sy.wait_ge(s.h, s.n)
                for ch in [self.ch_const, self.ch_wc, self.ch_x, self.ch_dx,
                           self.ch_l1, self.ch_t2, self.ch_cat, self.ch_t4,
                           self.ch_xb, self.ch_y, self.ch_z, self.ch_slab,
                           self.ch_t5, self.ch_cnt, self.ch_cna, self.ch_off,
                           self.ch_orow, self.ch_rdf, self.ch_tz, self.ch_gbr,
                           self.ch_wi, self.ch_wo, self.ch_sc, self.ch_tr,
                           self.ch_direp, self.ch_dg, self.ch_eo, self.ch_gb2,
                           self.ch_zy, self.ch_out, self.ch_grp]:
                    ch.wait_all(sy)
                i = 0
                for srct, name in taps:
                    t = nc.dram_tensor(name, list(srct.shape), srct.ap().dtype,
                                       kind="ExternalOutput")
                    tap_ch.inc(sy.dma_start(out=t.ap(), in_=srct.ap()), i)
                    i += 1
                for srct, name, rows, cols in sbtaps:
                    t = nc.dram_tensor(name, [rows, cols], srct.ap().dtype,
                                       kind="ExternalOutput")
                    tap_ch.inc(sy.dma_start(out=t.ap(), in_=srct[0:rows, 0:cols]), i)
                    i += 1
                if UPTO < 7:
                    tap_ch.inc(sy.dma_start(out=self.out.ap()[0:128, 0:128],
                                            in_=self.c_onesf[:]), i)
                tap_ch.wait_all(sy)


def host_inputs(inputs):
    x = np.ascontiguousarray(inputs["L0_attn_x"], dtype=np.float32)
    Wq = np.ascontiguousarray(inputs["L0_attn_Wq"], dtype=np.float32)
    Wk = np.ascontiguousarray(inputs["L0_attn_Wk"], dtype=np.float32)
    Wv = np.ascontiguousarray(inputs["L0_attn_Wv"], dtype=np.float32)
    Wp = np.ascontiguousarray(inputs["L0_attn_Wproj"], dtype=np.float32)
    bq = np.asarray(inputs["L0_attn_bq"], dtype=np.float32)
    bk = np.asarray(inputs["L0_attn_bk"], dtype=np.float32)
    bv = np.asarray(inputs["L0_attn_bv"], dtype=np.float32)
    bp = np.asarray(inputs["L0_attn_bproj"], dtype=np.float32)
    wqf = np.asarray(inputs["L0_attn_wq_feat"], dtype=np.float32)
    wkf = np.asarray(inputs["L0_attn_wk_feat"], dtype=np.float32)
    gw = np.asarray(inputs["L0_ffn_gate_weight"], dtype=np.float32)
    lng = np.asarray(inputs["L0_ffn_ln_gamma"], dtype=np.float32)
    lnb = np.asarray(inputs["L0_ffn_ln_beta"], dtype=np.float32)
    wi = np.ascontiguousarray(inputs["L0_ffn_wi"], dtype=np.float32)
    wo = np.ascontiguousarray(inputs["L0_ffn_wo"], dtype=np.float32)

    wqf4 = np.zeros((128, 512), np.float32)
    wkf2 = np.zeros((128, 512), np.float32)
    for par in range(2):
        for fh in range(2):
            v = par * 2 + fh
            wqf4[par * 64:(par + 1) * 64, v * 128:(v + 1) * 128] = \
                wqf[:, fh * 128:(fh + 1) * 128]
        wkf2[par * 64:(par + 1) * 64, par * 256:(par + 1) * 256] = wkf
    wgt = gw.T.reshape(8, 128, 8).transpose(1, 0, 2).reshape(128, 64).copy()
    sel8 = np.zeros((128, 128), np.float32)
    for r in range(8):
        for q in range(16):
            sel8[16 * r + q, r * 16 + q] = 1.0
    shared = dict(
        wq=Wq, wk=Wk, wv=Wv, wp=Wp,
        bq=np.ascontiguousarray(bq.reshape(8, 128).T),
        bk=np.ascontiguousarray(bk.reshape(8, 128).T),
        bvr=bv.reshape(1, D).copy(), bpr=bp.reshape(1, D).copy(),
        wqf4=wqf4, wkf2=wkf2, wgt=wgt,
        grep=np.ascontiguousarray(np.broadcast_to(lng, (128, D))),
        brep=np.ascontiguousarray(np.broadcast_to(lnb, (128, D))),
        triu=np.triu(np.ones((128, 128), np.float32), 1),
        onesc=np.ones((128, 128), np.float32),
        identc=np.eye(128, dtype=np.float32),
        sel8=sel8,
        iota8u=np.ascontiguousarray(np.broadcast_to(np.arange(8, dtype=np.uint32),
                                                    (128, 8))),
    )
    in_maps = []
    for c in range(NC):
        ids = (c * S + np.arange(NT)[None, :] * 128 +
               np.arange(128)[:, None]).astype(np.float32)
        pref = np.zeros((128, 1), np.float32)
        pref[:c, 0] = 1.0
        meta = np.array([[c * CAP, c]], dtype=np.uint32)
        m = dict(shared)
        m.update(x=x[c], wi_c=wi[c], wo_c=wo[c], ids=ids, pref=pref, meta_u=meta)
        in_maps.append(m)
    return in_maps


def build_and_run(inputs):
    in_maps = host_inputs(inputs)
    b = Bld()
    nc = b.build()
    res = run_bass_kernel_spmd(nc, in_maps, core_ids=list(range(NC)))
    return res


def kernel(**inputs):
    res = build_and_run(inputs)
    out = np.stack([np.asarray(r["out"]) for r in res.results], axis=0)
    return out.astype(np.float32)


if __name__ == "__main__":
    import reference as refmod
    ins = refmod.setup_inputs(0)
    out = kernel(**{k: np.asarray(v) for k, v in ins.items()})
    print(out.shape, out.dtype)


# version 02
# speedup vs baseline: 1.0335x; 1.0335x over previous
import os
import numpy as np
import ml_dtypes
import concourse.bass as bass
import concourse.mybir as mybir
import concourse.library_config as library_config
from concourse.library_overlay import lower_extended_insts
from concourse.bass_utils import run_bass_kernel_spmd

F32 = mybir.dt.float32
BF16 = mybir.dt.bfloat16
I16 = mybir.dt.int16
U32 = mybir.dt.uint32
AF = mybir.ActivationFunctionType
ALU = mybir.AluOpType
AX = mybir.AxisListType

S, D, H, DH, FF, E, CAP, IFF = 4096, 1024, 16, 64, 256, 8, 2048, 4096
NT, NG, TPG, NC = 32, 8, 4, 8
EPS = 1e-5
DROP = 60000.0

OFF_A = 34816
OFF_B = 100352
OFF_C = 165888
OFF_D = 198656
OFF_OV = 206848

UPTO = int(os.environ.get("KUPTO", "7"))
KSKIP = set(s for s in os.environ.get("KSKIP", "").split(",") if s)
TAPS = set(t for t in os.environ.get("KTAPS", "").split(",") if t)


class Sem:
    def __init__(self, nc, name):
        self.h = nc.alloc_semaphore(name)
        self.n = 0

    def inc(self, ins, k=1):
        ins.then_inc(self.h, k)
        self.n += k
        return ins


class Chan:

    def __init__(self, nc, name, k=1, per_slot=1):
        self.sems = [nc.alloc_semaphore(f"{name}{i}") for i in range(k)]
        self.k = k
        self.per_slot = per_slot
        self.tot = [0] * k
        self.cnt = {}

    def inc(self, ins, slot):
        j = slot % self.k
        self.tot[j] += 16
        ins.then_inc(self.sems[j], 16)
        self.cnt[slot] = self.cnt.get(slot, 0) + 1
        assert self.cnt[slot] <= self.per_slot

    def wait(self, eng, slot):
        if slot < 0:
            return
        v = 16 * self.per_slot * (slot // self.k + 1)
        eng.wait_ge(self.sems[slot % self.k], v)

    def wait_all(self, eng):
        for j in range(self.k):
            if self.tot[j]:
                eng.wait_ge(self.sems[j], self.tot[j])


class Bld:
    def __init__(self):
        nc = bass.Bass("TRN2", target_bir_lowering=False, detect_race_conditions=False)
        self.nc = nc
        dram = lambda n, s, d, k="Internal": nc.dram_tensor(n, s, d, kind=k)
        drsh = lambda n, s, d: nc.dram_tensor(n, s, d, kind="Internal",
                                              addr_space="Shared")
        self.inp = {}
        for name, shape in [
                ("x", [S, D]), ("wq", [D, D]), ("wk", [D, D]), ("wv", [D, D]),
                ("wp", [D, D]), ("bq", [128, 8]), ("bk", [128, 8]),
                ("bvr", [1, D]), ("bpr", [1, D]), ("wqf4", [128, 512]),
                ("wkf2", [128, 512]), ("wgt", [128, 64]), ("grep", [128, D]),
                ("brep", [128, D]), ("wi_c", [D, IFF]), ("wo_c", [IFF, D]),
                ("triu", [128, 128]), ("onesc", [128, 128]),
                ("identc", [128, 128]), ("sel8", [128, 128]),
                ("ids", [128, NT]), ("pref", [128, 1])]:
            self.inp[name] = dram(name, shape, F32, "ExternalInput")
        self.inp["iota8u"] = dram("iota8u", [128, 8], U32, "ExternalInput")
        self.inp["meta_u"] = dram("meta_u", [1, 2], U32, "ExternalInput")
        self.out = dram("out", [S, D], F32, "ExternalOutput")

        self.d_x = dram("d_x", [S, D], BF16)
        self.d_ln1 = dram("d_ln1", [S, D], BF16)
        self.d_cat = dram("d_cat", [S, D], BF16)
        self.d_y = dram("d_y", [S, D], BF16)
        self.d_z = dram("d_z", [S, D], BF16)
        self.slab_loc = dram("slab_loc", [S, D], BF16)
        self.slab_all = drsh("slab_all", [NC * S, D], BF16)
        self.cnt_loc = dram("cnt_loc", [1, E], F32)
        self.cnt_all = drsh("cnt_all", [NC, E], F32)
        self.tab_loc = dram("tab_loc", [E * CAP, 1], F32)
        self.tab_all = drsh("tab_all", [E * CAP, 1], F32)
        self.eo_loc = dram("eo_loc", [CAP, D], BF16)
        self.eo_all = drsh("eo_all", [E * CAP, D], BF16)
        self.offs_d = dram("offs_d", [NT, E], F32)

        sb = nc.alloc_sbuf_tensor
        self.c_wqf4 = sb("c_wqf4", [128, 512], BF16)
        self.c_wkf2 = sb("c_wkf2", [128, 512], BF16)
        self.c_bq = sb("c_bq", [128, 8], F32)
        self.c_bk = sb("c_bk", [128, 8], F32)
        self.c_bvr = sb("c_bvr", [1, D], BF16)
        self.c_bpr = sb("c_bpr", [1, D], BF16)
        self.c_wgt = sb("c_wgt", [128, 64], BF16)
        self.c_triu = sb("c_triu", [128, 128], BF16)
        self.c_ones = sb("c_ones", [128, 128], BF16)
        self.c_onesf = sb("c_onesf", [128, 128], F32)
        self.c_identb = sb("c_identb", [128, 128], BF16)
        self.c_identf = sb("c_identf", [128, 128], F32)
        self.c_sel8 = sb("c_sel8", [128, 128], F32)
        self.c_iota8u = sb("c_iota8u", [128, 8], U32)
        self.c_iota8f = sb("c_iota8f", [128, 8], F32)
        self.c_ids = sb("c_ids", [128, NT], F32)
        self.c_pref = sb("c_pref", [128, 1], F32)
        self.c_meta = sb("c_meta", [1, 2], U32)
        self.zero128 = sb("zero128", [128, 128], F32)
        self.st_sum = sb("st_sum", [128, NT], F32)
        self.st_ssq = sb("st_ssq", [128, NT], F32)
        self.st_negmu = sb("st_negmu", [128, NT], F32)
        self.st_inv = sb("st_inv", [128, NT], F32)
        self.st_b2 = sb("st_b2", [128, NT], F32)
        self.st2_ssq = sb("st2_ssq", [128, NT], F32)
        self.st2_negmu = sb("st2_negmu", [128, NT], F32)
        self.st2_inv = sb("st2_inv", [128, NT], F32)
        self.st2_b2 = sb("st2_b2", [128, NT], F32)
        self.r_rank = sb("r_rank", [128, NT * E], F32)
        self.r_ohall = sb("r_ohall", [128, NT * E], BF16)
        self.r_w = sb("r_w", [128, NT], F32)
        self.r_keep = sb("r_keep", [128, NT], F32)
        self.r_e = sb("r_e", [128, NT], F32)
        self.r_rk = sb("r_rk", [128, NT], F32)
        self.r_rm = sb("r_rm", [128, NT], F32)
        self.r_rdi = sb("r_rdi", [128, 2 * NT], I16)
        self.r_rdf = sb("r_rdf", [128, 2 * NT], F32)
        self.r_a1 = sb("r_a1", [128, NT], F32)
        self.r_a1u = sb("r_a1u", [128, NT], U32)
        self.r_a2 = sb("r_a2", [128, NT], F32)
        self.r_t1 = sb("r_t1", [128, NT], F32)
        self.r_max = sb("r_max", [128, 8], F32)
        self.r_idx = sb("r_idx", [128, 8], U32)
        self.r_oh = sb("r_oh", [128, 8], BF16)
        self.r_ohf = sb("r_ohf", [128, 8], F32)
        self.r_exp = sb("r_exp", [128, 8], F32)
        self.r_s1 = sb("r_s1", [128, 1], F32)
        self.r_cnt = sb("r_cnt", [8, NT], F32)
        self.r_cntall = sb("r_cntall", [8, 8], F32)
        self.r_scan = sb("r_scan", [8, NT], F32)
        self.r_offs = sb("r_offs", [8, NT], F32)
        self.r_base = sb("r_base", [8, 1], F32)
        self.r_offsT = sb("r_offsT", [NT, 8], F32)
        self.r_offsrow = sb("r_offsrow", [1, NT * E], F32)
        self.gb_f = sb("gb_f", [16, 256], F32)
        self.gb_i = sb("gb_i", [128, 256], I16)
        self.di_f = sb("di_f", [16, 128], F32)
        self.di_i = sb("di_i", [128, 128], I16)
        self.dall = sb("dall", [128, 16], F32)
        self.dinv = sb("dinv", [128, 16], F32)

        at = nc.alloc_sbuf_tensor_at
        self.qT = at("qT", [128, 8 * S], BF16, offset=OFF_A)
        self.wi_bf = at("wi_bf", [128, 8 * IFF], BF16, offset=OFF_A)
        self.comb = at("comb", [128, NT // 2, 2, D], BF16, offset=OFF_A)
        self.wq_bf = at("wq_bf", [128, 8 * D], BF16, offset=OFF_B)
        self.wk_bf = at("wk_bf", [128, 8 * D], BF16, offset=OFF_B + 16384)
        self.wv_bf = at("wv_bf", [128, 8 * D], BF16, offset=OFF_B + 32768)
        self.wp_bf = at("wp_bf", [128, 8 * D], BF16, offset=OFF_B + 49152)
        self.wo_bf = at("wo_bf", [128, 32 * D], BF16, offset=OFF_B)
        self.lnxT = at("lnxT", [128, 8 * 512], BF16, offset=OFF_C)
        self.kT = at("kT", [128, 8 * 512], BF16, offset=OFF_C + 8192)
        self.v_g = at("v_g", [128, TPG * D], BF16, offset=OFF_C + 16384)
        self.kp = at("kp", [128, 2 * FF], BF16, offset=OFF_C + 24576)
        self.qp = at("qp", [128, 32 * 512], BF16, offset=OFF_C)
        self.catT = at("catT", [128, 8 * 512], BF16, offset=OFF_C)
        self.ln2tT = at("ln2tT", [128, 8 * 512], BF16, offset=OFF_C)
        self.dispT = at("dispT", [128, 8, 8, 256], BF16, offset=OFF_C)
        self.bf_st = at("bf_st", [128, 4 * D], BF16, offset=OFF_D)
        self.disc = at("disc", [128, D], BF16, offset=OFF_C + 30720)
        self.xs = at("xs", [128, 2 * D], F32, offset=OFF_OV)
        self.ctx = at("ctx", [128, H * FF], F32, offset=OFF_OV)
        self.ctxb = at("ctxb", [128, H * 130], BF16, offset=OFF_OV)
        self.c_grep = at("c_grep", [128, D], BF16, offset=OFF_OV)
        self.c_brep = at("c_brep", [128, D], BF16, offset=OFF_OV + 2048)
        self.xb = at("xb", [128, 2 * D], BF16, offset=OFF_OV + 4096)
        self.hT = at("hT", [128, 32 * 256], BF16, offset=OFF_OV)
        self.zst = at("zst", [128, 2 * D], BF16, offset=OFF_OV)
        self.yst = at("yst", [128, 2 * D], BF16, offset=OFF_OV + 4096)
        self.fst = at("fst", [128, 2 * D], BF16, offset=OFF_OV + 8192)
        self.ost = at("ost", [128, 2 * D], F32, offset=OFF_OV + 12288)

        ps = nc.alloc_psum_tensor
        self.pA = ps("pA", [128, 512], F32)
        self.pB = ps("pB", [128, 512], F32)
        self.pC = ps("pC", [128, 512], F32)
        self.pD = ps("pD", [128, 512], F32)
        self.pE = ps("pE", [128, 512], F32)
        self.pF = ps("pF", [128, 512], F32)
        self.pG = ps("pG", [128, 512], F32)
        self.pT = ps("pT", [128, 1024], BF16)

        self.s_ld = Sem(nc, "s_ld")
        self.s_gd = Sem(nc, "s_gd")
        self.s_td = Sem(nc, "s_td")
        self.s_st = Sem(nc, "s_st")
        self.s_mm = Sem(nc, "s_mm")
        self.s_ev = Sem(nc, "s_ev")
        self.s_ac = Sem(nc, "s_ac")
        self.s_cc = Sem(nc, "s_cc")
        self.s_gp = Sem(nc, "s_gp")
        self.s_slabg = Sem(nc, "s_slabg")
        self.s_eog = Sem(nc, "s_eog")
        C = lambda n, k=1: Chan(nc, n, k)
        self.ch_const = C("ccst")
        self.ch_wc = C("cwc")
        self.ch_x = C("cx", 2)
        self.ch_dx = C("cdx", 2)
        self.ch_l1 = C("cl1", 2)
        self.ch_t2 = Chan(nc, "ct2", 2, per_slot=8)
        self.ch_cat = C("ccat", 2)
        self.ch_t4 = Chan(nc, "ct4", 2, per_slot=8)
        self.ch_xb = C("cxb", 2)
        self.ch_y = C("cy", 2)
        self.ch_z = C("cz", 2)
        self.ch_slab = C("cslab", 2)
        self.ch_t5 = Chan(nc, "ct5", 2, per_slot=8)
        self.ch_cnt = C("ccnt")
        self.ch_cna = C("ccna")
        self.ch_off = C("coff")
        self.ch_orow = C("corow")
        self.ch_rdf = C("crdf")
        self.ch_tz = C("ctz")
        self.ch_gbr = C("cgbr")
        self.ch_wi = C("cwi")
        self.ch_wo = C("cwo")
        self.ch_sc = C("csc")
        self.ch_tr = C("ctr")
        self.ch_direp = C("cdirep")
        self.ch_dg = C("cdg")
        self.ch_eo = C("ceo", 2)
        self.ch_gb2 = C("cgb2")
        self.ch_zy = Chan(nc, "czy", 2, per_slot=2)
        self.ch_out = C("cout", 2)
        self.ch_grp = C("cgrp")

    def incd(self, eng, sem, k=1):
        eng.drain()
        sem.inc(eng.nop(), k)

    def p0(self):
        nc = self.nc
        with nc.Block() as blk:
            @blk.sync
            def _(sy):
                for i, (dst, src) in enumerate([
                        (self.c_bq, "bq"), (self.c_bk, "bk"),
                        (self.c_sel8, "sel8"), (self.c_ids, "ids"),
                        (self.c_pref, "pref"), (self.c_iota8u, "iota8u"),
                        (self.c_onesf, "onesc"), (self.c_identf, "identc"),
                        (self.c_meta, "meta_u")]):
                    self.ch_const.inc(sy.dma_start(out=dst[:], in_=self.inp[src].ap()), i)

            @blk.gpsimd
            def _(gp):
                i = 0
                for dst, src in [(self.c_wqf4, "wqf4"), (self.c_wkf2, "wkf2"),
                                 (self.c_bvr, "bvr"), (self.c_bpr, "bpr"),
                                 (self.c_wgt, "wgt"), (self.c_triu, "triu"),
                                 (self.c_ones, "onesc"), (self.c_identb, "identc"),
                                 (self.c_iota8f, "iota8u")]:
                    self.ch_wc.inc(gp.dma_start(out=dst[:], in_=self.inp[src].ap()), i)
                    i += 1
                for kc in range(8):
                    for wbf, wsrc in [(self.wq_bf, "wq"), (self.wk_bf, "wk"),
                                      (self.wv_bf, "wv"), (self.wp_bf, "wp")]:
                        self.ch_wc.inc(gp.dma_start(
                            out=wbf[:, kc * D:(kc + 1) * D],
                            in_=self.inp[wsrc].ap()[kc * 128:(kc + 1) * 128, :]), i)
                        i += 1

            @blk.vector
            def _(ve):
                self.ch_const.wait_all(ve)
                self.s_ev.inc(ve.memset(self.zero128[:], 0.0), 1)
        self.p0_zero = self.s_ev.n

    def p1(self):
        nc = self.nc
        ld, ev, ac, st = self.s_ld, self.s_ev, self.s_ac, self.s_st
        EV0, AC0 = ev.n, ac.n
        ev_t = lambda t, j: EV0 + 5 * t + j
        ac_t = lambda t, j: AC0 + 2 * t + j

        def p1_store(sy, t):
            sy.wait_ge(ev.h, ev_t(t, 2))
            self.ch_dx.inc(sy.dma_start(out=self.d_x.ap()[t * 128:(t + 1) * 128, :],
                                        in_=self.bf_st[:, (t % 2) * D:(t % 2 + 1) * D]), t)
            sy.wait_ge(ev.h, ev_t(t, 5))
            self.ch_l1.inc(sy.dma_start(out=self.d_ln1.ap()[t * 128:(t + 1) * 128, :],
                                        in_=self.bf_st[:, (2 + t % 2) * D:(3 + t % 2) * D]), t)

        with nc.Block() as blk:
            @blk.sync
            def _(sy):
                for t in range(NT):
                    if t >= 2:
                        sy.wait_ge(ev.h, ev_t(t - 2, 5))
                    self.ch_x.inc(sy.dma_start(out=self.xs[:, (t % 2) * D:(t % 2 + 1) * D],
                                        in_=self.inp["x"].ap()[t * 128:(t + 1) * 128, :]), t)
                    if t >= 2:
                        p1_store(sy, t - 2)
                for t in range(NT - 2, NT):
                    p1_store(sy, t)

            @blk.vector
            def _(ve):
                for t in range(NT):
                    xt = self.xs[:, (t % 2) * D:(t % 2 + 1) * D]
                    self.ch_x.wait(ve, t)
                    ve.reduce_sum(self.st_sum[:, t:t + 1], xt, axis=AX.X)
                    ve.drain()
                    ve.tensor_scalar_mul(self.st_negmu[:, t:t + 1],
                                         self.st_sum[:, t:t + 1], -1.0 / D)
                    self.incd(ve, ev)
                    self.ch_dx.wait(ve, t - 2)
                    ev.inc(ve.tensor_copy(self.bf_st[:, (t % 2) * D:(t % 2 + 1) * D], xt), 1)
                    ve.wait_ge(ac.h, ac_t(t, 1))
                    ve.tensor_scalar(self.st_inv[:, t:t + 1], self.st_ssq[:, t:t + 1],
                                     1.0 / D, EPS, ALU.mult, ALU.add)
                    ve.drain()
                    ve.reciprocal(self.st_inv[:, t:t + 1], self.st_inv[:, t:t + 1])
                    self.incd(ve, ev)
                    ve.wait_ge(ac.h, ac_t(t, 2))
                    ve.tensor_mul(self.st_b2[:, t:t + 1], self.st_negmu[:, t:t + 1],
                                  self.st_inv[:, t:t + 1])
                    ve.drain()
                    self.incd(ve, ev)
                    self.ch_l1.wait(ve, t - 2)
                    ev.inc(ve.tensor_scalar(self.bf_st[:, (2 + t % 2) * D:(3 + t % 2) * D],
                                            xt, self.st_inv[:, t:t + 1],
                                            self.st_b2[:, t:t + 1], ALU.mult, ALU.add), 1)

            @blk.scalar
            def _(sc):
                for t in range(NT):
                    xt = self.xs[:, (t % 2) * D:(t % 2 + 1) * D]
                    sc.wait_ge(ev.h, ev_t(t, 1))
                    sc.activation(self.disc[:], xt, AF.Square,
                                  bias=self.st_negmu[:, t:t + 1], scale=1.0,
                                  accum_out=self.st_ssq[:, t:t + 1])
                    self.incd(sc, ac)
                    sc.wait_ge(ev.h, ev_t(t, 3))
                    sc.activation(self.st_inv[:, t:t + 1], self.st_inv[:, t:t + 1],
                                  AF.Sqrt)
                    self.incd(sc, ac)

        assert ev.n == EV0 + 5 * NT and ac.n == AC0 + 2 * NT

    def p2(self):
        nc = self.nc
        mm, ev, ac = self.s_mm, self.s_ev, self.s_ac
        MM0, EV0, AC0 = mm.n, ev.n, ac.n
        GPT, EVG, ACG = 152, 41, 64
        mm_qk = lambda g, w, oc: MM0 + GPT * g + w * 8 + oc + 1
        mm_v = lambda g, tl, hf: MM0 + GPT * g + 16 + tl * 2 + hf + 1
        mm_kp = lambda g, h, tl: MM0 + GPT * g + 24 + 8 * h + 2 * tl + 1
        mm_ctx = lambda g, h: MM0 + GPT * g + 24 + 8 * h + 8
        ev_qk = lambda g, w, oc: EV0 + EVG * g + w * 8 + oc + 1
        ev_v = lambda g, tl, hf: EV0 + EVG * g + 16 + tl * 2 + hf + 1
        ev_fold = lambda g, h: EV0 + EVG * g + 24 + h + 1
        ac_exp = lambda g, h, tl: AC0 + ACG * g + 4 * h + tl + 1

        with nc.Block() as blk:
            @blk.scalar
            def _(sc):
                for g in range(NG):
                    if g == 0:
                        self.ch_l1.wait_all(sc)
                    if g >= 1:
                        sc.wait_ge(mm.h, mm_v(g - 1, TPG - 1, 1))
                    for kc in range(8):
                        self.ch_t2.inc(sc.dma_start(
                            out=self.lnxT[:, kc * 512:(kc + 1) * 512],
                            in_=self.d_ln1.ap()[g * 512:(g + 1) * 512,
                                                kc * 128:(kc + 1) * 128],
                            transpose=True), g)
                    for h in range(H):
                        for tl in range(TPG):
                            sc.wait_ge(mm.h, mm_kp(g, h, tl))
                            bank = [self.pD, self.pE][tl % 2]
                            ac.inc(sc.activation(
                                self.kp[:, (tl % 2) * FF:(tl % 2 + 1) * FF],
                                bank[:, 0:256], AF.Exp), 1)

            @blk.tensor
            def _(te):
                self.ch_wc.wait_all(te)
                self.ch_const.wait_all(te)
                for g in range(NG):
                    self.ch_t2.wait(te, g)
                    for w, wbf in [(0, self.wq_bf), (1, self.wk_bf)]:
                        for oc in range(8):
                            bank = [self.pA, self.pB][oc % 2]
                            k = 2 * g * 8 + w * 8 + oc
                            if k >= 2:
                                pk = k - 2
                                pg, r = divmod(pk, 16)
                                te.wait_ge(ev.h, ev_qk(pg, r // 8, r % 8))
                            ins = None
                            for i in range(8):
                                ins = te.matmul(bank[:, 0:512],
                                                wbf[:, i * D + oc * 128: i * D + (oc + 1) * 128],
                                                self.lnxT[:, i * 512:(i + 1) * 512],
                                                start=(i == 0), stop=(i == 7))
                            mm.inc(ins, 1)
                    for tl in range(TPG):
                        for hf in range(2):
                            k = g * 8 + tl * 2 + hf
                            if k >= 1:
                                pg, r = divmod(k - 1, 8)
                                te.wait_ge(ev.h, ev_v(pg, r // 2, r % 2))
                            for i in range(8):
                                te.matmul(self.pC[:, 0:512],
                                          self.lnxT[:, i * 512 + tl * 128: i * 512 + (tl + 1) * 128],
                                          self.wv_bf[:, i * D + hf * 512: i * D + (hf + 1) * 512],
                                          start=(i == 0), stop=False)
                            ins = te.matmul(self.pC[:, 0:512], self.c_ones[0:1, :],
                                            self.c_bvr[0:1, hf * 512:(hf + 1) * 512],
                                            start=False, stop=True)
                            mm.inc(ins, 1)
                    te.wait_ge(ev.h, ev_qk(g, 1, 7))
                    te.wait_ge(ev.h, ev_v(g, TPG - 1, 1))
                    for h in range(H):
                        ch, par = h // 2, h % 2
                        cbank = [self.pF, self.pG][h % 2]
                        if h >= 2:
                            te.wait_ge(ev.h, ev_fold(g, h - 2))
                        elif g >= 1:
                            te.wait_ge(ev.h, ev_fold(g - 1, H - 2 + h))
                        for tl in range(TPG):
                            bank = [self.pD, self.pE][tl % 2]
                            k = (g * H + h) * TPG + tl
                            if k >= 2:
                                pk = k - 2
                                pg2, r = divmod(pk, H * TPG)
                                te.wait_ge(ac.h, ac_exp(pg2, r // TPG, r % TPG))
                            ins = te.matmul(bank[:, 0:256],
                                            self.kT[:, ch * 512 + tl * 128: ch * 512 + (tl + 1) * 128],
                                            self.c_wkf2[:, par * 256:(par + 1) * 256],
                                            start=True, stop=True)
                            mm.inc(ins, 1)
                            te.wait_ge(ac.h, ac_exp(g, h, tl))
                            kpb = self.kp[:, (tl % 2) * FF:(tl % 2 + 1) * FF]
                            te.matmul(cbank[0:64, 0:256],
                                      self.v_g[:, tl * D + h * DH: tl * D + (h + 1) * DH],
                                      kpb, start=(tl == 0), stop=False,
                                      skip_group_check=True)
                            ins = te.matmul(cbank[64:65, 0:256], self.c_ones[:, 0:1], kpb,
                                            start=(tl == 0), stop=(tl == TPG - 1),
                                            skip_group_check=True)
                            mm.inc(ins, 1)

            @blk.vector
            def _(ve):
                for g in range(NG):
                    for w in range(2):
                        for oc in range(8):
                            bank = [self.pA, self.pB][oc % 2]
                            ve.wait_ge(mm.h, mm_qk(g, w, oc))
                            dst = (self.qT[:, oc * S + g * 512: oc * S + (g + 1) * 512]
                                   if w == 0 else self.kT[:, oc * 512:(oc + 1) * 512])
                            bias = (self.c_bq if w == 0 else self.c_bk)[:, oc:oc + 1]
                            ev.inc(ve.tensor_scalar_add(dst, bank[:, 0:512], bias), 1)
                    for tl in range(TPG):
                        for hf in range(2):
                            ve.wait_ge(mm.h, mm_v(g, tl, hf))
                            ev.inc(ve.tensor_copy(
                                self.v_g[:, tl * D + hf * 512: tl * D + (hf + 1) * 512],
                                self.pC[:, 0:512]), 1)
                    for h in range(H):
                        cbank = [self.pF, self.pG][h % 2]
                        ve.wait_ge(mm.h, mm_ctx(g, h))
                        cslice = self.ctx[0:65, h * FF:(h + 1) * FF]
                        if g == 0:
                            ev.inc(ve.tensor_copy(cslice, cbank[0:65, 0:256]), 1)
                        else:
                            ev.inc(ve.tensor_add(cslice, cslice, cbank[0:65, 0:256]), 1)
                    ev.inc(ve.nop(), 1)
        assert mm.n == MM0 + GPT * NG and ev.n == EV0 + EVG * NG and ac.n == AC0 + ACG * NG

    def p25(self):
        nc = self.nc
        mm, ev = self.s_mm, self.s_ev
        MM0, EV0 = mm.n, ev.n
        with nc.Block() as blk:
            @blk.vector
            def _(ve):
                for h in range(H):
                    ev.inc(ve.tensor_copy(self.bf_st[0:65, h * FF:(h + 1) * FF],
                                          self.ctx[0:65, h * FF:(h + 1) * FF]), 1)

            @blk.tensor
            def _(te):
                te.wait_ge(ev.h, EV0 + H)
                for j in range(2 * H):
                    h, cc = divmod(j, 2)
                    if j >= 1:
                        te.wait_ge(ev.h, EV0 + H + j)
                    ins = te.transpose(self.pT[:, 0:65],
                                       self.bf_st[0:65, h * FF + cc * 128: h * FF + cc * 128 + 128],
                                       self.c_identb[0:65, 0:65])
                    mm.inc(ins, 1)

            @blk.vector
            def _(ve):
                for j in range(2 * H):
                    h, cc = divmod(j, 2)
                    ve.wait_ge(mm.h, MM0 + j + 1)
                    ev.inc(ve.tensor_copy(
                        self.ctxb[:, h * 130 + cc * 65: h * 130 + (cc + 1) * 65],
                        self.pT[:, 0:65]), 1)
        self.p25_ev = ev.n

    def p3(self):
        nc = self.nc
        mm, ev, ac = self.s_mm, self.s_ev, self.s_ac
        MM0, EV0, AC0 = mm.n, ev.n, ac.n
        GT = 32 + TPG * H
        mm_qp = lambda g, h, fh: MM0 + GT * g + h * 2 + fh + 1
        mm_nm = lambda g, tl, h: MM0 + GT * g + 32 + tl * H + h + 1
        ac_qp = lambda g, h, fh: AC0 + 32 * g + h * 2 + fh + 1
        ev_tile = lambda g, tl: EV0 + 32 * g + 8 * (tl + 1)
        NBK = [(0, 7, "pD"), (7, 14, "pE"), (14, 16, "pF")]

        with nc.Block() as blk:
            @blk.tensor
            def _(te):
                te.wait_ge(ev.h, self.p25_ev)
                for g in range(NG):
                    for h in range(H):
                        for fh in range(2):
                            bank = [self.pA, self.pB][(h * 2 + fh) % 2]
                            k = g * 32 + h * 2 + fh
                            if k >= 2:
                                pk = k - 2
                                pg, r = divmod(pk, 32)
                                te.wait_ge(ac.h, ac_qp(pg, r // 2, r % 2))
                            ins = te.matmul(bank[:, 0:512],
                                            self.c_wqf4[:, (h % 2) * 256 + fh * 128:
                                                        (h % 2) * 256 + (fh + 1) * 128],
                                            self.qT[:, (h // 2) * S + g * 512:
                                                    (h // 2) * S + (g + 1) * 512],
                                            start=True, stop=True)
                            mm.inc(ins, 1)
                    for tl in range(TPG):
                        gk = g * TPG + tl
                        if gk >= 1:
                            pg, ptl = divmod(gk - 1, TPG)
                            te.wait_ge(ev.h, ev_tile(pg, ptl))
                        for h in range(H):
                            lo, hi, bname = next(b for b in NBK if b[0] <= h < b[1])
                            bank = getattr(self, bname)
                            col = (h - lo) * 65
                            te.wait_ge(ac.h, ac_qp(g, h, 1))
                            ins = None
                            for fh in range(2):
                                ins = te.matmul(bank[:, col:col + 65],
                                                self.qp[:, (h * 2 + fh) * 512 + tl * 128:
                                                        (h * 2 + fh) * 512 + (tl + 1) * 128],
                                                self.ctxb[:, h * 130 + fh * 65:
                                                          h * 130 + (fh + 1) * 65],
                                                start=(fh == 0), stop=(fh == 1),
                                                skip_group_check=True)
                            mm.inc(ins, 1)

            @blk.scalar
            def _(sc):
                for g in range(NG):
                    for h in range(H):
                        for fh in range(2):
                            bank = [self.pA, self.pB][(h * 2 + fh) % 2]
                            sc.wait_ge(mm.h, mm_qp(g, h, fh))
                            ac.inc(sc.activation(
                                self.qp[:, (h * 2 + fh) * 512:(h * 2 + fh + 1) * 512],
                                bank[:, 0:512], AF.Exp), 1)

            @blk.vector
            def _(ve):
                for g in range(NG):
                    for tl in range(TPG):
                        t = g * TPG + tl
                        ve.wait_ge(mm.h, mm_nm(g, tl, H - 1))
                        self.ch_cat.wait(ve, t - 2)
                        for bi, (lo, hi, bname) in enumerate(NBK):
                            bank = getattr(self, bname)
                            nh = hi - lo
                            dsrc = bass.AP(bank, 64, [[512, 128], [65, nh]])
                            if bi == 2:
                                ve.tensor_copy(self.dall[:, lo:hi], dsrc)
                                ve.drain()
                                ev.inc(ve.nop(), 1)
                            else:
                                ev.inc(ve.tensor_copy(self.dall[:, lo:hi], dsrc), 1)
                        ve.tensor_scalar_add(self.dall[:], self.dall[:], 1e-6)
                        ve.drain()
                        ev.inc(ve.nop(), 1)
                        ve.reciprocal(self.dinv[:], self.dall[:])
                        ve.drain()
                        ev.inc(ve.nop(), 1)
                        for lo, hi, bname in NBK:
                            bank = getattr(self, bname)
                            nh = hi - lo
                            nsrc = bass.AP(bank, 0, [[512, 128], [65, nh], [1, 64]])
                            dsrc = bass.AP(self.dinv, lo, [[16, 128], [1, nh], [0, 64]])
                            dst = bass.AP(self.bf_st, (t % 2) * 2 * D + lo * 64,
                                          [[4 * D, 128], [64, nh], [1, 64]])
                            ev.inc(ve.tensor_tensor(dst, nsrc, dsrc, ALU.mult), 1)

            @blk.sync
            def _(sy):
                for g in range(NG):
                    for tl in range(TPG):
                        t = g * TPG + tl
                        sy.wait_ge(ev.h, ev_tile(g, tl))
                        self.ch_cat.inc(sy.dma_start(
                            out=self.d_cat.ap()[t * 128:(t + 1) * 128, :],
                            in_=self.bf_st[:, (t % 2) * 2 * D:(t % 2) * 2 * D + D]), t)
        assert mm.n == MM0 + GT * NG and ev.n == EV0 + 32 * NG
        assert ac.n == AC0 + 32 * NG

    def p4(self):
        nc = self.nc
        mm, ev, ac = self.s_mm, self.s_ev, self.s_ac
        MM0, EV0, AC0 = mm.n, ev.n, ac.n
        mm_pj = lambda t, hf: MM0 + 2 * t + hf + 1

        with nc.Block() as blk:
            @blk.gpsimd
            def _(gp):
                gp.wait_ge(mm.h, MM0)
                self.ch_grp.inc(gp.dma_start(out=self.c_grep[:],
                                             in_=self.inp["grep"].ap()), 0)
                self.ch_grp.inc(gp.dma_start(out=self.c_brep[:],
                                             in_=self.inp["brep"].ap()), 1)
        ac_t = lambda t, j: AC0 + 6 * t + j
        ev_t = lambda t, j: EV0 + 9 * t + j

        with nc.Block() as blk:
            @blk.sync
            def _(sy):
                def p4_store(t):
                    A = self.bf_st[:, (t % 2) * D:(t % 2 + 1) * D]
                    B = self.bf_st[:, (2 + t % 2) * D:(3 + t % 2) * D]
                    xbt = self.xb[:, (t % 2) * D:(t % 2 + 1) * D]
                    sy.wait_ge(ev.h, ev_t(t, 5))
                    self.ch_y.inc(sy.dma_start(
                        out=self.d_y.ap()[t * 128:(t + 1) * 128, :], in_=A), t)
                    sy.wait_ge(ev.h, ev_t(t, 8))
                    self.ch_z.inc(sy.dma_start(
                        out=self.d_z.ap()[t * 128:(t + 1) * 128, :], in_=B), t)
                    sy.wait_ge(ev.h, ev_t(t, 9))
                    self.ch_slab.inc(sy.dma_start(
                        out=self.slab_loc.ap()[t * 128:(t + 1) * 128, :], in_=xbt), t)
                for t in range(NT):
                    self.ch_dx.wait_all(sy) if t == 0 else None
                    self.ch_xb.inc(sy.dma_start(
                        out=self.xb[:, (t % 2) * D:(t % 2 + 1) * D],
                        in_=self.d_x.ap()[t * 128:(t + 1) * 128, :]), t)
                    if t >= 1:
                        p4_store(t - 1)
                        if t % 8 == 0:
                            self.ch_slab.wait(sy, t - 1)
                            self.s_slabg.inc(sy.nop(), 1)
                p4_store(NT - 1)
                self.ch_slab.wait(sy, NT - 1)
                self.s_slabg.inc(sy.nop(), 1)

            @blk.tensor
            def _(te):
                for t in range(NT):
                    g, tl = divmod(t, TPG)
                    self.ch_t4.wait(te, g)
                    if t >= 1:
                        te.wait_ge(ac.h, ac_t(t - 1, 2))
                    for hf in range(2):
                        bank = [self.pA, self.pB][hf]
                        for i in range(8):
                            te.matmul(bank[:, 0:512],
                                      self.catT[:, i * 512 + tl * 128: i * 512 + (tl + 1) * 128],
                                      self.wp_bf[:, i * D + hf * 512: i * D + (hf + 1) * 512],
                                      start=(i == 0), stop=False)
                        ins = te.matmul(bank[:, 0:512], self.c_ones[0:1, :],
                                        self.c_bpr[0:1, hf * 512:(hf + 1) * 512],
                                        start=False, stop=True)
                        mm.inc(ins, 1)

            @blk.scalar
            def _(sc):
                for t in range(NT):
                    if t % TPG == 0:
                        g = t // TPG
                        if g == 0:
                            self.ch_cat.wait_all(sc)
                        else:
                            sc.wait_ge(mm.h, mm_pj(g * TPG - 1, 1))
                        for kc in range(8):
                            self.ch_t4.inc(sc.dma_start(
                                out=self.catT[:, kc * 512:(kc + 1) * 512],
                                in_=self.d_cat.ap()[g * 512:(g + 1) * 512,
                                                    kc * 128:(kc + 1) * 128],
                                transpose=True), g)
                    A = self.bf_st[:, (t % 2) * D:(t % 2 + 1) * D]
                    sc.wait_ge(mm.h, mm_pj(t, 1))
                    self.ch_y.wait(sc, t - 2)
                    sc.activation(A[:, 0:512], self.pA[:, 0:512], AF.Copy,
                                  accum_out=self.st_sum[:, t:t + 1])
                    sc.activation(A[:, 512:1024], self.pB[:, 0:512], AF.Copy,
                                  accum_out=self.st_ssq[:, t:t + 1])
                    self.incd(sc, ac, 2)
                    sc.wait_ge(ev.h, ev_t(t, 1))
                    sc.activation(self.disc[:], A, AF.Square,
                                  bias=self.st_negmu[:, t:t + 1], scale=1.0,
                                  accum_out=self.st2_ssq[:, t:t + 1])
                    self.incd(sc, ac)
                    sc.wait_ge(ev.h, ev_t(t, 2))
                    sc.activation(self.st_inv[:, t:t + 1], self.st_inv[:, t:t + 1],
                                  AF.Sqrt)
                    self.incd(sc, ac)
                    sc.wait_ge(ev.h, ev_t(t, 6))
                    sc.activation(self.disc[:], A, AF.Square,
                                  bias=self.st2_negmu[:, t:t + 1], scale=1.0,
                                  accum_out=self.st2_ssq[:, t:t + 1])
                    self.incd(sc, ac)
                    sc.wait_ge(ev.h, ev_t(t, 7))
                    sc.activation(self.st2_inv[:, t:t + 1], self.st2_inv[:, t:t + 1],
                                  AF.Sqrt)
                    self.incd(sc, ac)

            @blk.vector
            def _(ve):
                for t in range(NT):
                    A = self.bf_st[:, (t % 2) * D:(t % 2 + 1) * D]
                    B = self.bf_st[:, (2 + t % 2) * D:(3 + t % 2) * D]
                    xbt = self.xb[:, (t % 2) * D:(t % 2 + 1) * D]
                    self.ch_z.wait(ve, t - 2)
                    ve.wait_ge(ac.h, ac_t(t, 2))
                    ve.tensor_add(self.st_sum[:, t:t + 1], self.st_sum[:, t:t + 1],
                                  self.st_ssq[:, t:t + 1])
                    ve.drain()
                    ve.tensor_scalar_mul(self.st_negmu[:, t:t + 1],
                                         self.st_sum[:, t:t + 1], -1.0 / D)
                    self.incd(ve, ev)
                    ve.wait_ge(ac.h, ac_t(t, 3))
                    ve.tensor_scalar(self.st_inv[:, t:t + 1], self.st2_ssq[:, t:t + 1],
                                     1.0 / D, EPS, ALU.mult, ALU.add)
                    ve.drain()
                    ve.reciprocal(self.st_inv[:, t:t + 1], self.st_inv[:, t:t + 1])
                    self.incd(ve, ev)
                    ve.wait_ge(ac.h, ac_t(t, 4))
                    ve.tensor_mul(self.st_b2[:, t:t + 1], self.st_negmu[:, t:t + 1],
                                  self.st_inv[:, t:t + 1])
                    ve.drain()
                    ev.inc(ve.nop(), 1)
                    ev.inc(ve.tensor_scalar(B, A, self.st_inv[:, t:t + 1],
                                            self.st_b2[:, t:t + 1], ALU.mult, ALU.add), 1)
                    self.ch_xb.wait(ve, t)
                    ev.inc(ve.tensor_add(A, xbt, B), 1)
                    ve.reduce_sum(self.st_sum[:, t:t + 1], A, axis=AX.X)
                    ve.drain()
                    ve.tensor_scalar_mul(self.st2_negmu[:, t:t + 1],
                                         self.st_sum[:, t:t + 1], -1.0 / D)
                    self.incd(ve, ev)
                    ve.wait_ge(ac.h, ac_t(t, 5))
                    ve.tensor_scalar(self.st2_inv[:, t:t + 1], self.st2_ssq[:, t:t + 1],
                                     1.0 / D, EPS, ALU.mult, ALU.add)
                    ve.drain()
                    ve.reciprocal(self.st2_inv[:, t:t + 1], self.st2_inv[:, t:t + 1])
                    self.incd(ve, ev)
                    ve.wait_ge(ac.h, ac_t(t, 6))
                    ve.tensor_mul(self.st2_b2[:, t:t + 1], self.st2_negmu[:, t:t + 1],
                                  self.st2_inv[:, t:t + 1])
                    ve.drain()
                    ev.inc(ve.tensor_scalar(B, A, self.st2_inv[:, t:t + 1],
                                            self.st2_b2[:, t:t + 1], ALU.mult, ALU.add), 1)
                    if t == 0:
                        self.ch_grp.wait_all(ve)
                    ve.tensor_mul(xbt, B, self.c_grep[:])
                    ev.inc(ve.tensor_add(xbt, xbt, self.c_brep[:]), 1)

        assert mm.n == MM0 + 2 * NT and ac.n == AC0 + 6 * NT
        assert ev.n == EV0 + 9 * NT
        self.p4_mm_end = mm.n

    def p5(self):
        nc = self.nc
        mm, ev, ac = self.s_mm, self.s_ev, self.s_ac
        cc = self.s_cc
        MM0, EV0, AC0 = mm.n, ev.n, ac.n
        mm_lg = lambda t: MM0 + 3 * t + 1
        mm_rk = lambda t: MM0 + 3 * t + 2
        EVT = 8
        ev_t = lambda t, j: EV0 + EVT * t + j
        ac_exp = lambda t: AC0 + t + 1

        with nc.Block() as blk:
            @blk.scalar
            def _(sc):
                for g in range(NG):
                    if g == 0:
                        self.ch_slab.wait_all(sc)
                    if g >= 1:
                        sc.wait_ge(mm.h, mm_lg((g - 1) * TPG + TPG - 1))
                    for kc in range(8):
                        self.ch_t5.inc(sc.dma_start(
                            out=self.ln2tT[:, kc * 512:(kc + 1) * 512],
                            in_=self.slab_loc.ap()[g * 512:(g + 1) * 512,
                                                   kc * 128:(kc + 1) * 128],
                            transpose=True), g)
                    for tl in range(TPG):
                        t = g * TPG + tl
                        sc.wait_ge(ev.h, ev_t(t, 2))
                        ac.inc(sc.activation(self.r_exp[:], self.pD[:, 0:8], AF.Exp,
                                             bias=self.r_s1[:, 0:1], scale=1.0), 1)

            @blk.tensor
            def _(te):
                for t in range(NT):
                    g, tl = divmod(t, TPG)
                    self.ch_t5.wait(te, g)
                    if t >= 1:
                        te.wait_ge(ev.h, ev_t(t - 1, 3))
                        te.wait_ge(ac.h, ac_exp(t - 1))
                    ins = None
                    for i in range(8):
                        ins = te.matmul(self.pD[:, 0:8],
                                        self.ln2tT[:, i * 512 + tl * 128: i * 512 + (tl + 1) * 128],
                                        self.c_wgt[:, i * 8:(i + 1) * 8],
                                        start=(i == 0), stop=(i == 7))
                    mm.inc(ins, 1)
                    te.wait_ge(ev.h, ev_t(t, 5))
                    if t >= 1:
                        te.wait_ge(ev.h, ev_t(t - 1, 8))
                    ins = te.matmul(self.pE[:, 0:8], self.c_triu[:], self.r_oh[:],
                                    start=True, stop=True)
                    mm.inc(ins, 1)
                    ins = te.matmul(self.pF[0:8, t:t + 1], self.r_oh[:], self.c_ones[:, 0:1],
                                    start=True, stop=True, skip_group_check=True)
                    mm.inc(ins, 1)

            @blk.vector
            def _(ve):
                for t in range(NT):
                    ve.wait_ge(mm.h, mm_lg(t))
                    ve.max(self.r_max[:], self.pD[:, 0:8])
                    ve.drain()
                    ve.max_index(self.r_idx[:], self.r_max[:], self.pD[:, 0:8])
                    self.incd(ve, ev)
                    ve.tensor_scalar_mul(self.r_s1[:], self.r_max[:, 0:1], -1.0)
                    self.incd(ve, ev)
                    ve.wait_ge(ac.h, ac_exp(t))
                    ve.reduce_sum(self.r_s1[:], self.r_exp[:], axis=AX.X)
                    ve.drain()
                    ve.reciprocal(self.r_w[:, t:t + 1], self.r_s1[:])
                    self.incd(ve, ev)
                    if t >= 1:
                        ve.wait_ge(mm.h, MM0 + 3 * (t - 1) + 3)
                    ve.tensor_single_scalar(
                        self.r_ohf[:], self.c_iota8u.ap().bitcast(F32),
                        bass.AP(self.r_idx, 0, [[8, 128], [1, 1]]).bitcast(F32),
                        ALU.is_equal)
                    ve.drain()
                    ev.inc(ve.nop(), 1)
                    ve.tensor_copy(self.r_oh[:], self.r_ohf[:])
                    self.incd(ve, ev)
                    ev.inc(ve.tensor_copy(self.r_ohall[:, t * 8:(t + 1) * 8],
                                          self.r_ohf[:]), 1)
                    ve.tensor_mul(self.r_exp[:], self.r_ohf[:], self.c_iota8f[:])
                    ve.drain()
                    ev.inc(ve.reduce_sum(self.r_e[:, t:t + 1], self.r_exp[:], axis=AX.X), 1)
                    ve.wait_ge(mm.h, mm_rk(t))
                    ev.inc(ve.tensor_copy(self.r_rank[:, t * 8:(t + 1) * 8],
                                          self.pE[:, 0:8]), 1)
        assert mm.n == MM0 + 3 * NT and ev.n == EV0 + EVT * NT and ac.n == AC0 + NT

        MM1, EV1 = mm.n, ev.n
        with nc.Block() as blk:
            @blk.vector
            def _(ve):
                ve.wait_ge(mm.h, MM1)
                ve.tensor_copy(self.r_cnt[:], self.pF[0:8, 0:NT])
                ve.drain()
                ve.reduce_sum(self.r_base[:], self.r_cnt[:], axis=AX.X)
                self.incd(ve, ev)
                ve.tensor_tensor_scan(self.r_scan[:], self.r_cnt[:],
                                      self.zero128[0:8, 0:NT], 0.0, ALU.add, ALU.add)
                ve.drain()
                ve.tensor_sub(self.r_offs[:], self.r_scan[:], self.r_cnt[:])
                self.incd(ve, ev)

            @blk.sync
            def _(sy):
                sy.wait_ge(ev.h, EV1 + 1)
                self.ch_cnt.inc(sy.dma_start(out=self.cnt_loc.ap(),
                                              in_=self.r_base[0:8, 0:1]), 0)

        with nc.Block() as blk:
            @blk.gpsimd
            def _(gpe):
                for g2 in range(4):
                    gpe.wait_ge(self.s_slabg.h, g2 + 1)
                    cc.inc(gpe.collective_compute(
                        "AllGather", ALU.bypass, replica_groups=[list(range(NC))],
                        ins=[self.slab_loc.ap()[g2 * 1024:(g2 + 1) * 1024, :].opt()],
                        outs=[self.slab_all.ap()[g2 * NC * 1024:(g2 + 1) * NC * 1024, :].opt()]), 1)
                self.ch_cnt.wait_all(gpe)
                cc.inc(gpe.collective_compute(
                    "AllGather", ALU.bypass, replica_groups=[list(range(NC))],
                    ins=[self.cnt_loc.ap().opt()], outs=[self.cnt_all.ap().opt()]), 1)
        CC_AGS, CC_AGC = cc.n - 1, cc.n

        with nc.Block() as blk:
            @blk.sync
            def _(sy):
                sy.wait_ge(cc.h, CC_AGC)
                self.ch_cna.inc(sy.dma_start(out=self.r_cntall[:], in_=self.cnt_all.ap()), 0)

        with nc.Block() as blk:
            @blk.tensor
            def _(te):
                self.ch_cna.wait_all(te)
                ins = te.matmul(self.pG[0:8, 0:1], self.r_cntall[:], self.c_pref[0:8, 0:1],
                                start=True, stop=True)
                mm.inc(ins, 1)

            @blk.vector
            def _(ve):
                ve.wait_ge(mm.h, MM1 + 1)
                ve.tensor_copy(self.r_base[:], self.pG[0:8, 0:1])
                ve.drain()
                ev.inc(ve.nop(), 1)
                ve.wait_ge(ev.h, EV1 + 2)
                ve.tensor_scalar_add(self.r_offs[:], self.r_offs[:],
                                     self.r_base[:, 0:1])
                self.incd(ve, ev)
        EV2 = ev.n

        with nc.Block() as blk:
            @blk.tensor
            def _(te):
                te.wait_ge(ev.h, EV2)
                ins = te.transpose(self.pG[0:32, 0:8], self.r_offs[:],
                                   self.c_identf[0:8, 0:8])
                mm.inc(ins, 1)

            @blk.vector
            def _(ve):
                ve.wait_ge(mm.h, MM1 + 2)
                ve.tensor_copy(self.r_offsT[:], self.pG[0:32, 0:8])
                self.incd(ve, ev)

            @blk.sync
            def _(sy):
                sy.wait_ge(ev.h, EV2 + 1)
                self.ch_off.inc(sy.dma_start(out=self.offs_d.ap(), in_=self.r_offsT[:]), 0)
                self.ch_off.wait_all(sy)
                self.ch_orow.inc(sy.dma_start(
                    out=self.r_offsrow[:],
                    in_=bass.AP(self.offs_d, 0, [[1, 1], [1, NT * E]])), 0)

        MM2 = mm.n
        with nc.Block() as blk:
            @blk.tensor
            def _(te):
                self.ch_orow.wait_all(te)
                for t in range(NT):
                    if t >= 1:
                        te.wait_ge(ev.h, EV2 + 1 + 3 * t)
                    ins = te.matmul(self.pG[:, 0:8], self.c_onesf[0:1, :],
                                    self.r_offsrow[0:1, t * 8:(t + 1) * 8],
                                    start=True, stop=True)
                    mm.inc(ins, 1)

            @blk.vector
            def _(ve):
                for t in range(NT):
                    ve.wait_ge(mm.h, MM2 + t + 1)
                    ve.tensor_add(self.r_exp[:], self.pG[:, 0:8],
                                  self.r_rank[:, t * 8:(t + 1) * 8])
                    ve.drain()
                    ev.inc(ve.nop(), 1)
                    ve.tensor_mul(self.r_exp[:], self.r_exp[:],
                                  self.r_ohall[:, t * 8:(t + 1) * 8])
                    ve.drain()
                    ev.inc(ve.nop(), 1)
                    ve.reduce_sum(self.r_rk[:, t:t + 1], self.r_exp[:], axis=AX.X)
                    self.incd(ve, ev)
        EV3 = ev.n

        with nc.Block() as blk:
            @blk.vector
            def _(ve):
                ve.wait_ge(ev.h, EV3)
                ve.tensor_single_scalar(self.r_keep[:], self.r_rk[:], float(CAP), ALU.is_lt)
                ve.drain()
                ve.tensor_mul(self.r_w[:], self.r_w[:], self.r_keep[:])
                ve.drain()
                ve.tensor_scalar(self.r_t1[:], self.r_rk[:], 1.0 / 16.0, -0.499,
                                 ALU.mult, ALU.add)
                ve.drain()
                ve.tensor_copy(self.r_rdi[:, 0:NT], self.r_t1[:])
                ve.tensor_scalar(self.r_t1[:], self.r_rk[:], 1.0 / 256.0, -0.499,
                                 ALU.mult, ALU.add)
                ve.drain()
                ve.tensor_copy(self.r_rdi[:, NT:2 * NT], self.r_t1[:])
                self.incd(ve, ev)
        EV4 = ev.n

        with nc.Block() as blk:
            @blk.gpsimd
            def _(gpe):
                gpe.wait_ge(ev.h, EV4)
                self.ch_rdf.inc(gpe.dma_start(out=self.r_rdf[:], in_=self.r_rdi[:]), 0)

        with nc.Block() as blk:
            @blk.vector
            def _(ve):
                self.ch_rdf.wait_all(ve)
                ve.scalar_tensor_tensor(self.r_rm[:], self.r_rdf[:, 0:NT], -16.0,
                                        self.r_rk[:], ALU.mult, ALU.add)
                ve.drain()
                ve.tensor_scalar_mul(self.r_a1[:], self.r_e[:], float(CAP))
                ve.drain()
                ve.scalar_tensor_tensor(self.r_a1[:], self.r_rm[:], 128.0, self.r_a1[:],
                                        ALU.mult, ALU.add)
                ve.drain()
                ve.tensor_add(self.r_a1[:], self.r_a1[:], self.r_rdf[:, 0:NT])
                ve.drain()
                ve.tensor_mul(self.r_a1[:], self.r_a1[:], self.r_keep[:])
                ve.drain()
                ve.tensor_scalar(self.r_t1[:], self.r_keep[:], -DROP, DROP,
                                 ALU.mult, ALU.add)
                ve.drain()
                ve.tensor_add(self.r_a1[:], self.r_a1[:], self.r_t1[:])
                ve.drain()
                ve.tensor_copy(self.r_a1u[:], self.r_a1[:])
                self.incd(ve, ev)
                ve.tensor_scalar_mul(self.r_a2[:], self.r_e[:], 256.0)
                ve.drain()
                ve.tensor_add(self.r_a2[:], self.r_a2[:], self.r_rk[:])
                ve.drain()
                ve.scalar_tensor_tensor(self.r_a2[:], self.r_rdf[:, NT:2 * NT],
                                        1792.0, self.r_a2[:], ALU.mult, ALU.add)
                ve.drain()
                ve.tensor_mul(self.r_a2[:], self.r_a2[:], self.r_keep[:])
                self.incd(ve, ev)
        EV5 = ev.n

        MM3 = mm.n
        with nc.Block() as blk:
            @blk.tensor
            def _(te):
                te.wait_ge(ev.h, EV5)
                for r in range(8):
                    if r >= 1:
                        te.wait_ge(ev.h, EV5 + r)
                    ins = te.matmul(self.pG[0:16, 0:32], self.c_sel8[:, r * 16:(r + 1) * 16],
                                    self.r_a2[:], start=True, stop=True)
                    mm.inc(ins, 1)

            @blk.vector
            def _(ve):
                for r in range(8):
                    ve.wait_ge(mm.h, MM3 + r + 1)
                    dst = bass.AP(self.gb_f, r, [[256, 16], [8, 32]])
                    if r == 7:
                        ve.tensor_copy(dst, self.pG[0:16, 0:32])
                        self.incd(ve, ev)
                    else:
                        ev.inc(ve.tensor_copy(dst, self.pG[0:16, 0:32]), 1)
                ve.wait_ge(ev.h, EV5 + 8)
                ve.memset(self.gb_i[:], 0)
                ve.drain()
                ve.tensor_copy(self.gb_i[0:16, :], self.gb_f[:])
                self.incd(ve, ev)
        EV6 = ev.n
        self.p5_ev_gb = EV6

        with nc.Block() as blk:
            @blk.sync
            def _(sy):
                sy.wait_ge(ev.h, self.p0_zero)
                self.ch_tz.inc(sy.dma_start(
                    out=bass.AP(self.tab_loc, 0, [[1, 1], [1, E * CAP]]),
                    in_=self.zero128[:]), 0)
                sy.wait_ge(ev.h, EV6)
                self.ch_gbr.inc(sy.dma_start(out=self.gb_i[16:32, :],
                                             in_=self.gb_i[0:16, :]), 0)

        with nc.Block() as blk:
            @blk.gpsimd
            def _(gpe):
                gpe.wait_ge(mm.h, self.p3_mm_end)
                for kc in range(8):
                    self.ch_wi.inc(gpe.dma_start(
                        out=self.wi_bf[:, kc * IFF:(kc + 1) * IFF],
                        in_=self.inp["wi_c"].ap()[kc * 128:(kc + 1) * 128, :]), kc)
                gpe.wait_ge(mm.h, self.p4_mm_end)
                for ic in range(32):
                    self.ch_wo.inc(gpe.dma_start(
                        out=self.wo_bf[:, ic * D:(ic + 1) * D],
                        in_=self.inp["wo_c"].ap()[ic * 128:(ic + 1) * 128, :]), ic)
                gpe.wait_ge(ev.h, EV5)
                self.ch_tz.wait_all(gpe)
                for t in range(NT):
                    self.ch_sc.inc(gpe.indirect_dma_start(
                        self.tab_loc.ap(),
                        bass.IndirectOffsetOnAxis(self.r_a1u[:, t:t + 1], 0),
                        self.c_ids[:, t:t + 1], None,
                        bounds_check=E * CAP - 1, oob_is_err=False), t)
                self.ch_sc.wait_all(gpe)
                cc.inc(gpe.collective_compute(
                    "AllReduce", ALU.add, replica_groups=[list(range(NC))],
                    ins=[self.tab_loc.ap().opt()], outs=[self.tab_all.ap().opt()]), 1)
        self.p5_cc_tab = cc.n
        self.p5_cc_slab = CC_AGS

    def p6(self):
        nc = self.nc
        mm, ev, ac, cc = self.s_mm, self.s_ev, self.s_ac, self.s_cc
        MM0, EV0, AC0 = mm.n, ev.n, ac.n

        with nc.Block() as blk:
            @blk.sync
            def _(sy):
                sy.wait_ge(cc.h, self.p5_cc_tab)
                if "tabread" in KSKIP:
                    self.ch_tr.inc(sy.dma_start(
                        out=self.di_f[:],
                        in_=bass.AP(self.tab_all, 0, [[128, 16], [1, 128]])), 0)
                else:
                    with sy.register("tabreg") as reg:
                        sy.reg_load(reg, self.c_meta[0:1, 0:1])
                        tsrc = bass.AP(self.tab_all, reg, [[128, 16], [1, 128]])
                        self.ch_tr.inc(sy.dma_start(out=self.di_f[:], in_=tsrc), 0)

        with nc.Block() as blk:
            @blk.vector
            def _(ve):
                self.ch_tr.wait_all(ve)
                ve.memset(self.di_i[:], 0)
                ve.drain()
                ve.tensor_copy(self.di_i[0:16, :], self.di_f[:])
                self.incd(ve, ev)
        EV1 = ev.n

        with nc.Block() as blk:
            @blk.sync
            def _(sy):
                sy.wait_ge(ev.h, EV1)
                self.ch_direp.inc(sy.dma_start(out=self.di_i[16:32, :],
                                               in_=self.di_i[0:16, :]), 0)

        with nc.Block() as blk:
            @blk.gpsimd
            def _(gpe):
                gpe.load_library(library_config.mlp)
                self.ch_direp.wait_all(gpe)
                gpe.wait_ge(cc.h, self.p5_cc_slab)
                self.g_nreg = gpe.alloc_register("gnum")
                gpe.reg_mov(self.g_nreg, 256)
                for sg in range(8):
                    self.ch_dg.inc(gpe.dma_gather(
                        self.dispT[:, sg], self.slab_all.ap(),
                        self.di_i[:, sg * 16:(sg + 1) * 16],
                        256, self.g_nreg, D, transpose=True), sg)

        NSG = 8
        MM0, EV0, AC0 = mm.n, ev.n, ac.n
        mm_wi = lambda sg, ic: MM0 + 36 * sg + ic + 1
        mm_wo = lambda sg, stt, hf: MM0 + 36 * sg + 32 + stt * 2 + hf + 1
        ac_ge = lambda sg, ic: AC0 + 32 * sg + ic + 1
        ev_wo = lambda sg, stt, hf: EV0 + 4 * sg + stt * 2 + hf + 1

        with nc.Block() as blk:
            @blk.tensor
            def _(te):
                self.ch_dg.wait_all(te)
                self.ch_wi.wait_all(te)
                self.ch_wo.wait_all(te)
                for sg in range(NSG):
                    for ic in range(32):
                        bank = [self.pA, self.pB][ic % 2]
                        k = sg * 32 + ic
                        if k >= 2:
                            pk = k - 2
                            te.wait_ge(ac.h, ac_ge(pk // 32, pk % 32))
                        ins = None
                        for kc in range(8):
                            ins = te.matmul(bank[:, 0:256],
                                            self.wi_bf[:, kc * IFF + ic * 128: kc * IFF + (ic + 1) * 128],
                                            self.dispT[:, sg, kc],
                                            start=(kc == 0), stop=(kc == 7))
                        mm.inc(ins, 1)
                    for stt in range(2):
                        for hf in range(2):
                            bank = [self.pC, self.pD][(stt * 2 + hf) % 2]
                            k = sg * 4 + stt * 2 + hf
                            if k >= 2:
                                pk = k - 2
                                te.wait_ge(ev.h, ev_wo(pk // 4, (pk % 4) // 2, pk % 2))
                            te.wait_ge(ac.h, ac_ge(sg, 31))
                            ins = None
                            for ic in range(32):
                                ins = te.matmul(bank[:, 0:512],
                                                self.hT[:, ic * 256 + stt * 128: ic * 256 + (stt + 1) * 128],
                                                self.wo_bf[:, ic * D + hf * 512: ic * D + (hf + 1) * 512],
                                                start=(ic == 0), stop=(ic == 31))
                            mm.inc(ins, 1)

            @blk.scalar
            def _(sc):
                for sg in range(NSG):
                    for ic in range(32):
                        bank = [self.pA, self.pB][ic % 2]
                        sc.wait_ge(mm.h, mm_wi(sg, ic))
                        if sg >= 1:
                            sc.wait_ge(mm.h, mm_wo(sg - 1, 1, 1))
                        ac.inc(sc.activation(self.hT[:, ic * 256:(ic + 1) * 256],
                                             bank[:, 0:256], AF.Gelu), 1)

            @blk.vector
            def _(ve):
                for sg in range(NSG):
                    for stt in range(2):
                        for hf in range(2):
                            bank = [self.pC, self.pD][(stt * 2 + hf) % 2]
                            ve.wait_ge(mm.h, mm_wo(sg, stt, hf))
                            self.ch_eo.wait(ve, sg * 2 + stt - 2)
                            ev.inc(ve.tensor_copy(
                                self.bf_st[:, (stt % 2) * D + hf * 512:
                                           (stt % 2) * D + (hf + 1) * 512],
                                bank[:, 0:512]), 1)

            @blk.sync
            def _(sy):
                for sg in range(NSG):
                    for stt in range(2):
                        sy.wait_ge(ev.h, ev_wo(sg, stt, 1))
                        row = sg * 256 + stt * 128
                        self.ch_eo.inc(sy.dma_start(
                            out=self.eo_loc.ap()[row:row + 128, :],
                            in_=self.bf_st[:, (stt % 2) * D:(stt % 2 + 1) * D]), sg * 2 + stt)
                    self.ch_eo.wait(sy, sg * 2 + 1)
                    self.s_eog.inc(sy.nop(), 1)
        assert mm.n == MM0 + 36 * NSG and ac.n == AC0 + 32 * NSG
        assert ev.n == EV0 + 4 * NSG

        with nc.Block() as blk:
            @blk.gpsimd
            def _(gpe):
                for sg in range(NSG):
                    gpe.wait_ge(self.s_eog.h, sg + 1)
                    cc.inc(gpe.collective_compute(
                        "AllGather", ALU.bypass, replica_groups=[list(range(NC))],
                        ins=[self.eo_loc.ap()[sg * 256:(sg + 1) * 256, :].opt()],
                        outs=[self.eo_all.ap()[sg * E * 256:(sg + 1) * E * 256, :].opt()]), 1)
        self.p6_cc_eo = cc.n

    def p7(self):
        nc = self.nc
        mm, ev, ac, cc = self.s_mm, self.s_ev, self.s_ac, self.s_cc
        EV0, AC0 = ev.n, ac.n

        with nc.Block() as blk:
            @blk.gpsimd
            def _(gpe):
                gpe.wait_ge(cc.h, self.p6_cc_eo)
                gpe.wait_ge(ev.h, self.p5_ev_gb)
                self.ch_gbr.wait_all(gpe)
                for c in range(NT // 2):
                    self.ch_gb2.inc(gpe.dma_gather(
                        self.comb[:, c], self.eo_all.ap(),
                        self.gb_i[:, c * 16:(c + 1) * 16],
                        256, self.g_nreg, D, transpose=False), c)

        ev_t = lambda t, j: EV0 + 5 * t + j
        ac_t = lambda t, j: AC0 + 2 * t + j

        with nc.Block() as blk:
            @blk.sync
            def _(sy):
                def p7_store(t):
                    sy.wait_ge(ev.h, ev_t(t, 5))
                    self.ch_out.inc(sy.dma_start(
                        out=self.out.ap()[t * 128:(t + 1) * 128, :],
                        in_=self.ost[:, (t % 2) * D:(t % 2 + 1) * D]), t)
                for t in range(NT):
                    if t >= 2:
                        sy.wait_ge(ev.h, ev_t(t - 2, 5))
                    self.ch_zy.inc(sy.dma_start(out=self.zst[:, (t % 2) * D:(t % 2 + 1) * D],
                                        in_=self.d_z.ap()[t * 128:(t + 1) * 128, :]), t)
                    self.ch_zy.inc(sy.dma_start(out=self.yst[:, (t % 2) * D:(t % 2 + 1) * D],
                                        in_=self.d_y.ap()[t * 128:(t + 1) * 128, :]), t)
                    if t >= 1:
                        p7_store(t - 1)
                p7_store(NT - 1)

            @blk.vector
            def _(ve):
                self.ch_gb2.wait_all(ve)
                for t in range(NT):
                    Ft = self.fst[:, (t % 2) * D:(t % 2 + 1) * D]
                    Ot = self.ost[:, (t % 2) * D:(t % 2 + 1) * D]
                    self.ch_zy.wait(ve, t)
                    self.ch_out.wait(ve, t - 2)
                    ve.scalar_tensor_tensor(Ft, self.comb.ap()[:, t // 2, t % 2, :],
                                            self.r_w[:, t:t + 1],
                                            self.zst[:, (t % 2) * D:(t % 2 + 1) * D],
                                            ALU.mult, ALU.add)
                    ve.reduce_sum(self.st_sum[:, t:t + 1], Ft, axis=AX.X)
                    ve.drain()
                    ve.tensor_scalar_mul(self.st_negmu[:, t:t + 1],
                                         self.st_sum[:, t:t + 1], -1.0 / D)
                    self.incd(ve, ev)
                    ve.wait_ge(ac.h, ac_t(t, 1))
                    ve.tensor_scalar(self.st_inv[:, t:t + 1], self.st_ssq[:, t:t + 1],
                                     1.0 / D, EPS, ALU.mult, ALU.add)
                    ve.drain()
                    ve.reciprocal(self.st_inv[:, t:t + 1], self.st_inv[:, t:t + 1])
                    self.incd(ve, ev)
                    ve.wait_ge(ac.h, ac_t(t, 2))
                    ve.tensor_mul(self.st_b2[:, t:t + 1], self.st_negmu[:, t:t + 1],
                                  self.st_inv[:, t:t + 1])
                    ve.drain()
                    ev.inc(ve.nop(), 1)
                    ev.inc(ve.tensor_scalar(Ot, Ft, self.st_inv[:, t:t + 1],
                                            self.st_b2[:, t:t + 1], ALU.mult, ALU.add), 1)
                    ev.inc(ve.tensor_add(Ot, Ot,
                                         self.yst[:, (t % 2) * D:(t % 2 + 1) * D]), 1)

            @blk.scalar
            def _(sc):
                for t in range(NT):
                    Ft = self.fst[:, (t % 2) * D:(t % 2 + 1) * D]
                    sc.wait_ge(ev.h, ev_t(t, 1))
                    sc.activation(self.disc[:], Ft, AF.Square,
                                  bias=self.st_negmu[:, t:t + 1], scale=1.0,
                                  accum_out=self.st_ssq[:, t:t + 1])
                    self.incd(sc, ac)
                    sc.wait_ge(ev.h, ev_t(t, 2))
                    sc.activation(self.st_inv[:, t:t + 1], self.st_inv[:, t:t + 1],
                                  AF.Sqrt)
                    self.incd(sc, ac)

        assert ev.n == EV0 + 5 * NT and ac.n == AC0 + 2 * NT

    def build(self):
        self.p0()
        if UPTO >= 1:
            self.p1()
        if UPTO >= 2:
            self.p2()
            self.p25()
        if UPTO >= 3:
            self.p3()
        self.p3_mm_end = self.s_mm.n
        if UPTO >= 4:
            self.p4()
        else:
            self.p4_mm_end = self.s_mm.n
        if UPTO >= 5:
            self.p5()
        if UPTO >= 6:
            self.p6()
        if UPTO >= 7:
            self.p7()
        self.finish()
        lower_extended_insts(self.nc)
        return self.nc

    def finish(self):
        nc = self.nc
        tapmap = {"ln1": self.d_ln1, "dx": self.d_x, "cat": self.d_cat,
                  "y": self.d_y, "z": self.d_z, "slab": self.slab_loc,
                  "tab": self.tab_all, "eo": self.eo_loc, "offs": self.offs_d}
        taps = [(tapmap[n], "t_" + n) for n in sorted(TAPS) if n in tapmap]
        sbtaps = []
        for n in sorted(TAPS):
            if n.startswith("sb:"):
                _, attr, rows, cols = n.split(":")
                t = getattr(self, attr)
                sbtaps.append((t, "t_" + attr, int(rows), int(cols)))
        tap_ch = Chan(nc, "ctap")
        with nc.Block() as blk:
            @blk.sync
            def _(sy):
                for s in [self.s_mm, self.s_ev, self.s_ac, self.s_cc, self.s_gp]:
                    if s.n:
                        sy.wait_ge(s.h, s.n)
                for ch in [self.ch_const, self.ch_wc, self.ch_x, self.ch_dx,
                           self.ch_l1, self.ch_t2, self.ch_cat, self.ch_t4,
                           self.ch_xb, self.ch_y, self.ch_z, self.ch_slab,
                           self.ch_t5, self.ch_cnt, self.ch_cna, self.ch_off,
                           self.ch_orow, self.ch_rdf, self.ch_tz, self.ch_gbr,
                           self.ch_wi, self.ch_wo, self.ch_sc, self.ch_tr,
                           self.ch_direp, self.ch_dg, self.ch_eo, self.ch_gb2,
                           self.ch_zy, self.ch_out, self.ch_grp]:
                    ch.wait_all(sy)
                i = 0
                for srct, name in taps:
                    t = nc.dram_tensor(name, list(srct.shape), srct.ap().dtype,
                                       kind="ExternalOutput")
                    tap_ch.inc(sy.dma_start(out=t.ap(), in_=srct.ap()), i)
                    i += 1
                for srct, name, rows, cols in sbtaps:
                    t = nc.dram_tensor(name, [rows, cols], srct.ap().dtype,
                                       kind="ExternalOutput")
                    tap_ch.inc(sy.dma_start(out=t.ap(), in_=srct[0:rows, 0:cols]), i)
                    i += 1
                if UPTO < 7:
                    tap_ch.inc(sy.dma_start(out=self.out.ap()[0:128, 0:128],
                                            in_=self.c_onesf[:]), i)
                tap_ch.wait_all(sy)


def host_inputs(inputs):
    x = np.ascontiguousarray(inputs["L0_attn_x"], dtype=np.float32)
    Wq = np.ascontiguousarray(inputs["L0_attn_Wq"], dtype=np.float32)
    Wk = np.ascontiguousarray(inputs["L0_attn_Wk"], dtype=np.float32)
    Wv = np.ascontiguousarray(inputs["L0_attn_Wv"], dtype=np.float32)
    Wp = np.ascontiguousarray(inputs["L0_attn_Wproj"], dtype=np.float32)
    bq = np.asarray(inputs["L0_attn_bq"], dtype=np.float32)
    bk = np.asarray(inputs["L0_attn_bk"], dtype=np.float32)
    bv = np.asarray(inputs["L0_attn_bv"], dtype=np.float32)
    bp = np.asarray(inputs["L0_attn_bproj"], dtype=np.float32)
    wqf = np.asarray(inputs["L0_attn_wq_feat"], dtype=np.float32)
    wkf = np.asarray(inputs["L0_attn_wk_feat"], dtype=np.float32)
    gw = np.asarray(inputs["L0_ffn_gate_weight"], dtype=np.float32)
    lng = np.asarray(inputs["L0_ffn_ln_gamma"], dtype=np.float32)
    lnb = np.asarray(inputs["L0_ffn_ln_beta"], dtype=np.float32)
    wi = np.ascontiguousarray(inputs["L0_ffn_wi"], dtype=np.float32)
    wo = np.ascontiguousarray(inputs["L0_ffn_wo"], dtype=np.float32)

    wqf4 = np.zeros((128, 512), np.float32)
    wkf2 = np.zeros((128, 512), np.float32)
    for par in range(2):
        for fh in range(2):
            v = par * 2 + fh
            wqf4[par * 64:(par + 1) * 64, v * 128:(v + 1) * 128] = \
                wqf[:, fh * 128:(fh + 1) * 128]
        wkf2[par * 64:(par + 1) * 64, par * 256:(par + 1) * 256] = wkf
    wgt = gw.T.reshape(8, 128, 8).transpose(1, 0, 2).reshape(128, 64).copy()
    sel8 = np.zeros((128, 128), np.float32)
    for r in range(8):
        for q in range(16):
            sel8[16 * r + q, r * 16 + q] = 1.0
    shared = dict(
        wq=Wq, wk=Wk, wv=Wv, wp=Wp,
        bq=np.ascontiguousarray(bq.reshape(8, 128).T),
        bk=np.ascontiguousarray(bk.reshape(8, 128).T),
        bvr=bv.reshape(1, D).copy(), bpr=bp.reshape(1, D).copy(),
        wqf4=wqf4, wkf2=wkf2, wgt=wgt,
        grep=np.ascontiguousarray(np.broadcast_to(lng, (128, D))),
        brep=np.ascontiguousarray(np.broadcast_to(lnb, (128, D))),
        triu=np.triu(np.ones((128, 128), np.float32), 1),
        onesc=np.ones((128, 128), np.float32),
        identc=np.eye(128, dtype=np.float32),
        sel8=sel8,
        iota8u=np.ascontiguousarray(np.broadcast_to(np.arange(8, dtype=np.uint32),
                                                    (128, 8))),
    )
    in_maps = []
    for c in range(NC):
        j = np.arange(NT)[None, :] * 128 + np.arange(128)[:, None]
        ids = ((j // 1024) * (NC * 1024) + c * 1024 + (j % 1024)).astype(np.float32)
        pref = np.zeros((128, 1), np.float32)
        pref[:c, 0] = 1.0
        meta = np.array([[c * CAP, c]], dtype=np.uint32)
        m = dict(shared)
        m.update(x=x[c], wi_c=wi[c], wo_c=wo[c], ids=ids, pref=pref, meta_u=meta)
        in_maps.append(m)
    return in_maps


def build_and_run(inputs):
    in_maps = host_inputs(inputs)
    b = Bld()
    nc = b.build()
    res = run_bass_kernel_spmd(nc, in_maps, core_ids=list(range(NC)))
    return res


def kernel(**inputs):
    res = build_and_run(inputs)
    out = np.stack([np.asarray(r["out"]) for r in res.results], axis=0)
    return out.astype(np.float32)


if __name__ == "__main__":
    import reference as refmod
    ins = refmod.setup_inputs(0)
    out = kernel(**{k: np.asarray(v) for k, v in ins.items()})
    print(out.shape, out.dtype)


# version 03
# speedup vs baseline: 1.1444x; 1.1074x over previous
import os
import numpy as np
import ml_dtypes
import concourse.bass as bass
import concourse.mybir as mybir
import concourse.library_config as library_config
from concourse.library_overlay import lower_extended_insts
from concourse.bass_utils import run_bass_kernel_spmd

F32 = mybir.dt.float32
BF16 = mybir.dt.bfloat16
I16 = mybir.dt.int16
U32 = mybir.dt.uint32
AF = mybir.ActivationFunctionType
ALU = mybir.AluOpType
AX = mybir.AxisListType

S, D, H, DH, FF, E, CAP, IFF = 4096, 1024, 16, 64, 256, 8, 2048, 4096
NT, NG, TPG, NC = 32, 8, 4, 8
EPS = 1e-5
DROP = 60000.0

OFF_A = 34816
OFF_B = 100352
OFF_C = 165888
OFF_D = 198656
OFF_OV = 206848

UPTO = int(os.environ.get("KUPTO", "7"))
KSKIP = set(s for s in os.environ.get("KSKIP", "").split(",") if s)
TAPS = set(t for t in os.environ.get("KTAPS", "").split(",") if t)


class Sem:
    def __init__(self, nc, name):
        self.h = nc.alloc_semaphore(name)
        self.n = 0

    def inc(self, ins, k=1):
        ins.then_inc(self.h, k)
        self.n += k
        return ins


class Chan:

    def __init__(self, nc, name, k=1, per_slot=1):
        self.sems = [nc.alloc_semaphore(f"{name}{i}") for i in range(k)]
        self.k = k
        self.per_slot = per_slot
        self.tot = [0] * k
        self.cnt = {}

    def inc(self, ins, slot):
        j = slot % self.k
        self.tot[j] += 16
        ins.then_inc(self.sems[j], 16)
        self.cnt[slot] = self.cnt.get(slot, 0) + 1
        assert self.cnt[slot] <= self.per_slot

    def wait(self, eng, slot):
        if slot < 0:
            return
        v = 16 * self.per_slot * (slot // self.k + 1)
        eng.wait_ge(self.sems[slot % self.k], v)

    def wait_all(self, eng):
        for j in range(self.k):
            if self.tot[j]:
                eng.wait_ge(self.sems[j], self.tot[j])


class Bld:
    def __init__(self):
        nc = bass.Bass("TRN2", target_bir_lowering=False, detect_race_conditions=False)
        self.nc = nc
        dram = lambda n, s, d, k="Internal": nc.dram_tensor(n, s, d, kind=k)
        drsh = lambda n, s, d: nc.dram_tensor(n, s, d, kind="Internal",
                                              addr_space="Shared")
        self.inp = {}
        for name, shape in [
                ("x", [S, D]), ("wq", [D, D]), ("wk", [D, D]), ("wv", [D, D]),
                ("wp", [D, D]), ("bq", [128, 8]), ("bk", [128, 8]),
                ("bvr", [1, D]), ("bpr", [1, D]), ("wqf4", [128, 512]),
                ("wkf2", [128, 512]), ("wgt", [128, 64]), ("grep", [128, D]),
                ("brep", [128, D]), ("wi_c", [D, IFF]), ("wo_c", [IFF, D]),
                ("triu", [128, 128]), ("onesc", [128, 128]),
                ("identc", [128, 128]), ("sel8", [128, 128]),
                ("ids", [128, NT]), ("pref", [128, 1])]:
            self.inp[name] = dram(name, shape, F32, "ExternalInput")
        self.inp["iota8u"] = dram("iota8u", [128, 8], U32, "ExternalInput")
        self.inp["meta_u"] = dram("meta_u", [1, 2], U32, "ExternalInput")
        self.out = dram("out", [S, D], F32, "ExternalOutput")

        self.d_x = dram("d_x", [S, D], BF16)
        self.d_ln1 = dram("d_ln1", [S, D], BF16)
        self.d_cat = dram("d_cat", [S, D], BF16)
        self.d_y = dram("d_y", [S, D], BF16)
        self.d_z = dram("d_z", [S, D], BF16)
        self.slab_loc = dram("slab_loc", [S, D], BF16)
        self.slab_all = drsh("slab_all", [NC * S, D], BF16)
        self.cnt_loc = dram("cnt_loc", [1, E], F32)
        self.cnt_all = drsh("cnt_all", [NC, E], F32)
        self.tab_loc = dram("tab_loc", [E * CAP, 1], F32)
        self.tab_all = drsh("tab_all", [E * CAP, 1], F32)
        self.eo_loc = dram("eo_loc", [CAP, D], BF16)
        self.eo_all = drsh("eo_all", [E * CAP, D], BF16)
        self.offs_d = dram("offs_d", [NT, E], F32)

        sb = nc.alloc_sbuf_tensor
        self.c_wqf4 = sb("c_wqf4", [128, 512], BF16)
        self.c_wkf2 = sb("c_wkf2", [128, 512], BF16)
        self.c_bq = sb("c_bq", [128, 8], F32)
        self.c_bk = sb("c_bk", [128, 8], F32)
        self.c_bvr = sb("c_bvr", [1, D], BF16)
        self.c_bpr = sb("c_bpr", [1, D], BF16)
        self.c_wgt = sb("c_wgt", [128, 64], BF16)
        self.c_triu = sb("c_triu", [128, 128], BF16)
        self.c_ones = sb("c_ones", [128, 128], BF16)
        self.c_onesf = sb("c_onesf", [128, 128], F32)
        self.c_identb = sb("c_identb", [128, 128], BF16)
        self.c_identf = sb("c_identf", [128, 128], F32)
        self.c_sel8 = sb("c_sel8", [128, 128], F32)
        self.c_iota8u = sb("c_iota8u", [128, 8], U32)
        self.c_iota8f = sb("c_iota8f", [128, 8], F32)
        self.c_ids = sb("c_ids", [128, NT], F32)
        self.c_pref = sb("c_pref", [128, 1], F32)
        self.c_meta = sb("c_meta", [1, 2], U32)
        self.zero128 = sb("zero128", [128, 128], F32)
        self.st_sum = sb("st_sum", [128, NT], F32)
        self.st_ssq = sb("st_ssq", [128, NT], F32)
        self.st_negmu = sb("st_negmu", [128, NT], F32)
        self.st_inv = sb("st_inv", [128, NT], F32)
        self.st_b2 = sb("st_b2", [128, NT], F32)
        self.st2_ssq = sb("st2_ssq", [128, NT], F32)
        self.st2_negmu = sb("st2_negmu", [128, NT], F32)
        self.st2_inv = sb("st2_inv", [128, NT], F32)
        self.st2_b2 = sb("st2_b2", [128, NT], F32)
        self.r_rank = sb("r_rank", [128, NT * E], F32)
        self.r_ohall = sb("r_ohall", [128, NT * E], BF16)
        self.r_w = sb("r_w", [128, NT], F32)
        self.r_keep = sb("r_keep", [128, NT], F32)
        self.r_e = sb("r_e", [128, NT], F32)
        self.r_rk = sb("r_rk", [128, NT], F32)
        self.r_rm = sb("r_rm", [128, NT], F32)
        self.r_rdi = sb("r_rdi", [128, 2 * NT], I16)
        self.r_rdf = sb("r_rdf", [128, 2 * NT], F32)
        self.r_a1 = sb("r_a1", [128, NT], F32)
        self.r_a1u = sb("r_a1u", [128, NT], U32)
        self.r_a2 = sb("r_a2", [128, NT], F32)
        self.r_t1 = sb("r_t1", [128, NT], F32)
        self.r_max = sb("r_max", [128, 8], F32)
        self.r_idx = sb("r_idx", [128, 8], U32)
        self.r_oh = sb("r_oh", [128, 8], BF16)
        self.r_ohf = sb("r_ohf", [128, 8], F32)
        self.r_exp = sb("r_exp", [128, 8], F32)
        self.r_s1 = sb("r_s1", [128, 1], F32)
        self.r_cnt = sb("r_cnt", [8, NT], F32)
        self.r_cntall = sb("r_cntall", [8, 8], F32)
        self.r_scan = sb("r_scan", [8, NT], F32)
        self.r_offs = sb("r_offs", [8, NT], F32)
        self.r_base = sb("r_base", [8, 1], F32)
        self.r_offsT = sb("r_offsT", [NT, 8], F32)
        self.r_offsrow = sb("r_offsrow", [1, NT * E], F32)
        self.gb_f = sb("gb_f", [16, 256], F32)
        self.gb_i = sb("gb_i", [128, 256], I16)
        self.di_f = sb("di_f", [16, 128], F32)
        self.di_i = sb("di_i", [128, 128], I16)
        self.dall = sb("dall", [128, 16], F32)
        self.dinv = sb("dinv", [128, 16], F32)

        at = nc.alloc_sbuf_tensor_at
        self.qT = at("qT", [128, 8 * S], BF16, offset=OFF_A)
        self.wi_bf = at("wi_bf", [128, 8 * IFF], BF16, offset=OFF_A)
        self.comb = at("comb", [128, NT // 2, 2, D], BF16, offset=OFF_A)
        self.wq_bf = at("wq_bf", [128, 8 * D], BF16, offset=OFF_B)
        self.wk_bf = at("wk_bf", [128, 8 * D], BF16, offset=OFF_B + 16384)
        self.wv_bf = at("wv_bf", [128, 8 * D], BF16, offset=OFF_B + 32768)
        self.wp_bf = at("wp_bf", [128, 8 * D], BF16, offset=OFF_B + 49152)
        self.wo_bf = at("wo_bf", [128, 32 * D], BF16, offset=OFF_B)
        self.lnxT = at("lnxT", [128, 8 * 512], BF16, offset=OFF_C)
        self.kT = at("kT", [128, 8 * 512], BF16, offset=OFF_C + 8192)
        self.v_g = at("v_g", [128, TPG * D], BF16, offset=OFF_C + 16384)
        self.kp = at("kp", [128, 2 * FF], BF16, offset=OFF_C + 24576)
        self.qp = at("qp", [128, 32 * 512], BF16, offset=OFF_C)
        self.catT = at("catT", [128, 8 * 512], BF16, offset=OFF_C)
        self.ln2tT = at("ln2tT", [128, 8 * 512], BF16, offset=OFF_C)
        self.dispT = at("dispT", [128, 8, 8, 256], BF16, offset=OFF_C)
        self.bf_st = at("bf_st", [128, 4 * D], BF16, offset=OFF_D)
        self.disc = at("disc", [128, D], BF16, offset=OFF_C + 30720)
        self.xs = at("xs", [128, 2 * D], F32, offset=OFF_OV)
        self.ctx = at("ctx", [128, H * FF], F32, offset=OFF_OV)
        self.ctxb = at("ctxb", [128, H * 130], BF16, offset=OFF_OV)
        self.c_grep = at("c_grep", [128, D], BF16, offset=OFF_OV)
        self.c_brep = at("c_brep", [128, D], BF16, offset=OFF_OV + 2048)
        self.xb = at("xb", [128, 2 * D], BF16, offset=OFF_OV + 4096)
        self.hT = at("hT", [128, 32 * 256], BF16, offset=OFF_OV)
        self.zst = at("zst", [128, 2 * D], BF16, offset=OFF_OV)
        self.yst = at("yst", [128, 2 * D], BF16, offset=OFF_OV + 4096)
        self.fst = at("fst", [128, 2 * D], BF16, offset=OFF_OV + 8192)
        self.ost = at("ost", [128, 2 * D], F32, offset=OFF_OV + 12288)

        ps = nc.alloc_psum_tensor
        self.pA = ps("pA", [128, 512], F32)
        self.pB = ps("pB", [128, 512], F32)
        self.pC = ps("pC", [128, 512], F32)
        self.pD = ps("pD", [128, 512], F32)
        self.pE = ps("pE", [128, 512], F32)
        self.pF = ps("pF", [128, 512], F32)
        self.pG = ps("pG", [128, 512], F32)
        self.pT = ps("pT", [128, 1024], BF16)

        self.s_ld = Sem(nc, "s_ld")
        self.s_gd = Sem(nc, "s_gd")
        self.s_td = Sem(nc, "s_td")
        self.s_st = Sem(nc, "s_st")
        self.s_mm = Sem(nc, "s_mm")
        self.s_ev = Sem(nc, "s_ev")
        self.s_ac = Sem(nc, "s_ac")
        self.s_cc = Sem(nc, "s_cc")
        self.s_gp = Sem(nc, "s_gp")
        self.s_slabg = Sem(nc, "s_slabg")
        self.s_eog = Sem(nc, "s_eog")
        C = lambda n, k=1: Chan(nc, n, k)
        self.ch_const = C("ccst")
        self.ch_wc = C("cwc")
        self.ch_x = C("cx", 2)
        self.ch_dx = C("cdx", 2)
        self.ch_l1 = C("cl1", 2)
        self.ch_t2 = Chan(nc, "ct2", 2, per_slot=8)
        self.ch_cat = C("ccat", 2)
        self.ch_t4 = Chan(nc, "ct4", 2, per_slot=8)
        self.ch_xb = C("cxb", 2)
        self.ch_y = C("cy", 2)
        self.ch_z = C("cz", 2)
        self.ch_slab = C("cslab", 2)
        self.ch_t5 = Chan(nc, "ct5", 2, per_slot=8)
        self.ch_cnt = C("ccnt")
        self.ch_cna = C("ccna")
        self.ch_off = C("coff")
        self.ch_orow = C("corow")
        self.ch_rdf = C("crdf")
        self.ch_tz = C("ctz")
        self.ch_gbr = C("cgbr")
        self.ch_wi = C("cwi")
        self.ch_wo = C("cwo")
        self.ch_sc = C("csc")
        self.ch_tr = C("ctr")
        self.ch_direp = C("cdirep")
        self.ch_dg = C("cdg")
        self.ch_eo = C("ceo", 2)
        self.ch_gb2 = C("cgb2")
        self.ch_zy = Chan(nc, "czy", 2, per_slot=2)
        self.ch_out = C("cout", 2)
        self.ch_grp = C("cgrp")

    def incd(self, eng, sem, k=1):
        eng.drain()
        sem.inc(eng.nop(), k)

    def p0(self):
        nc = self.nc
        if True:
            blk = self.blk
            @blk.sync
            def _(sy):
                for i, (dst, src) in enumerate([
                        (self.c_bq, "bq"), (self.c_bk, "bk"),
                        (self.c_sel8, "sel8"), (self.c_ids, "ids"),
                        (self.c_pref, "pref"), (self.c_iota8u, "iota8u"),
                        (self.c_onesf, "onesc"), (self.c_identf, "identc"),
                        (self.c_meta, "meta_u")]):
                    self.ch_const.inc(sy.dma_start(out=dst[:], in_=self.inp[src].ap()), i)

            @blk.gpsimd
            def _(gp):
                i = 0
                for dst, src in [(self.c_wqf4, "wqf4"), (self.c_wkf2, "wkf2"),
                                 (self.c_bvr, "bvr"), (self.c_bpr, "bpr"),
                                 (self.c_wgt, "wgt"), (self.c_triu, "triu"),
                                 (self.c_ones, "onesc"), (self.c_identb, "identc"),
                                 (self.c_iota8f, "iota8u")]:
                    self.ch_wc.inc(gp.dma_start(out=dst[:], in_=self.inp[src].ap()), i)
                    i += 1
                for kc in range(8):
                    for wbf, wsrc in [(self.wq_bf, "wq"), (self.wk_bf, "wk"),
                                      (self.wv_bf, "wv"), (self.wp_bf, "wp")]:
                        self.ch_wc.inc(gp.dma_start(
                            out=wbf[:, kc * D:(kc + 1) * D],
                            in_=self.inp[wsrc].ap()[kc * 128:(kc + 1) * 128, :]), i)
                        i += 1

            @blk.vector
            def _(ve):
                self.ch_const.wait_all(ve)
                self.s_ev.inc(ve.memset(self.zero128[:], 0.0), 1)
        self.p0_zero = self.s_ev.n

    def p1(self):
        nc = self.nc
        ld, ev, ac, st = self.s_ld, self.s_ev, self.s_ac, self.s_st
        EV0, AC0 = ev.n, ac.n
        ev_t = lambda t, j: EV0 + 5 * t + j
        ac_t = lambda t, j: AC0 + 2 * t + j

        def p1_store(sy, t):
            sy.wait_ge(ev.h, ev_t(t, 2))
            self.ch_dx.inc(sy.dma_start(out=self.d_x.ap()[t * 128:(t + 1) * 128, :],
                                        in_=self.bf_st[:, (t % 2) * D:(t % 2 + 1) * D]), t)
            sy.wait_ge(ev.h, ev_t(t, 5))
            self.ch_l1.inc(sy.dma_start(out=self.d_ln1.ap()[t * 128:(t + 1) * 128, :],
                                        in_=self.bf_st[:, (2 + t % 2) * D:(3 + t % 2) * D]), t)

        if True:
            blk = self.blk
            @blk.sync
            def _(sy):
                for t in range(NT):
                    if t >= 2:
                        sy.wait_ge(ev.h, ev_t(t - 2, 5))
                    self.ch_x.inc(sy.dma_start(out=self.xs[:, (t % 2) * D:(t % 2 + 1) * D],
                                        in_=self.inp["x"].ap()[t * 128:(t + 1) * 128, :]), t)
                    if t >= 2:
                        p1_store(sy, t - 2)
                for t in range(NT - 2, NT):
                    p1_store(sy, t)

            @blk.vector
            def _(ve):
                for t in range(NT):
                    xt = self.xs[:, (t % 2) * D:(t % 2 + 1) * D]
                    self.ch_x.wait(ve, t)
                    ve.reduce_sum(self.st_sum[:, t:t + 1], xt, axis=AX.X)
                    ve.drain()
                    ve.tensor_scalar_mul(self.st_negmu[:, t:t + 1],
                                         self.st_sum[:, t:t + 1], -1.0 / D)
                    self.incd(ve, ev)
                    self.ch_dx.wait(ve, t - 2)
                    ev.inc(ve.tensor_copy(self.bf_st[:, (t % 2) * D:(t % 2 + 1) * D], xt), 1)
                    ve.wait_ge(ac.h, ac_t(t, 1))
                    ve.tensor_scalar(self.st_inv[:, t:t + 1], self.st_ssq[:, t:t + 1],
                                     1.0 / D, EPS, ALU.mult, ALU.add)
                    ve.drain()
                    ve.reciprocal(self.st_inv[:, t:t + 1], self.st_inv[:, t:t + 1])
                    self.incd(ve, ev)
                    ve.wait_ge(ac.h, ac_t(t, 2))
                    ve.tensor_mul(self.st_b2[:, t:t + 1], self.st_negmu[:, t:t + 1],
                                  self.st_inv[:, t:t + 1])
                    ve.drain()
                    self.incd(ve, ev)
                    self.ch_l1.wait(ve, t - 2)
                    ev.inc(ve.tensor_scalar(self.bf_st[:, (2 + t % 2) * D:(3 + t % 2) * D],
                                            xt, self.st_inv[:, t:t + 1],
                                            self.st_b2[:, t:t + 1], ALU.mult, ALU.add), 1)

            @blk.scalar
            def _(sc):
                for t in range(NT):
                    xt = self.xs[:, (t % 2) * D:(t % 2 + 1) * D]
                    sc.wait_ge(ev.h, ev_t(t, 1))
                    sc.activation(self.disc[:], xt, AF.Square,
                                  bias=self.st_negmu[:, t:t + 1], scale=1.0,
                                  accum_out=self.st_ssq[:, t:t + 1])
                    self.incd(sc, ac)
                    sc.wait_ge(ev.h, ev_t(t, 3))
                    sc.activation(self.st_inv[:, t:t + 1], self.st_inv[:, t:t + 1],
                                  AF.Sqrt)
                    self.incd(sc, ac)

        assert ev.n == EV0 + 5 * NT and ac.n == AC0 + 2 * NT

    def p2(self):
        nc = self.nc
        mm, ev, ac = self.s_mm, self.s_ev, self.s_ac
        MM0, EV0, AC0 = mm.n, ev.n, ac.n
        GPT, EVG, ACG = 152, 41, 64
        mm_qk = lambda g, w, oc: MM0 + GPT * g + w * 8 + oc + 1
        mm_v = lambda g, tl, hf: MM0 + GPT * g + 16 + tl * 2 + hf + 1
        mm_kp = lambda g, h, tl: MM0 + GPT * g + 24 + 8 * h + 2 * tl + 1
        mm_ctx = lambda g, h: MM0 + GPT * g + 24 + 8 * h + 8
        ev_qk = lambda g, w, oc: EV0 + EVG * g + w * 8 + oc + 1
        ev_v = lambda g, tl, hf: EV0 + EVG * g + 16 + tl * 2 + hf + 1
        ev_fold = lambda g, h: EV0 + EVG * g + 24 + h + 1
        ac_exp = lambda g, h, tl: AC0 + ACG * g + 4 * h + tl + 1

        if True:
            blk = self.blk
            @blk.scalar
            def _(sc):
                for g in range(NG):
                    if g == 0:
                        self.ch_l1.wait_all(sc)
                    if g >= 1:
                        sc.wait_ge(mm.h, mm_v(g - 1, TPG - 1, 1))
                    for kc in range(8):
                        self.ch_t2.inc(sc.dma_start(
                            out=self.lnxT[:, kc * 512:(kc + 1) * 512],
                            in_=self.d_ln1.ap()[g * 512:(g + 1) * 512,
                                                kc * 128:(kc + 1) * 128],
                            transpose=True), g)
                    for h in range(H):
                        for tl in range(TPG):
                            sc.wait_ge(mm.h, mm_kp(g, h, tl))
                            bank = [self.pD, self.pE][tl % 2]
                            ac.inc(sc.activation(
                                self.kp[:, (tl % 2) * FF:(tl % 2 + 1) * FF],
                                bank[:, 0:256], AF.Exp), 1)

            @blk.tensor
            def _(te):
                self.ch_wc.wait_all(te)
                self.ch_const.wait_all(te)
                for g in range(NG):
                    self.ch_t2.wait(te, g)
                    for w, wbf in [(0, self.wq_bf), (1, self.wk_bf)]:
                        for oc in range(8):
                            bank = [self.pA, self.pB][oc % 2]
                            k = 2 * g * 8 + w * 8 + oc
                            if k >= 2:
                                pk = k - 2
                                pg, r = divmod(pk, 16)
                                te.wait_ge(ev.h, ev_qk(pg, r // 8, r % 8))
                            ins = None
                            for i in range(8):
                                ins = te.matmul(bank[:, 0:512],
                                                wbf[:, i * D + oc * 128: i * D + (oc + 1) * 128],
                                                self.lnxT[:, i * 512:(i + 1) * 512],
                                                start=(i == 0), stop=(i == 7))
                            mm.inc(ins, 1)
                    for tl in range(TPG):
                        for hf in range(2):
                            k = g * 8 + tl * 2 + hf
                            if k >= 1:
                                pg, r = divmod(k - 1, 8)
                                te.wait_ge(ev.h, ev_v(pg, r // 2, r % 2))
                            for i in range(8):
                                te.matmul(self.pC[:, 0:512],
                                          self.lnxT[:, i * 512 + tl * 128: i * 512 + (tl + 1) * 128],
                                          self.wv_bf[:, i * D + hf * 512: i * D + (hf + 1) * 512],
                                          start=(i == 0), stop=False)
                            ins = te.matmul(self.pC[:, 0:512], self.c_ones[0:1, :],
                                            self.c_bvr[0:1, hf * 512:(hf + 1) * 512],
                                            start=False, stop=True)
                            mm.inc(ins, 1)
                    te.wait_ge(ev.h, ev_qk(g, 1, 7))
                    te.wait_ge(ev.h, ev_v(g, TPG - 1, 1))
                    for h in range(H):
                        ch, par = h // 2, h % 2
                        cbank = [self.pF, self.pG][h % 2]
                        if h >= 2:
                            te.wait_ge(ev.h, ev_fold(g, h - 2))
                        elif g >= 1:
                            te.wait_ge(ev.h, ev_fold(g - 1, H - 2 + h))
                        for tl in range(TPG):
                            bank = [self.pD, self.pE][tl % 2]
                            k = (g * H + h) * TPG + tl
                            if k >= 2:
                                pk = k - 2
                                pg2, r = divmod(pk, H * TPG)
                                te.wait_ge(ac.h, ac_exp(pg2, r // TPG, r % TPG))
                            ins = te.matmul(bank[:, 0:256],
                                            self.kT[:, ch * 512 + tl * 128: ch * 512 + (tl + 1) * 128],
                                            self.c_wkf2[:, par * 256:(par + 1) * 256],
                                            start=True, stop=True)
                            mm.inc(ins, 1)
                            te.wait_ge(ac.h, ac_exp(g, h, tl))
                            kpb = self.kp[:, (tl % 2) * FF:(tl % 2 + 1) * FF]
                            te.matmul(cbank[0:64, 0:256],
                                      self.v_g[:, tl * D + h * DH: tl * D + (h + 1) * DH],
                                      kpb, start=(tl == 0), stop=False,
                                      skip_group_check=True)
                            ins = te.matmul(cbank[64:65, 0:256], self.c_ones[:, 0:1], kpb,
                                            start=(tl == 0), stop=(tl == TPG - 1),
                                            skip_group_check=True)
                            mm.inc(ins, 1)

            @blk.vector
            def _(ve):
                for g in range(NG):
                    for w in range(2):
                        for oc in range(8):
                            bank = [self.pA, self.pB][oc % 2]
                            ve.wait_ge(mm.h, mm_qk(g, w, oc))
                            dst = (self.qT[:, oc * S + g * 512: oc * S + (g + 1) * 512]
                                   if w == 0 else self.kT[:, oc * 512:(oc + 1) * 512])
                            bias = (self.c_bq if w == 0 else self.c_bk)[:, oc:oc + 1]
                            ev.inc(ve.tensor_scalar_add(dst, bank[:, 0:512], bias), 1)
                    for tl in range(TPG):
                        for hf in range(2):
                            ve.wait_ge(mm.h, mm_v(g, tl, hf))
                            ev.inc(ve.tensor_copy(
                                self.v_g[:, tl * D + hf * 512: tl * D + (hf + 1) * 512],
                                self.pC[:, 0:512]), 1)
                    for h in range(H):
                        cbank = [self.pF, self.pG][h % 2]
                        ve.wait_ge(mm.h, mm_ctx(g, h))
                        cslice = self.ctx[0:65, h * FF:(h + 1) * FF]
                        if g == 0:
                            ev.inc(ve.tensor_copy(cslice, cbank[0:65, 0:256]), 1)
                        else:
                            ev.inc(ve.tensor_add(cslice, cslice, cbank[0:65, 0:256]), 1)
                    ev.inc(ve.nop(), 1)
        assert mm.n == MM0 + GPT * NG and ev.n == EV0 + EVG * NG and ac.n == AC0 + ACG * NG

    def p25(self):
        nc = self.nc
        mm, ev = self.s_mm, self.s_ev
        MM0, EV0 = mm.n, ev.n
        if True:
            blk = self.blk
            @blk.vector
            def _(ve):
                for h in range(H):
                    ev.inc(ve.tensor_copy(self.bf_st[0:65, h * FF:(h + 1) * FF],
                                          self.ctx[0:65, h * FF:(h + 1) * FF]), 1)

            @blk.tensor
            def _(te):
                te.wait_ge(ev.h, EV0 + H)
                for j in range(2 * H):
                    h, cc = divmod(j, 2)
                    if j >= 1:
                        te.wait_ge(ev.h, EV0 + H + j)
                    ins = te.transpose(self.pT[:, 0:65],
                                       self.bf_st[0:65, h * FF + cc * 128: h * FF + cc * 128 + 128],
                                       self.c_identb[0:65, 0:65])
                    mm.inc(ins, 1)

            @blk.vector
            def _(ve):
                for j in range(2 * H):
                    h, cc = divmod(j, 2)
                    ve.wait_ge(mm.h, MM0 + j + 1)
                    ev.inc(ve.tensor_copy(
                        self.ctxb[:, h * 130 + cc * 65: h * 130 + (cc + 1) * 65],
                        self.pT[:, 0:65]), 1)
        self.p25_ev = ev.n

    def p3(self):
        nc = self.nc
        mm, ev, ac = self.s_mm, self.s_ev, self.s_ac
        MM0, EV0, AC0 = mm.n, ev.n, ac.n
        GT = 32 + TPG * H
        mm_qp = lambda g, h, fh: MM0 + GT * g + h * 2 + fh + 1
        mm_nm = lambda g, tl, h: MM0 + GT * g + 32 + tl * H + h + 1
        ac_qp = lambda g, h, fh: AC0 + 32 * g + h * 2 + fh + 1
        ev_tile = lambda g, tl: EV0 + 32 * g + 8 * (tl + 1)
        NBK = [(0, 7, "pD"), (7, 14, "pE"), (14, 16, "pF")]

        if True:
            blk = self.blk
            @blk.tensor
            def _(te):
                te.wait_ge(ev.h, self.p25_ev)
                for g in range(NG):
                    for h in range(H):
                        for fh in range(2):
                            bank = [self.pA, self.pB][(h * 2 + fh) % 2]
                            k = g * 32 + h * 2 + fh
                            if k >= 2:
                                pk = k - 2
                                pg, r = divmod(pk, 32)
                                te.wait_ge(ac.h, ac_qp(pg, r // 2, r % 2))
                            ins = te.matmul(bank[:, 0:512],
                                            self.c_wqf4[:, (h % 2) * 256 + fh * 128:
                                                        (h % 2) * 256 + (fh + 1) * 128],
                                            self.qT[:, (h // 2) * S + g * 512:
                                                    (h // 2) * S + (g + 1) * 512],
                                            start=True, stop=True)
                            mm.inc(ins, 1)
                    for tl in range(TPG):
                        gk = g * TPG + tl
                        if gk >= 1:
                            pg, ptl = divmod(gk - 1, TPG)
                            te.wait_ge(ev.h, ev_tile(pg, ptl))
                        for h in range(H):
                            lo, hi, bname = next(b for b in NBK if b[0] <= h < b[1])
                            bank = getattr(self, bname)
                            col = (h - lo) * 65
                            te.wait_ge(ac.h, ac_qp(g, h, 1))
                            ins = None
                            for fh in range(2):
                                ins = te.matmul(bank[:, col:col + 65],
                                                self.qp[:, (h * 2 + fh) * 512 + tl * 128:
                                                        (h * 2 + fh) * 512 + (tl + 1) * 128],
                                                self.ctxb[:, h * 130 + fh * 65:
                                                          h * 130 + (fh + 1) * 65],
                                                start=(fh == 0), stop=(fh == 1),
                                                skip_group_check=True)
                            mm.inc(ins, 1)

            @blk.scalar
            def _(sc):
                for g in range(NG):
                    for h in range(H):
                        for fh in range(2):
                            bank = [self.pA, self.pB][(h * 2 + fh) % 2]
                            sc.wait_ge(mm.h, mm_qp(g, h, fh))
                            ac.inc(sc.activation(
                                self.qp[:, (h * 2 + fh) * 512:(h * 2 + fh + 1) * 512],
                                bank[:, 0:512], AF.Exp), 1)

            @blk.vector
            def _(ve):
                for g in range(NG):
                    for tl in range(TPG):
                        t = g * TPG + tl
                        ve.wait_ge(mm.h, mm_nm(g, tl, H - 1))
                        self.ch_cat.wait(ve, t - 2)
                        for bi, (lo, hi, bname) in enumerate(NBK):
                            bank = getattr(self, bname)
                            nh = hi - lo
                            dsrc = bass.AP(bank, 64, [[512, 128], [65, nh]])
                            if bi == 2:
                                ve.tensor_copy(self.dall[:, lo:hi], dsrc)
                                ve.drain()
                                ev.inc(ve.nop(), 1)
                            else:
                                ev.inc(ve.tensor_copy(self.dall[:, lo:hi], dsrc), 1)
                        ve.tensor_scalar_add(self.dall[:], self.dall[:], 1e-6)
                        ve.drain()
                        ev.inc(ve.nop(), 1)
                        ve.reciprocal(self.dinv[:], self.dall[:])
                        ve.drain()
                        ev.inc(ve.nop(), 1)
                        for lo, hi, bname in NBK:
                            bank = getattr(self, bname)
                            nh = hi - lo
                            nsrc = bass.AP(bank, 0, [[512, 128], [65, nh], [1, 64]])
                            dsrc = bass.AP(self.dinv, lo, [[16, 128], [1, nh], [0, 64]])
                            dst = bass.AP(self.bf_st, (t % 2) * 2 * D + lo * 64,
                                          [[4 * D, 128], [64, nh], [1, 64]])
                            ev.inc(ve.tensor_tensor(dst, nsrc, dsrc, ALU.mult), 1)

            @blk.sync
            def _(sy):
                for g in range(NG):
                    for tl in range(TPG):
                        t = g * TPG + tl
                        sy.wait_ge(ev.h, ev_tile(g, tl))
                        self.ch_cat.inc(sy.dma_start(
                            out=self.d_cat.ap()[t * 128:(t + 1) * 128, :],
                            in_=self.bf_st[:, (t % 2) * 2 * D:(t % 2) * 2 * D + D]), t)
        assert mm.n == MM0 + GT * NG and ev.n == EV0 + 32 * NG
        assert ac.n == AC0 + 32 * NG

    def p4(self):
        nc = self.nc
        mm, ev, ac = self.s_mm, self.s_ev, self.s_ac
        MM0, EV0, AC0 = mm.n, ev.n, ac.n
        mm_pj = lambda t, hf: MM0 + 2 * t + hf + 1

        if True:
            blk = self.blk
            @blk.gpsimd
            def _(gp):
                gp.wait_ge(mm.h, MM0)
                self.ch_grp.inc(gp.dma_start(out=self.c_grep[:],
                                             in_=self.inp["grep"].ap()), 0)
                self.ch_grp.inc(gp.dma_start(out=self.c_brep[:],
                                             in_=self.inp["brep"].ap()), 1)
        ac_t = lambda t, j: AC0 + 6 * t + j
        ev_t = lambda t, j: EV0 + 9 * t + j

        if True:
            blk = self.blk
            @blk.sync
            def _(sy):
                def p4_store(t):
                    A = self.bf_st[:, (t % 2) * D:(t % 2 + 1) * D]
                    B = self.bf_st[:, (2 + t % 2) * D:(3 + t % 2) * D]
                    xbt = self.xb[:, (t % 2) * D:(t % 2 + 1) * D]
                    sy.wait_ge(ev.h, ev_t(t, 5))
                    self.ch_y.inc(sy.dma_start(
                        out=self.d_y.ap()[t * 128:(t + 1) * 128, :], in_=A), t)
                    sy.wait_ge(ev.h, ev_t(t, 8))
                    self.ch_z.inc(sy.dma_start(
                        out=self.d_z.ap()[t * 128:(t + 1) * 128, :], in_=B), t)
                    sy.wait_ge(ev.h, ev_t(t, 9))
                    self.ch_slab.inc(sy.dma_start(
                        out=self.slab_loc.ap()[t * 128:(t + 1) * 128, :], in_=xbt), t)
                for t in range(NT):
                    self.ch_dx.wait_all(sy) if t == 0 else None
                    self.ch_xb.inc(sy.dma_start(
                        out=self.xb[:, (t % 2) * D:(t % 2 + 1) * D],
                        in_=self.d_x.ap()[t * 128:(t + 1) * 128, :]), t)
                    if t >= 1:
                        p4_store(t - 1)
                        if t % 8 == 0:
                            self.ch_slab.wait(sy, t - 1)
                            self.s_slabg.inc(sy.nop(), 1)
                p4_store(NT - 1)
                self.ch_slab.wait(sy, NT - 1)
                self.s_slabg.inc(sy.nop(), 1)

            @blk.tensor
            def _(te):
                for t in range(NT):
                    g, tl = divmod(t, TPG)
                    self.ch_t4.wait(te, g)
                    if t >= 1:
                        te.wait_ge(ac.h, ac_t(t - 1, 2))
                    for hf in range(2):
                        bank = [self.pA, self.pB][hf]
                        for i in range(8):
                            te.matmul(bank[:, 0:512],
                                      self.catT[:, i * 512 + tl * 128: i * 512 + (tl + 1) * 128],
                                      self.wp_bf[:, i * D + hf * 512: i * D + (hf + 1) * 512],
                                      start=(i == 0), stop=False)
                        ins = te.matmul(bank[:, 0:512], self.c_ones[0:1, :],
                                        self.c_bpr[0:1, hf * 512:(hf + 1) * 512],
                                        start=False, stop=True)
                        mm.inc(ins, 1)

            @blk.scalar
            def _(sc):
                for t in range(NT):
                    if t % TPG == 0:
                        g = t // TPG
                        if g == 0:
                            self.ch_cat.wait_all(sc)
                        else:
                            sc.wait_ge(mm.h, mm_pj(g * TPG - 1, 1))
                        for kc in range(8):
                            self.ch_t4.inc(sc.dma_start(
                                out=self.catT[:, kc * 512:(kc + 1) * 512],
                                in_=self.d_cat.ap()[g * 512:(g + 1) * 512,
                                                    kc * 128:(kc + 1) * 128],
                                transpose=True), g)
                    A = self.bf_st[:, (t % 2) * D:(t % 2 + 1) * D]
                    sc.wait_ge(mm.h, mm_pj(t, 1))
                    self.ch_y.wait(sc, t - 2)
                    sc.activation(A[:, 0:512], self.pA[:, 0:512], AF.Copy,
                                  accum_out=self.st_sum[:, t:t + 1])
                    sc.activation(A[:, 512:1024], self.pB[:, 0:512], AF.Copy,
                                  accum_out=self.st_ssq[:, t:t + 1])
                    self.incd(sc, ac, 2)
                    sc.wait_ge(ev.h, ev_t(t, 1))
                    sc.activation(self.disc[:], A, AF.Square,
                                  bias=self.st_negmu[:, t:t + 1], scale=1.0,
                                  accum_out=self.st2_ssq[:, t:t + 1])
                    self.incd(sc, ac)
                    sc.wait_ge(ev.h, ev_t(t, 2))
                    sc.activation(self.st_inv[:, t:t + 1], self.st_inv[:, t:t + 1],
                                  AF.Sqrt)
                    self.incd(sc, ac)
                    sc.wait_ge(ev.h, ev_t(t, 6))
                    sc.activation(self.disc[:], A, AF.Square,
                                  bias=self.st2_negmu[:, t:t + 1], scale=1.0,
                                  accum_out=self.st2_ssq[:, t:t + 1])
                    self.incd(sc, ac)
                    sc.wait_ge(ev.h, ev_t(t, 7))
                    sc.activation(self.st2_inv[:, t:t + 1], self.st2_inv[:, t:t + 1],
                                  AF.Sqrt)
                    self.incd(sc, ac)

            @blk.vector
            def _(ve):
                for t in range(NT):
                    A = self.bf_st[:, (t % 2) * D:(t % 2 + 1) * D]
                    B = self.bf_st[:, (2 + t % 2) * D:(3 + t % 2) * D]
                    xbt = self.xb[:, (t % 2) * D:(t % 2 + 1) * D]
                    self.ch_z.wait(ve, t - 2)
                    ve.wait_ge(ac.h, ac_t(t, 2))
                    ve.tensor_add(self.st_sum[:, t:t + 1], self.st_sum[:, t:t + 1],
                                  self.st_ssq[:, t:t + 1])
                    ve.drain()
                    ve.tensor_scalar_mul(self.st_negmu[:, t:t + 1],
                                         self.st_sum[:, t:t + 1], -1.0 / D)
                    self.incd(ve, ev)
                    ve.wait_ge(ac.h, ac_t(t, 3))
                    ve.tensor_scalar(self.st_inv[:, t:t + 1], self.st2_ssq[:, t:t + 1],
                                     1.0 / D, EPS, ALU.mult, ALU.add)
                    ve.drain()
                    ve.reciprocal(self.st_inv[:, t:t + 1], self.st_inv[:, t:t + 1])
                    self.incd(ve, ev)
                    ve.wait_ge(ac.h, ac_t(t, 4))
                    ve.tensor_mul(self.st_b2[:, t:t + 1], self.st_negmu[:, t:t + 1],
                                  self.st_inv[:, t:t + 1])
                    ve.drain()
                    ev.inc(ve.nop(), 1)
                    ev.inc(ve.tensor_scalar(B, A, self.st_inv[:, t:t + 1],
                                            self.st_b2[:, t:t + 1], ALU.mult, ALU.add), 1)
                    self.ch_xb.wait(ve, t)
                    ev.inc(ve.tensor_add(A, xbt, B), 1)
                    ve.reduce_sum(self.st_sum[:, t:t + 1], A, axis=AX.X)
                    ve.drain()
                    ve.tensor_scalar_mul(self.st2_negmu[:, t:t + 1],
                                         self.st_sum[:, t:t + 1], -1.0 / D)
                    self.incd(ve, ev)
                    ve.wait_ge(ac.h, ac_t(t, 5))
                    ve.tensor_scalar(self.st2_inv[:, t:t + 1], self.st2_ssq[:, t:t + 1],
                                     1.0 / D, EPS, ALU.mult, ALU.add)
                    ve.drain()
                    ve.reciprocal(self.st2_inv[:, t:t + 1], self.st2_inv[:, t:t + 1])
                    self.incd(ve, ev)
                    ve.wait_ge(ac.h, ac_t(t, 6))
                    ve.tensor_mul(self.st2_b2[:, t:t + 1], self.st2_negmu[:, t:t + 1],
                                  self.st2_inv[:, t:t + 1])
                    ve.drain()
                    ev.inc(ve.tensor_scalar(B, A, self.st2_inv[:, t:t + 1],
                                            self.st2_b2[:, t:t + 1], ALU.mult, ALU.add), 1)
                    if t == 0:
                        self.ch_grp.wait_all(ve)
                    ve.tensor_mul(xbt, B, self.c_grep[:])
                    ev.inc(ve.tensor_add(xbt, xbt, self.c_brep[:]), 1)

        assert mm.n == MM0 + 2 * NT and ac.n == AC0 + 6 * NT
        assert ev.n == EV0 + 9 * NT
        self.p4_mm_end = mm.n

    def p5(self):
        nc = self.nc
        mm, ev, ac = self.s_mm, self.s_ev, self.s_ac
        cc = self.s_cc
        MM0, EV0, AC0 = mm.n, ev.n, ac.n
        mm_lg = lambda t: MM0 + 3 * t + 1
        mm_rk = lambda t: MM0 + 3 * t + 2
        EVT = 8
        ev_t = lambda t, j: EV0 + EVT * t + j
        ac_exp = lambda t: AC0 + t + 1

        if True:
            blk = self.blk
            @blk.scalar
            def _(sc):
                for g in range(NG):
                    if g == 0:
                        self.ch_slab.wait_all(sc)
                    if g >= 1:
                        sc.wait_ge(mm.h, mm_lg((g - 1) * TPG + TPG - 1))
                    for kc in range(8):
                        self.ch_t5.inc(sc.dma_start(
                            out=self.ln2tT[:, kc * 512:(kc + 1) * 512],
                            in_=self.slab_loc.ap()[g * 512:(g + 1) * 512,
                                                   kc * 128:(kc + 1) * 128],
                            transpose=True), g)
                    for tl in range(TPG):
                        t = g * TPG + tl
                        sc.wait_ge(ev.h, ev_t(t, 2))
                        ac.inc(sc.activation(self.r_exp[:], self.pD[:, 0:8], AF.Exp,
                                             bias=self.r_s1[:, 0:1], scale=1.0), 1)

            @blk.tensor
            def _(te):
                for t in range(NT):
                    g, tl = divmod(t, TPG)
                    self.ch_t5.wait(te, g)
                    if t >= 1:
                        te.wait_ge(ev.h, ev_t(t - 1, 3))
                        te.wait_ge(ac.h, ac_exp(t - 1))
                    ins = None
                    for i in range(8):
                        ins = te.matmul(self.pD[:, 0:8],
                                        self.ln2tT[:, i * 512 + tl * 128: i * 512 + (tl + 1) * 128],
                                        self.c_wgt[:, i * 8:(i + 1) * 8],
                                        start=(i == 0), stop=(i == 7))
                    mm.inc(ins, 1)
                    te.wait_ge(ev.h, ev_t(t, 5))
                    if t >= 1:
                        te.wait_ge(ev.h, ev_t(t - 1, 8))
                    ins = te.matmul(self.pE[:, 0:8], self.c_triu[:], self.r_oh[:],
                                    start=True, stop=True)
                    mm.inc(ins, 1)
                    ins = te.matmul(self.pF[0:8, t:t + 1], self.r_oh[:], self.c_ones[:, 0:1],
                                    start=True, stop=True, skip_group_check=True)
                    mm.inc(ins, 1)

            @blk.vector
            def _(ve):
                for t in range(NT):
                    ve.wait_ge(mm.h, mm_lg(t))
                    ve.max(self.r_max[:], self.pD[:, 0:8])
                    ve.drain()
                    ve.max_index(self.r_idx[:], self.r_max[:], self.pD[:, 0:8])
                    self.incd(ve, ev)
                    ve.tensor_scalar_mul(self.r_s1[:], self.r_max[:, 0:1], -1.0)
                    self.incd(ve, ev)
                    ve.wait_ge(ac.h, ac_exp(t))
                    ve.reduce_sum(self.r_s1[:], self.r_exp[:], axis=AX.X)
                    ve.drain()
                    ve.reciprocal(self.r_w[:, t:t + 1], self.r_s1[:])
                    self.incd(ve, ev)
                    if t >= 1:
                        ve.wait_ge(mm.h, MM0 + 3 * (t - 1) + 3)
                    ve.tensor_single_scalar(
                        self.r_ohf[:], self.c_iota8u.ap().bitcast(F32),
                        bass.AP(self.r_idx, 0, [[8, 128], [1, 1]]).bitcast(F32),
                        ALU.is_equal)
                    ve.drain()
                    ev.inc(ve.nop(), 1)
                    ve.tensor_copy(self.r_oh[:], self.r_ohf[:])
                    self.incd(ve, ev)
                    ev.inc(ve.tensor_copy(self.r_ohall[:, t * 8:(t + 1) * 8],
                                          self.r_ohf[:]), 1)
                    ve.tensor_mul(self.r_exp[:], self.r_ohf[:], self.c_iota8f[:])
                    ve.drain()
                    ev.inc(ve.reduce_sum(self.r_e[:, t:t + 1], self.r_exp[:], axis=AX.X), 1)
                    ve.wait_ge(mm.h, mm_rk(t))
                    ev.inc(ve.tensor_copy(self.r_rank[:, t * 8:(t + 1) * 8],
                                          self.pE[:, 0:8]), 1)
        assert mm.n == MM0 + 3 * NT and ev.n == EV0 + EVT * NT and ac.n == AC0 + NT

        MM1, EV1 = mm.n, ev.n
        if True:
            blk = self.blk
            @blk.vector
            def _(ve):
                ve.wait_ge(mm.h, MM1)
                ve.tensor_copy(self.r_cnt[:], self.pF[0:8, 0:NT])
                ve.drain()
                ve.reduce_sum(self.r_base[:], self.r_cnt[:], axis=AX.X)
                self.incd(ve, ev)
                ve.tensor_tensor_scan(self.r_scan[:], self.r_cnt[:],
                                      self.zero128[0:8, 0:NT], 0.0, ALU.add, ALU.add)
                ve.drain()
                ve.tensor_sub(self.r_offs[:], self.r_scan[:], self.r_cnt[:])
                self.incd(ve, ev)

            @blk.sync
            def _(sy):
                sy.wait_ge(ev.h, EV1 + 1)
                self.ch_cnt.inc(sy.dma_start(out=self.cnt_loc.ap(),
                                              in_=self.r_base[0:8, 0:1]), 0)

        if True:
            blk = self.blk
            @blk.gpsimd
            def _(gpe):
                for g2 in range(4):
                    gpe.wait_ge(self.s_slabg.h, g2 + 1)
                    cc.inc(gpe.collective_compute(
                        "AllGather", ALU.bypass, replica_groups=[list(range(NC))],
                        ins=[self.slab_loc.ap()[g2 * 1024:(g2 + 1) * 1024, :].opt()],
                        outs=[self.slab_all.ap()[g2 * NC * 1024:(g2 + 1) * NC * 1024, :].opt()]), 1)
                self.ch_cnt.wait_all(gpe)
                cc.inc(gpe.collective_compute(
                    "AllGather", ALU.bypass, replica_groups=[list(range(NC))],
                    ins=[self.cnt_loc.ap().opt()], outs=[self.cnt_all.ap().opt()]), 1)
        CC_AGS, CC_AGC = cc.n - 1, cc.n

        if True:
            blk = self.blk
            @blk.sync
            def _(sy):
                sy.wait_ge(cc.h, CC_AGC)
                self.ch_cna.inc(sy.dma_start(out=self.r_cntall[:], in_=self.cnt_all.ap()), 0)

        if True:
            blk = self.blk
            @blk.tensor
            def _(te):
                self.ch_cna.wait_all(te)
                ins = te.matmul(self.pG[0:8, 0:1], self.r_cntall[:], self.c_pref[0:8, 0:1],
                                start=True, stop=True)
                mm.inc(ins, 1)

            @blk.vector
            def _(ve):
                ve.wait_ge(mm.h, MM1 + 1)
                ve.tensor_copy(self.r_base[:], self.pG[0:8, 0:1])
                ve.drain()
                ev.inc(ve.nop(), 1)
                ve.wait_ge(ev.h, EV1 + 2)
                ve.tensor_scalar_add(self.r_offs[:], self.r_offs[:],
                                     self.r_base[:, 0:1])
                self.incd(ve, ev)
        EV2 = ev.n

        if True:
            blk = self.blk
            @blk.tensor
            def _(te):
                te.wait_ge(ev.h, EV2)
                ins = te.transpose(self.pG[0:32, 0:8], self.r_offs[:],
                                   self.c_identf[0:8, 0:8])
                mm.inc(ins, 1)

            @blk.vector
            def _(ve):
                ve.wait_ge(mm.h, MM1 + 2)
                ve.tensor_copy(self.r_offsT[:], self.pG[0:32, 0:8])
                self.incd(ve, ev)

            @blk.sync
            def _(sy):
                sy.wait_ge(ev.h, EV2 + 1)
                self.ch_off.inc(sy.dma_start(out=self.offs_d.ap(), in_=self.r_offsT[:]), 0)
                self.ch_off.wait_all(sy)
                self.ch_orow.inc(sy.dma_start(
                    out=self.r_offsrow[:],
                    in_=bass.AP(self.offs_d, 0, [[1, 1], [1, NT * E]])), 0)

        MM2 = mm.n
        if True:
            blk = self.blk
            @blk.tensor
            def _(te):
                self.ch_orow.wait_all(te)
                for t in range(NT):
                    if t >= 1:
                        te.wait_ge(ev.h, EV2 + 1 + 3 * t)
                    ins = te.matmul(self.pG[:, 0:8], self.c_onesf[0:1, :],
                                    self.r_offsrow[0:1, t * 8:(t + 1) * 8],
                                    start=True, stop=True)
                    mm.inc(ins, 1)

            @blk.vector
            def _(ve):
                for t in range(NT):
                    ve.wait_ge(mm.h, MM2 + t + 1)
                    ve.tensor_add(self.r_exp[:], self.pG[:, 0:8],
                                  self.r_rank[:, t * 8:(t + 1) * 8])
                    ve.drain()
                    ev.inc(ve.nop(), 1)
                    ve.tensor_mul(self.r_exp[:], self.r_exp[:],
                                  self.r_ohall[:, t * 8:(t + 1) * 8])
                    ve.drain()
                    ev.inc(ve.nop(), 1)
                    ve.reduce_sum(self.r_rk[:, t:t + 1], self.r_exp[:], axis=AX.X)
                    self.incd(ve, ev)
        EV3 = ev.n

        if True:
            blk = self.blk
            @blk.vector
            def _(ve):
                ve.wait_ge(ev.h, EV3)
                ve.tensor_single_scalar(self.r_keep[:], self.r_rk[:], float(CAP), ALU.is_lt)
                ve.drain()
                ve.tensor_mul(self.r_w[:], self.r_w[:], self.r_keep[:])
                ve.drain()
                ve.tensor_scalar(self.r_t1[:], self.r_rk[:], 1.0 / 16.0, -0.499,
                                 ALU.mult, ALU.add)
                ve.drain()
                ve.tensor_copy(self.r_rdi[:, 0:NT], self.r_t1[:])
                ve.tensor_scalar(self.r_t1[:], self.r_rk[:], 1.0 / 256.0, -0.499,
                                 ALU.mult, ALU.add)
                ve.drain()
                ve.tensor_copy(self.r_rdi[:, NT:2 * NT], self.r_t1[:])
                self.incd(ve, ev)
        EV4 = ev.n

        if True:
            blk = self.blk
            @blk.gpsimd
            def _(gpe):
                gpe.wait_ge(ev.h, EV4)
                self.ch_rdf.inc(gpe.dma_start(out=self.r_rdf[:], in_=self.r_rdi[:]), 0)

        if True:
            blk = self.blk
            @blk.vector
            def _(ve):
                self.ch_rdf.wait_all(ve)
                ve.scalar_tensor_tensor(self.r_rm[:], self.r_rdf[:, 0:NT], -16.0,
                                        self.r_rk[:], ALU.mult, ALU.add)
                ve.drain()
                ve.tensor_scalar_mul(self.r_a1[:], self.r_e[:], float(CAP))
                ve.drain()
                ve.scalar_tensor_tensor(self.r_a1[:], self.r_rm[:], 128.0, self.r_a1[:],
                                        ALU.mult, ALU.add)
                ve.drain()
                ve.tensor_add(self.r_a1[:], self.r_a1[:], self.r_rdf[:, 0:NT])
                ve.drain()
                ve.tensor_mul(self.r_a1[:], self.r_a1[:], self.r_keep[:])
                ve.drain()
                ve.tensor_scalar(self.r_t1[:], self.r_keep[:], -DROP, DROP,
                                 ALU.mult, ALU.add)
                ve.drain()
                ve.tensor_add(self.r_a1[:], self.r_a1[:], self.r_t1[:])
                ve.drain()
                ve.tensor_copy(self.r_a1u[:], self.r_a1[:])
                self.incd(ve, ev)
                ve.tensor_scalar_mul(self.r_a2[:], self.r_e[:], 256.0)
                ve.drain()
                ve.tensor_add(self.r_a2[:], self.r_a2[:], self.r_rk[:])
                ve.drain()
                ve.scalar_tensor_tensor(self.r_a2[:], self.r_rdf[:, NT:2 * NT],
                                        1792.0, self.r_a2[:], ALU.mult, ALU.add)
                ve.drain()
                ve.tensor_mul(self.r_a2[:], self.r_a2[:], self.r_keep[:])
                self.incd(ve, ev)
        EV5 = ev.n

        MM3 = mm.n
        if True:
            blk = self.blk
            @blk.tensor
            def _(te):
                te.wait_ge(ev.h, EV5)
                for r in range(8):
                    if r >= 1:
                        te.wait_ge(ev.h, EV5 + r)
                    ins = te.matmul(self.pG[0:16, 0:32], self.c_sel8[:, r * 16:(r + 1) * 16],
                                    self.r_a2[:], start=True, stop=True)
                    mm.inc(ins, 1)

            @blk.vector
            def _(ve):
                for r in range(8):
                    ve.wait_ge(mm.h, MM3 + r + 1)
                    dst = bass.AP(self.gb_f, r, [[256, 16], [8, 32]])
                    if r == 7:
                        ve.tensor_copy(dst, self.pG[0:16, 0:32])
                        self.incd(ve, ev)
                    else:
                        ev.inc(ve.tensor_copy(dst, self.pG[0:16, 0:32]), 1)
                ve.wait_ge(ev.h, EV5 + 8)
                ve.memset(self.gb_i[:], 0)
                ve.drain()
                ve.tensor_copy(self.gb_i[0:16, :], self.gb_f[:])
                self.incd(ve, ev)
        EV6 = ev.n
        self.p5_ev_gb = EV6

        if True:
            blk = self.blk
            @blk.sync
            def _(sy):
                sy.wait_ge(ev.h, self.p0_zero)
                self.ch_tz.inc(sy.dma_start(
                    out=bass.AP(self.tab_loc, 0, [[1, 1], [1, E * CAP]]),
                    in_=self.zero128[:]), 0)
                sy.wait_ge(ev.h, EV6)
                self.ch_gbr.inc(sy.dma_start(out=self.gb_i[16:32, :],
                                             in_=self.gb_i[0:16, :]), 0)

        if True:
            blk = self.blk
            @blk.gpsimd
            def _(gpe):
                gpe.wait_ge(mm.h, self.p3_mm_end)
                for kc in range(8):
                    self.ch_wi.inc(gpe.dma_start(
                        out=self.wi_bf[:, kc * IFF:(kc + 1) * IFF],
                        in_=self.inp["wi_c"].ap()[kc * 128:(kc + 1) * 128, :]), kc)
                gpe.wait_ge(mm.h, self.p4_mm_end)
                for ic in range(32):
                    self.ch_wo.inc(gpe.dma_start(
                        out=self.wo_bf[:, ic * D:(ic + 1) * D],
                        in_=self.inp["wo_c"].ap()[ic * 128:(ic + 1) * 128, :]), ic)
                gpe.wait_ge(ev.h, EV5)
                self.ch_tz.wait_all(gpe)
                for t in range(NT):
                    self.ch_sc.inc(gpe.indirect_dma_start(
                        self.tab_loc.ap(),
                        bass.IndirectOffsetOnAxis(self.r_a1u[:, t:t + 1], 0),
                        self.c_ids[:, t:t + 1], None,
                        bounds_check=E * CAP - 1, oob_is_err=False), t)
                self.ch_sc.wait_all(gpe)
                cc.inc(gpe.collective_compute(
                    "AllReduce", ALU.add, replica_groups=[list(range(NC))],
                    ins=[self.tab_loc.ap().opt()], outs=[self.tab_all.ap().opt()]), 1)
        self.p5_cc_tab = cc.n
        self.p5_cc_slab = CC_AGS

    def p6(self):
        nc = self.nc
        mm, ev, ac, cc = self.s_mm, self.s_ev, self.s_ac, self.s_cc
        MM0, EV0, AC0 = mm.n, ev.n, ac.n

        if True:
            blk = self.blk
            @blk.sync
            def _(sy):
                sy.wait_ge(cc.h, self.p5_cc_tab)
                if "tabread" in KSKIP:
                    self.ch_tr.inc(sy.dma_start(
                        out=self.di_f[:],
                        in_=bass.AP(self.tab_all, 0, [[128, 16], [1, 128]])), 0)
                else:
                    with sy.register("tabreg") as reg:
                        sy.reg_load(reg, self.c_meta[0:1, 0:1])
                        tsrc = bass.AP(self.tab_all, reg, [[128, 16], [1, 128]])
                        self.ch_tr.inc(sy.dma_start(out=self.di_f[:], in_=tsrc), 0)

        if True:
            blk = self.blk
            @blk.vector
            def _(ve):
                self.ch_tr.wait_all(ve)
                ve.memset(self.di_i[:], 0)
                ve.drain()
                ve.tensor_copy(self.di_i[0:16, :], self.di_f[:])
                self.incd(ve, ev)
        EV1 = ev.n

        if True:
            blk = self.blk
            @blk.sync
            def _(sy):
                sy.wait_ge(ev.h, EV1)
                self.ch_direp.inc(sy.dma_start(out=self.di_i[16:32, :],
                                               in_=self.di_i[0:16, :]), 0)

        if True:
            blk = self.blk
            @blk.gpsimd
            def _(gpe):
                gpe.load_library(library_config.mlp)
                self.ch_direp.wait_all(gpe)
                gpe.wait_ge(cc.h, self.p5_cc_slab)
                self.g_nreg = gpe.alloc_register("gnum")
                gpe.reg_mov(self.g_nreg, 256)
                for sg in range(8):
                    self.ch_dg.inc(gpe.dma_gather(
                        self.dispT[:, sg], self.slab_all.ap(),
                        self.di_i[:, sg * 16:(sg + 1) * 16],
                        256, self.g_nreg, D, transpose=True), sg)

        NSG = 8
        MM0, EV0, AC0 = mm.n, ev.n, ac.n
        mm_wi = lambda sg, ic: MM0 + 36 * sg + ic + 1
        mm_wo = lambda sg, stt, hf: MM0 + 36 * sg + 32 + stt * 2 + hf + 1
        ac_ge = lambda sg, ic: AC0 + 32 * sg + ic + 1
        ev_wo = lambda sg, stt, hf: EV0 + 4 * sg + stt * 2 + hf + 1

        if True:
            blk = self.blk
            @blk.tensor
            def _(te):
                self.ch_dg.wait_all(te)
                self.ch_wi.wait_all(te)
                self.ch_wo.wait_all(te)
                for sg in range(NSG):
                    for ic in range(32):
                        bank = [self.pA, self.pB][ic % 2]
                        k = sg * 32 + ic
                        if k >= 2:
                            pk = k - 2
                            te.wait_ge(ac.h, ac_ge(pk // 32, pk % 32))
                        ins = None
                        for kc in range(8):
                            ins = te.matmul(bank[:, 0:256],
                                            self.wi_bf[:, kc * IFF + ic * 128: kc * IFF + (ic + 1) * 128],
                                            self.dispT[:, sg, kc],
                                            start=(kc == 0), stop=(kc == 7))
                        mm.inc(ins, 1)
                    for stt in range(2):
                        for hf in range(2):
                            bank = [self.pC, self.pD][(stt * 2 + hf) % 2]
                            k = sg * 4 + stt * 2 + hf
                            if k >= 2:
                                pk = k - 2
                                te.wait_ge(ev.h, ev_wo(pk // 4, (pk % 4) // 2, pk % 2))
                            te.wait_ge(ac.h, ac_ge(sg, 31))
                            ins = None
                            for ic in range(32):
                                ins = te.matmul(bank[:, 0:512],
                                                self.hT[:, ic * 256 + stt * 128: ic * 256 + (stt + 1) * 128],
                                                self.wo_bf[:, ic * D + hf * 512: ic * D + (hf + 1) * 512],
                                                start=(ic == 0), stop=(ic == 31))
                            mm.inc(ins, 1)

            @blk.scalar
            def _(sc):
                for sg in range(NSG):
                    for ic in range(32):
                        bank = [self.pA, self.pB][ic % 2]
                        sc.wait_ge(mm.h, mm_wi(sg, ic))
                        if sg >= 1:
                            sc.wait_ge(mm.h, mm_wo(sg - 1, 1, 1))
                        ac.inc(sc.activation(self.hT[:, ic * 256:(ic + 1) * 256],
                                             bank[:, 0:256], AF.Gelu), 1)

            @blk.vector
            def _(ve):
                for sg in range(NSG):
                    for stt in range(2):
                        for hf in range(2):
                            bank = [self.pC, self.pD][(stt * 2 + hf) % 2]
                            ve.wait_ge(mm.h, mm_wo(sg, stt, hf))
                            self.ch_eo.wait(ve, sg * 2 + stt - 2)
                            ev.inc(ve.tensor_copy(
                                self.bf_st[:, (stt % 2) * D + hf * 512:
                                           (stt % 2) * D + (hf + 1) * 512],
                                bank[:, 0:512]), 1)

            @blk.sync
            def _(sy):
                for sg in range(NSG):
                    for stt in range(2):
                        sy.wait_ge(ev.h, ev_wo(sg, stt, 1))
                        row = sg * 256 + stt * 128
                        self.ch_eo.inc(sy.dma_start(
                            out=self.eo_loc.ap()[row:row + 128, :],
                            in_=self.bf_st[:, (stt % 2) * D:(stt % 2 + 1) * D]), sg * 2 + stt)
                    self.ch_eo.wait(sy, sg * 2 + 1)
                    self.s_eog.inc(sy.nop(), 1)
        assert mm.n == MM0 + 36 * NSG and ac.n == AC0 + 32 * NSG
        assert ev.n == EV0 + 4 * NSG

        if True:
            blk = self.blk
            @blk.gpsimd
            def _(gpe):
                for sg in range(NSG):
                    gpe.wait_ge(self.s_eog.h, sg + 1)
                    cc.inc(gpe.collective_compute(
                        "AllGather", ALU.bypass, replica_groups=[list(range(NC))],
                        ins=[self.eo_loc.ap()[sg * 256:(sg + 1) * 256, :].opt()],
                        outs=[self.eo_all.ap()[sg * E * 256:(sg + 1) * E * 256, :].opt()]), 1)
        self.p6_cc_eo = cc.n

    def p7(self):
        nc = self.nc
        mm, ev, ac, cc = self.s_mm, self.s_ev, self.s_ac, self.s_cc
        EV0, AC0 = ev.n, ac.n

        if True:
            blk = self.blk
            @blk.gpsimd
            def _(gpe):
                gpe.wait_ge(cc.h, self.p6_cc_eo)
                gpe.wait_ge(ev.h, self.p5_ev_gb)
                self.ch_gbr.wait_all(gpe)
                for c in range(NT // 2):
                    self.ch_gb2.inc(gpe.dma_gather(
                        self.comb[:, c], self.eo_all.ap(),
                        self.gb_i[:, c * 16:(c + 1) * 16],
                        256, self.g_nreg, D, transpose=False), c)

        ev_t = lambda t, j: EV0 + 5 * t + j
        ac_t = lambda t, j: AC0 + 2 * t + j

        if True:
            blk = self.blk
            @blk.sync
            def _(sy):
                def p7_store(t):
                    sy.wait_ge(ev.h, ev_t(t, 5))
                    self.ch_out.inc(sy.dma_start(
                        out=self.out.ap()[t * 128:(t + 1) * 128, :],
                        in_=self.ost[:, (t % 2) * D:(t % 2 + 1) * D]), t)
                for t in range(NT):
                    if t >= 2:
                        sy.wait_ge(ev.h, ev_t(t - 2, 5))
                    self.ch_zy.inc(sy.dma_start(out=self.zst[:, (t % 2) * D:(t % 2 + 1) * D],
                                        in_=self.d_z.ap()[t * 128:(t + 1) * 128, :]), t)
                    self.ch_zy.inc(sy.dma_start(out=self.yst[:, (t % 2) * D:(t % 2 + 1) * D],
                                        in_=self.d_y.ap()[t * 128:(t + 1) * 128, :]), t)
                    if t >= 1:
                        p7_store(t - 1)
                p7_store(NT - 1)

            @blk.vector
            def _(ve):
                self.ch_gb2.wait_all(ve)
                for t in range(NT):
                    Ft = self.fst[:, (t % 2) * D:(t % 2 + 1) * D]
                    Ot = self.ost[:, (t % 2) * D:(t % 2 + 1) * D]
                    self.ch_zy.wait(ve, t)
                    self.ch_out.wait(ve, t - 2)
                    ve.scalar_tensor_tensor(Ft, self.comb.ap()[:, t // 2, t % 2, :],
                                            self.r_w[:, t:t + 1],
                                            self.zst[:, (t % 2) * D:(t % 2 + 1) * D],
                                            ALU.mult, ALU.add)
                    ve.reduce_sum(self.st_sum[:, t:t + 1], Ft, axis=AX.X)
                    ve.drain()
                    ve.tensor_scalar_mul(self.st_negmu[:, t:t + 1],
                                         self.st_sum[:, t:t + 1], -1.0 / D)
                    self.incd(ve, ev)
                    ve.wait_ge(ac.h, ac_t(t, 1))
                    ve.tensor_scalar(self.st_inv[:, t:t + 1], self.st_ssq[:, t:t + 1],
                                     1.0 / D, EPS, ALU.mult, ALU.add)
                    ve.drain()
                    ve.reciprocal(self.st_inv[:, t:t + 1], self.st_inv[:, t:t + 1])
                    self.incd(ve, ev)
                    ve.wait_ge(ac.h, ac_t(t, 2))
                    ve.tensor_mul(self.st_b2[:, t:t + 1], self.st_negmu[:, t:t + 1],
                                  self.st_inv[:, t:t + 1])
                    ve.drain()
                    ev.inc(ve.nop(), 1)
                    ev.inc(ve.tensor_scalar(Ot, Ft, self.st_inv[:, t:t + 1],
                                            self.st_b2[:, t:t + 1], ALU.mult, ALU.add), 1)
                    ev.inc(ve.tensor_add(Ot, Ot,
                                         self.yst[:, (t % 2) * D:(t % 2 + 1) * D]), 1)

            @blk.scalar
            def _(sc):
                for t in range(NT):
                    Ft = self.fst[:, (t % 2) * D:(t % 2 + 1) * D]
                    sc.wait_ge(ev.h, ev_t(t, 1))
                    sc.activation(self.disc[:], Ft, AF.Square,
                                  bias=self.st_negmu[:, t:t + 1], scale=1.0,
                                  accum_out=self.st_ssq[:, t:t + 1])
                    self.incd(sc, ac)
                    sc.wait_ge(ev.h, ev_t(t, 2))
                    sc.activation(self.st_inv[:, t:t + 1], self.st_inv[:, t:t + 1],
                                  AF.Sqrt)
                    self.incd(sc, ac)

        assert ev.n == EV0 + 5 * NT and ac.n == AC0 + 2 * NT

    def build(self):
        with self.nc.Block() as blk:
            self.blk = blk
            self.p0()
            if UPTO >= 1:
                self.p1()
            if UPTO >= 2:
                self.p2()
                self.p25()
            if UPTO >= 3:
                self.p3()
            self.p3_mm_end = self.s_mm.n
            if UPTO >= 4:
                self.p4()
            else:
                self.p4_mm_end = self.s_mm.n
            if UPTO >= 5:
                self.p5()
            if UPTO >= 6:
                self.p6()
            if UPTO >= 7:
                self.p7()
            self.finish()
        lower_extended_insts(self.nc)
        return self.nc

    def finish(self):
        nc = self.nc
        tapmap = {"ln1": self.d_ln1, "dx": self.d_x, "cat": self.d_cat,
                  "y": self.d_y, "z": self.d_z, "slab": self.slab_loc,
                  "tab": self.tab_all, "eo": self.eo_loc, "offs": self.offs_d}
        taps = [(tapmap[n], "t_" + n) for n in sorted(TAPS) if n in tapmap]
        sbtaps = []
        for n in sorted(TAPS):
            if n.startswith("sb:"):
                _, attr, rows, cols = n.split(":")
                t = getattr(self, attr)
                sbtaps.append((t, "t_" + attr, int(rows), int(cols)))
        tap_ch = Chan(nc, "ctap")
        if True:
            blk = self.blk
            @blk.sync
            def _(sy):
                for s in [self.s_mm, self.s_ev, self.s_ac, self.s_cc, self.s_gp]:
                    if s.n:
                        sy.wait_ge(s.h, s.n)
                for ch in [self.ch_const, self.ch_wc, self.ch_x, self.ch_dx,
                           self.ch_l1, self.ch_t2, self.ch_cat, self.ch_t4,
                           self.ch_xb, self.ch_y, self.ch_z, self.ch_slab,
                           self.ch_t5, self.ch_cnt, self.ch_cna, self.ch_off,
                           self.ch_orow, self.ch_rdf, self.ch_tz, self.ch_gbr,
                           self.ch_wi, self.ch_wo, self.ch_sc, self.ch_tr,
                           self.ch_direp, self.ch_dg, self.ch_eo, self.ch_gb2,
                           self.ch_zy, self.ch_out, self.ch_grp]:
                    ch.wait_all(sy)
                i = 0
                for srct, name in taps:
                    t = nc.dram_tensor(name, list(srct.shape), srct.ap().dtype,
                                       kind="ExternalOutput")
                    tap_ch.inc(sy.dma_start(out=t.ap(), in_=srct.ap()), i)
                    i += 1
                for srct, name, rows, cols in sbtaps:
                    t = nc.dram_tensor(name, [rows, cols], srct.ap().dtype,
                                       kind="ExternalOutput")
                    tap_ch.inc(sy.dma_start(out=t.ap(), in_=srct[0:rows, 0:cols]), i)
                    i += 1
                if UPTO < 7:
                    tap_ch.inc(sy.dma_start(out=self.out.ap()[0:128, 0:128],
                                            in_=self.c_onesf[:]), i)
                tap_ch.wait_all(sy)


def host_inputs(inputs):
    x = np.ascontiguousarray(inputs["L0_attn_x"], dtype=np.float32)
    Wq = np.ascontiguousarray(inputs["L0_attn_Wq"], dtype=np.float32)
    Wk = np.ascontiguousarray(inputs["L0_attn_Wk"], dtype=np.float32)
    Wv = np.ascontiguousarray(inputs["L0_attn_Wv"], dtype=np.float32)
    Wp = np.ascontiguousarray(inputs["L0_attn_Wproj"], dtype=np.float32)
    bq = np.asarray(inputs["L0_attn_bq"], dtype=np.float32)
    bk = np.asarray(inputs["L0_attn_bk"], dtype=np.float32)
    bv = np.asarray(inputs["L0_attn_bv"], dtype=np.float32)
    bp = np.asarray(inputs["L0_attn_bproj"], dtype=np.float32)
    wqf = np.asarray(inputs["L0_attn_wq_feat"], dtype=np.float32)
    wkf = np.asarray(inputs["L0_attn_wk_feat"], dtype=np.float32)
    gw = np.asarray(inputs["L0_ffn_gate_weight"], dtype=np.float32)
    lng = np.asarray(inputs["L0_ffn_ln_gamma"], dtype=np.float32)
    lnb = np.asarray(inputs["L0_ffn_ln_beta"], dtype=np.float32)
    wi = np.ascontiguousarray(inputs["L0_ffn_wi"], dtype=np.float32)
    wo = np.ascontiguousarray(inputs["L0_ffn_wo"], dtype=np.float32)

    wqf4 = np.zeros((128, 512), np.float32)
    wkf2 = np.zeros((128, 512), np.float32)
    for par in range(2):
        for fh in range(2):
            v = par * 2 + fh
            wqf4[par * 64:(par + 1) * 64, v * 128:(v + 1) * 128] = \
                wqf[:, fh * 128:(fh + 1) * 128]
        wkf2[par * 64:(par + 1) * 64, par * 256:(par + 1) * 256] = wkf
    wgt = gw.T.reshape(8, 128, 8).transpose(1, 0, 2).reshape(128, 64).copy()
    sel8 = np.zeros((128, 128), np.float32)
    for r in range(8):
        for q in range(16):
            sel8[16 * r + q, r * 16 + q] = 1.0
    shared = dict(
        wq=Wq, wk=Wk, wv=Wv, wp=Wp,
        bq=np.ascontiguousarray(bq.reshape(8, 128).T),
        bk=np.ascontiguousarray(bk.reshape(8, 128).T),
        bvr=bv.reshape(1, D).copy(), bpr=bp.reshape(1, D).copy(),
        wqf4=wqf4, wkf2=wkf2, wgt=wgt,
        grep=np.ascontiguousarray(np.broadcast_to(lng, (128, D))),
        brep=np.ascontiguousarray(np.broadcast_to(lnb, (128, D))),
        triu=np.triu(np.ones((128, 128), np.float32), 1),
        onesc=np.ones((128, 128), np.float32),
        identc=np.eye(128, dtype=np.float32),
        sel8=sel8,
        iota8u=np.ascontiguousarray(np.broadcast_to(np.arange(8, dtype=np.uint32),
                                                    (128, 8))),
    )
    in_maps = []
    for c in range(NC):
        j = np.arange(NT)[None, :] * 128 + np.arange(128)[:, None]
        ids = ((j // 1024) * (NC * 1024) + c * 1024 + (j % 1024)).astype(np.float32)
        pref = np.zeros((128, 1), np.float32)
        pref[:c, 0] = 1.0
        meta = np.array([[c * CAP, c]], dtype=np.uint32)
        m = dict(shared)
        m.update(x=x[c], wi_c=wi[c], wo_c=wo[c], ids=ids, pref=pref, meta_u=meta)
        in_maps.append(m)
    return in_maps


def build_and_run(inputs):
    in_maps = host_inputs(inputs)
    b = Bld()
    nc = b.build()
    res = run_bass_kernel_spmd(nc, in_maps, core_ids=list(range(NC)))
    return res


def kernel(**inputs):
    res = build_and_run(inputs)
    out = np.stack([np.asarray(r["out"]) for r in res.results], axis=0)
    return out.astype(np.float32)


if __name__ == "__main__":
    import reference as refmod
    ins = refmod.setup_inputs(0)
    out = kernel(**{k: np.asarray(v) for k, v in ins.items()})
    print(out.shape, out.dtype)
